# version 22
# baseline (speedup 1.0000x reference)
import numpy as np
from contextlib import ExitStack

import concourse.bass as bass
import concourse.mybir as mybir
from concourse.bass_utils import run_bass_kernel_spmd

F32 = mybir.dt.float32
BF16 = mybir.dt.bfloat16
AF = mybir.ActivationFunctionType
ALU = mybir.AluOpType
AX = mybir.AxisListType

TOK = 2048
D = 1024
KC = D // 128
NE = 8
NJ = 32
EPS = 1e-5


def _ident(g, ident):
    g.memset(ident[:, :], 0.0)
    return g.affine_select(out=ident[:, :], in_=ident[:, :],
                           compare_op=ALU.not_equal, fill=1.0, base=0,
                           pattern=[[-1, ident.shape[1]]], channel_multiplier=1)


def _sel1(g, sel, ne):
    g.memset(sel[:, :], 0.0)
    return g.affine_select(out=sel[:, :], in_=sel[:, :],
                           compare_op=ALU.not_equal, fill=1.0, base=0,
                           pattern=[[-1, ne], [0, 128]], channel_multiplier=1)


def build_ffn0(tok=TOK, ne=NE, nj=NJ):
    NT = tok // 128
    NST = max(tok // 512, 1)
    STW = tok // NST
    NTT = STW // 128
    NJH = 2
    JH = nj // NJH
    NRND = ne * NJH
    GT = min(4, JH)
    NG = JH // GT
    RING = max(NG, 2)

    nc = bass.Bass("TRN2", target_bir_lowering=False, debug=False)
    x_in = nc.dram_tensor("x_in", [tok, D], F32, kind="ExternalInput").ap()
    w1r = nc.dram_tensor("w1r", [ne, nj // GT, 128, GT * D], F32, kind="ExternalInput").ap()
    w2r = nc.dram_tensor("w2r", [ne, nj // GT, 128, GT * D], F32, kind="ExternalInput").ap()
    b1r = nc.dram_tensor("b1r", [128, ne * nj], F32, kind="ExternalInput").ap()
    b2r = nc.dram_tensor("b2r", [ne, D], F32, kind="ExternalInput").ap()
    wgr = nc.dram_tensor("wgr", [128, KC * ne], F32, kind="ExternalInput").ap()
    x_out = nc.dram_tensor("x_out", [tok, D], F32, kind="ExternalOutput").ap()

    ctx = ExitStack()
    sb = lambda name, shape, dt: ctx.enter_context(nc.sbuf_tensor(name, shape, dt))
    sem = lambda name: ctx.enter_context(nc.semaphore(name))

    Xs = sb("Xs", [128, NT * D], F32)
    XlnT = sb("XlnT", [128, KC * tok], BF16)
    Hss = sb("Hss", [128, JH * STW], BF16)
    w1ring = sb("w1ring", [128, RING * GT * D], BF16)
    w2ring = sb("w2ring", [128, RING * GT * D], BF16)
    Hgel = sb("Hgel", [128, max(2 * STW, D)], BF16)
    cwbs = sb("cwbs", [128, 2 * STW], BF16)
    b1s = sb("b1s", [128, ne * nj], F32)
    b2s = sb("b2s", [ne, D], BF16)
    wgs = sb("wgs", [128, KC * ne], BF16)
    cwTs = sb("cwTs", [ne, tok], BF16)
    sel = sb("sel", [ne, ne * 128], BF16)
    ident = sb("ident", [128, 128], BF16)
    xlnbf = sb("xlnbf", [128, D], BF16)
    sums = sb("sums", [128, NT], F32)
    sqs = sb("sqs", [128, NT], F32)
    means = sb("means", [128, NT], F32)
    m2s = sb("m2s", [128, NT], F32)
    vars_ = sb("vars_", [128, NT], F32)
    stds = sb("stds", [128, NT], F32)
    rss = sb("rss", [128, NT], F32)
    negmrs = sb("negmrs", [128, NT], F32)
    gsm = sb("gsm", [128, NT * ne], F32)
    gs1 = sb("gs1", [128, ne], F32)
    gs2 = sb("gs2", [128, ne], F32)
    gs3 = sb("gs3", [128, ne], F32)
    r1 = sb("r1", [128, 1], F32)
    r2 = sb("r2", [128, 1], F32)
    epst = sb("epst", [128, 1], F32)
    cwb_all = sb("cwb_all", [128, NT * ne], BF16)

    arena = ctx.enter_context(nc.psum_tensor("arena", [128, 4096], F32))

    s_x = sem("s_x")
    s_pre = sem("s_pre")
    s_preh = sem("s_preh")
    s_sq = sem("s_sq")
    s_sum = sem("s_sum")
    s_var = sem("s_var")
    s_std = sem("s_std")
    s_rs = sem("s_rs")
    s_xln = sem("s_xln")
    s_tp = sem("s_tp")
    s_xlnT = sem("s_xlnT")
    s_z = sem("s_z")
    s_g = sem("s_g")
    s_cw = sem("s_cw")
    s_cwt = sem("s_cwt")
    s_cwts = sem("s_cwts")
    s_w1s = [sem(f"s_w1_{i}") for i in range(RING)]
    s_w2s = [sem(f"s_w2_{i}") for i in range(RING)]
    s_h = sem("s_h")
    s_gel = sem("s_gel")
    s_cwm = sem("s_cwm")
    s_selm = sem("s_selm")
    s_cwbc = sem("s_cwbc")
    s_b = sem("s_b")
    s_ye = sem("s_ye")
    s_out = sem("s_out")
    s_gch = sem("s_gch")
    s_vch = sem("s_vch")

    block = ctx.enter_context(nc.Block())

    def hcount(rnd, st, j):
        return (rnd * NST + st) * JH + j + 1

    def evacs_before(rnd, st):
        return (rnd * NST + st) * NTT

    def hbank(j):
        return arena[:, 3072 + (j % 2) * 512: 3072 + (j % 2) * 512 + STW]

    @block.sync
    def _(sy):
        sy.dma_start(out=Xs[:, :].rearrange("p (nt d) -> p nt d", d=D),
                     in_=x_in.rearrange("(nt p) d -> p nt d", p=128)).then_inc(s_x, 16)
        sy.dma_start(out=b1s[:, :], in_=b1r[:, :]).then_inc(s_preh, 16)

    @block.gpsimd
    def _(g):
        g.dma_start(out=b2s[:, :], in_=b2r[:, :]).then_inc(s_pre, 16)
        g.dma_start(out=wgs[:, :], in_=wgr[:, :]).then_inc(s_pre, 16)
        g.memset(epst[:, :], EPS).then_inc(s_gch, 1)
        g.wait_ge(s_gch, 1)
        g.memset(ident[:, :], 0.0).then_inc(s_gch, 1)
        g.wait_ge(s_gch, 2)
        g.affine_select(out=ident[:, :], in_=ident[:, :],
                        compare_op=ALU.not_equal, fill=1.0, base=0,
                        pattern=[[-1, 128]], channel_multiplier=1).then_inc(s_gch, 1)
        g.wait_ge(s_gch, 3)
        g.memset(sel[:, :], 0.0).then_inc(s_gch, 1)
        g.wait_ge(s_gch, 4)
        g.affine_select(out=sel[:, :], in_=sel[:, :],
                        compare_op=ALU.not_equal, fill=1.0, base=0,
                        pattern=[[-1, ne], [0, 128]], channel_multiplier=1).then_inc(s_gch, 1)
        for rnd in range(NRND):
            e, jh = divmod(rnd, NJH)
            for q in range(NG):
                for which in (0, 1):
                    gidx = rnd * NG + q
                    slot = gidx % RING
                    if gidx >= RING:
                        prnd, pq = divmod(gidx - RING, NG)
                        rel = hcount(prnd, NST - 1, pq * GT + (GT - 1))
                        g.wait_ge(s_h if which == 0 else s_b, rel)
                    gg = (jh * JH + q * GT) // GT
                    if which == 0:
                        g.dma_start(
                            out=w1ring[:, slot * GT * D:(slot + 1) * GT * D],
                            in_=w1r[e, gg],
                        ).then_inc(s_w1s[slot], 16)
                    else:
                        g.dma_start(
                            out=w2ring[:, slot * GT * D:(slot + 1) * GT * D],
                            in_=w2r[e, gg],
                        ).then_inc(s_w2s[slot], 16)

    @block.scalar
    def _(s):
        s.wait_ge(s_x, 16)
        for nt in range(NT):
            if nt > 0:
                s.wait_ge(s_sq, nt)
            s.activation(out=Hgel[:, 0:D],
                         in_=Xs[:, nt * D:(nt + 1) * D], func=AF.Square,
                         accum_out=sqs[:, nt:nt + 1]).then_inc(s_sq, 1)
        s.wait_ge(s_var, 1)
        s.wait_ge(s_pre, 32)
        s.wait_ge(s_preh, 16)
        s.wait_ge(s_gch, 5)
        s.activation(out=stds[:, :], in_=vars_[:, :], func=AF.Sqrt,
                     bias=epst[:, 0:1]).then_inc(s_std, 1)
        s.wait_ge(s_z, NT)
        s.activation(out=gsm[:, :], in_=arena[:, 1024:1024 + NT * ne],
                     func=AF.Exp).then_inc(s_g, 1)
        for rnd in range(NRND):
            e, jh = divmod(rnd, NJH)
            for st in range(NST):
                for j in range(JH):
                    idx = hcount(rnd, st, j)
                    s.wait_ge(s_h, idx)
                    if idx > 2:
                        s.wait_ge(s_cwm, idx - 2)
                    s.activation(out=Hgel[:, (j % 2) * STW:(j % 2) * STW + STW],
                                 in_=hbank(j), func=AF.Gelu,
                                 bias=b1s[:, e * nj + jh * JH + j: e * nj + jh * JH + j + 1],
                                 ).then_inc(s_gel, 1)

    @block.vector
    def _(v):
        v.wait_ge(s_x, 16)
        for nt in range(NT):
            v.reduce_sum(sums[:, nt:nt + 1], Xs[:, nt * D:(nt + 1) * D],
                         axis=AX.X).then_inc(s_sum, 1)
        v.wait_ge(s_sq, NT)
        v.wait_ge(s_sum, NT)
        v.tensor_scalar_mul(means[:, :], sums[:, :], 1.0 / D).then_inc(s_vch, 1)
        v.wait_ge(s_vch, 1)
        v.tensor_mul(m2s[:, :], means[:, :], means[:, :]).then_inc(s_vch, 1)
        v.wait_ge(s_vch, 2)
        v.scalar_tensor_tensor(vars_[:, :], sqs[:, :], 1.0 / D, m2s[:, :],
                               ALU.mult, ALU.subtract).then_inc(s_var, 1)
        v.wait_ge(s_std, 1)
        v.reciprocal(rss[:, :], stds[:, :]).then_inc(s_vch, 1)
        v.wait_ge(s_vch, 3)
        v.scalar_tensor_tensor(negmrs[:, :], means[:, :], -1.0, rss[:, :],
                               ALU.mult, ALU.mult).then_inc(s_rs, 1)
        v.wait_ge(s_rs, 1)
        for nt in range(NT):
            if nt > 0:
                v.wait_ge(s_tp, nt)
            v.tensor_scalar(xlnbf[:, :], Xs[:, nt * D:(nt + 1) * D],
                            rss[:, nt:nt + 1], negmrs[:, nt:nt + 1],
                            ALU.mult, ALU.add).then_inc(s_xln, 1)
            v.wait_ge(s_tp, nt + 1)
            bk = arena[:, (nt % 2) * 512:(nt % 2) * 512 + 512].bitcast(BF16)
            v.tensor_copy(
                XlnT[:, :].rearrange("p (kc t) -> p kc t", t=tok)[:, :, nt * 128:(nt + 1) * 128],
                bk[:, :].rearrange("p (kc t) -> p kc t", t=128),
            ).then_inc(s_xlnT, 1)
        v.wait_ge(s_g, 1)
        vchn = [3]
        def vstep(ins):
            vchn[0] += 1
            ins.then_inc(s_vch, 1)
            v.wait_ge(s_vch, vchn[0])
        for nt in range(NT):
            gt = gsm[:, nt * ne:(nt + 1) * ne]
            vstep(v.reduce_sum(r1[:, :], gt, axis=AX.X))
            vstep(v.reciprocal(r2[:, :], r1[:, :]))
            vstep(v.tensor_scalar(gs1[:, :], gt, r2[:, 0:1], None, ALU.mult))
            vstep(v.reduce_max(r1[:, :], gs1[:, :], axis=AX.X))
            vstep(v.tensor_scalar(gs2[:, :], gs1[:, :], r1[:, 0:1], None, ALU.is_ge))
            vstep(v.scalar_tensor_tensor(gs3[:, :], gs2[:, :], -1e30, gs1[:, :],
                                         ALU.mult, ALU.add))
            vstep(v.reduce_max(r1[:, :], gs3[:, :], axis=AX.X))
            vstep(v.tensor_scalar(gs2[:, :], gs1[:, :], r1[:, 0:1], None, ALU.is_ge))
            vstep(v.scalar_tensor_tensor(gs3[:, :], gs2[:, :], 0.5, gs1[:, :],
                                         ALU.mult, ALU.mult))
            v.tensor_copy(cwb_all[:, nt * ne:(nt + 1) * ne], gs3[:, :]).then_inc(s_cw, 1)
            v.wait_ge(s_cw, nt + 1)
        v.wait_ge(s_cwt, NT)
        v.tensor_copy(cwTs[:, :], arena[:, 1536:1536 + tok // 2].bitcast(BF16)[0:ne, :]).then_inc(s_cwts, 1)
        for rnd in range(NRND):
            for st in range(NST):
                v.wait_ge(s_selm, rnd * NST + st + 1)
                v.tensor_copy(cwbs[:, (st % 2) * STW:(st % 2) * STW + STW],
                              arena[:, 0:STW]).then_inc(s_cwbc, 1)
                v.wait_ge(s_cwbc, rnd * NST + st + 1)
                for j in range(JH):
                    idx = hcount(rnd, st, j)
                    v.wait_ge(s_gel, idx)
                    if rnd * NST + st > 0:
                        v.wait_ge(s_b, hcount(rnd, st - 1, j) if st > 0
                                  else hcount(rnd - 1, NST - 1, j))
                    v.tensor_tensor(Hss[:, j * STW:(j + 1) * STW],
                                    Hgel[:, (j % 2) * STW:(j % 2) * STW + STW],
                                    cwbs[:, (st % 2) * STW:(st % 2) * STW + STW],
                                    ALU.mult).then_inc(s_cwm, 1)
                v.wait_ge(s_b, hcount(rnd, st, JH - 1))
                v.wait_ge(s_ye, evacs_before(rnd, st))
                for t in range(NTT):
                    tk = st * NTT + t
                    v.tensor_tensor(Xs[:, tk * D:(tk + 1) * D],
                                    arena[:, t * 1024:(t + 1) * 1024],
                                    Xs[:, tk * D:(tk + 1) * D],
                                    ALU.add).then_inc(s_ye, 1)

    @block.tensor
    def _(t):
        t.wait_ge(s_pre, 32)
        t.wait_ge(s_preh, 16)
        t.wait_ge(s_gch, 5)
        for nt in range(NT):
            t.wait_ge(s_xln, nt + 1)
            if nt >= 2:
                t.wait_ge(s_xlnT, nt - 1)
            bk = arena[:, (nt % 2) * 512:(nt % 2) * 512 + 512].bitcast(BF16)
            for kc in range(KC):
                ins = t.transpose(bk[:, kc * 128:(kc + 1) * 128],
                                  xlnbf[:, kc * 128:(kc + 1) * 128], ident[:, :])
            ins.then_inc(s_tp, 1)
            t.wait_ge(s_xlnT, nt + 1)
            for kc in range(KC):
                ins = t.matmul(arena[:, 1024 + nt * ne:1024 + (nt + 1) * ne],
                               XlnT[:, kc * tok + nt * 128: kc * tok + (nt + 1) * 128],
                               wgs[:, kc * ne:(kc + 1) * ne],
                               start=(kc == 0), stop=(kc == KC - 1))
            ins.then_inc(s_z, 1)
        cwtb = arena[:, 1536:1536 + tok // 2].bitcast(BF16)
        for nt in range(NT):
            t.wait_ge(s_cw, nt + 1)
            ins = t.transpose(cwtb[0:ne, nt * 128:(nt + 1) * 128],
                              cwb_all[:, nt * ne:(nt + 1) * ne], ident[:, :])
            ins.then_inc(s_cwt, 1)
        t.wait_ge(s_xlnT, NT)
        t.wait_ge(s_g, 1)
        t.wait_ge(s_cwts, 1)
        for rnd in range(NRND):
            e, jh = divmod(rnd, NJH)
            for st in range(NST):
                gidx_base = rnd * NG
                t.wait_ge(s_ye, evacs_before(rnd, st))
                t.wait_ge(s_cwbc, rnd * NST + st)
                ins = t.matmul(arena[:, 0:STW], sel[:, e * 128:(e + 1) * 128],
                               cwTs[:, st * STW:(st + 1) * STW], start=True, stop=True)
                ins.then_inc(s_selm, 1)
                for j in range(JH):
                    q = j // GT
                    gidx = gidx_base + q
                    t.wait_ge(s_w1s[gidx % RING], 16 * (gidx // RING + 1))
                    idx = hcount(rnd, st, j)
                    if idx > 2:
                        t.wait_ge(s_gel, idx - 2)
                    slot = (gidx_base + q) % RING
                    w1t = w1ring[:, slot * GT * D + (j % GT) * D: slot * GT * D + (j % GT + 1) * D]
                    for kc in range(KC):
                        ins = t.matmul(hbank(j), w1t[:, kc * 128:(kc + 1) * 128],
                                       XlnT[:, kc * tok + st * STW: kc * tok + (st + 1) * STW],
                                       start=(kc == 0), stop=(kc == KC - 1))
                    ins.then_inc(s_h, 1)
                for j in range(JH):
                    q = j // GT
                    gidx = gidx_base + q
                    t.wait_ge(s_w2s[gidx % RING], 16 * (gidx // RING + 1))
                    t.wait_ge(s_cwm, hcount(rnd, st, j))
                    if j == 0 and NTT == 4:
                        t.wait_ge(s_gel, hcount(rnd, st, JH - 1))
                    slot = (gidx_base + q) % RING
                    w2t = w2ring[:, slot * GT * D + (j % GT) * D: slot * GT * D + (j % GT + 1) * D]
                    first = (j == 0)
                    last = (j == JH - 1)
                    for tt in range(NTT):
                        if first and rnd == 0:
                            t.matmul(arena[:, tt * 1024: tt * 1024 + 512],
                                     cwTs[:, (st * NTT + tt) * 128:(st * NTT + tt + 1) * 128],
                                     b2s[:, 0:512], start=True, stop=False)
                            t.matmul(arena[:, tt * 1024 + 512:(tt + 1) * 1024],
                                     cwTs[:, (st * NTT + tt) * 128:(st * NTT + tt + 1) * 128],
                                     b2s[:, 512:1024], start=True, stop=False)
                        hst = Hss[:, j * STW + tt * 128: j * STW + (tt + 1) * 128]
                        st0 = first and rnd > 0
                        t.matmul(arena[:, tt * 1024: tt * 1024 + 512], hst,
                                 w2t[:, 0:512], start=st0, stop=last)
                        ins = t.matmul(arena[:, tt * 1024 + 512:(tt + 1) * 1024], hst,
                                       w2t[:, 512:1024], start=st0, stop=last)
                    ins.then_inc(s_b, 1)

    @block.sync
    def _(sy):
        sy.wait_ge(s_ye, NRND * NST * NTT)
        sy.dma_start(out=x_out.rearrange("(nt p) d -> p nt d", p=128),
                     in_=Xs[:, :].rearrange("p (nt d) -> p nt d", d=D)).then_inc(s_out, 16)
        sy.wait_ge(s_out, 16)

    ctx.close()
    return nc



def prep_ffn0(W1, W2, b1, b2, wg):
    ne, d, hid = W1.shape
    nj = hid // 128
    kc = d // 128
    gt = min(4, nj // 2)
    w1r = np.ascontiguousarray(
        W1.reshape(ne, kc, 128, nj // gt, gt, 128)
          .transpose(0, 3, 2, 4, 1, 5).reshape(ne, nj // gt, 128, gt * kc * 128))
    w2r = np.ascontiguousarray(
        W2.reshape(ne, nj // gt, gt, 128, d)
          .transpose(0, 1, 3, 2, 4).reshape(ne, nj // gt, 128, gt * d))
    b1r = np.ascontiguousarray(b1.reshape(ne, nj, 128).transpose(2, 0, 1).reshape(128, ne * nj))
    b2r = np.ascontiguousarray(b2)
    wgr = np.ascontiguousarray(wg.T.reshape(d // 128, 128, ne).transpose(1, 0, 2).reshape(128, (d // 128) * ne))
    return {"w1r": w1r, "w2r": w2r, "b1r": b1r, "b2r": b2r, "wgr": wgr}


def run_ffn0(x_flat, inputs, nc_cache={}):
    if "ffn0" not in nc_cache:
        nc_cache["ffn0"] = build_ffn0()
    nc = nc_cache["ffn0"]
    wmap = prep_ffn0(inputs["L0_ffn_W1"], inputs["L0_ffn_W2"], inputs["L0_ffn_b1"],
                     inputs["L0_ffn_b2"], inputs["L0_ffn_w_gate"])
    in_maps = []
    for c in range(8):
        m = {"x_in": np.ascontiguousarray(x_flat[c * TOK:(c + 1) * TOK])}
        m.update(wmap)
        in_maps.append(m)
    res = run_bass_kernel_spmd(nc, in_maps, core_ids=list(range(8)))
    return np.concatenate([res.results[c]["x_out"] for c in range(8)], axis=0)


def kernel_unfused(**inputs):
    inputs = {k: np.asarray(v) for k, v in inputs.items()}
    x = np.ascontiguousarray(inputs["L0_attn_x"].astype(np.float32).reshape(16384, D))
    x = run_ffn0(x, inputs)
    x = run_attn_layer(x, inputs, 0)
    x = run_ffn0(x, inputs)
    x = run_ffn1(x, inputs)
    x = run_attn_layer(x, inputs, 1)
    x = run_ffn1(x, inputs)
    return np.ascontiguousarray(x.reshape(4, 4096, D))


def kernel(**inputs):
    return kernel_fused(**inputs)


def build_ffn1(tok=TOK, ne=16, hh=128):
    NT = tok // 128
    NST = max(tok // 512, 1)
    STW = tok // NST
    NTT = STW // 128

    nc = bass.Bass("TRN2", target_bir_lowering=False, debug=False)
    x_in = nc.dram_tensor("x_in", [tok, D], F32, kind="ExternalInput").ap()
    keyr = nc.dram_tensor("keyr", [ne, 128, KC * hh], F32, kind="ExternalInput").ap()
    valr = nc.dram_tensor("valr", [ne, 128, D], F32, kind="ExternalInput").ap()
    eselr = nc.dram_tensor("eselr", [128, KC * ne], F32, kind="ExternalInput").ap()
    x_out = nc.dram_tensor("x_out", [tok, D], F32, kind="ExternalOutput").ap()

    ctx = ExitStack()
    sb = lambda name, shape, dt: ctx.enter_context(nc.sbuf_tensor(name, shape, dt))
    sem = lambda name: ctx.enter_context(nc.semaphore(name))

    Xs = sb("Xs", [128, NT * D], F32)
    XlnT = sb("XlnT", [128, KC * tok], BF16)
    keys = sb("keys", [128, ne * KC * hh], BF16)
    vals = sb("vals", [128, ne * D], BF16)
    esels = sb("esels", [128, KC * ne], BF16)
    Ssall = sb("Ssall", [128, ne * STW], BF16)
    Sg = sb("Sg", [128, 2 * STW], BF16)
    selbs = sb("selbs", [128, 2 * STW], BF16)
    selTs = sb("selTs", [ne, tok], BF16)
    sel16 = sb("sel16", [ne, ne * 128], BF16)
    ident = sb("ident", [128, 128], BF16)
    xlnbf = sb("xlnbf", [128, D], BF16)
    sums = sb("sums", [128, NT], F32)
    sqs = sb("sqs", [128, NT], F32)
    means = sb("means", [128, NT], F32)
    m2s = sb("m2s", [128, NT], F32)
    vars_ = sb("vars_", [128, NT], F32)
    stds = sb("stds", [128, NT], F32)
    rss = sb("rss", [128, NT], F32)
    negmrs = sb("negmrs", [128, NT], F32)
    sgm = sb("sgm", [128, NT * ne], F32)
    sgmb = sb("sgmb", [128, NT * ne], BF16)
    epst = sb("epst", [128, 1], F32)

    arena = ctx.enter_context(nc.psum_tensor("arena", [128, 4096], F32))

    s_x = sem("s_x")
    s_pre = sem("s_pre")
    s_gch = sem("s_gch")
    s_vch = sem("s_vch")
    s_sq = sem("s_sq")
    s_sum = sem("s_sum")
    s_var = sem("s_var")
    s_std = sem("s_std")
    s_rs = sem("s_rs")
    s_xln = sem("s_xln")
    s_tp = sem("s_tp")
    s_xlnT = sem("s_xlnT")
    s_z = sem("s_z")
    s_g = sem("s_g")
    s_sgb = sem("s_sgb")
    s_st = sem("s_st")
    s_sts = sem("s_sts")
    s_selm = sem("s_selm")
    s_sbc = sem("s_sbc")
    s_s = sem("s_s")
    s_rel = sem("s_rel")
    s_ssm = sem("s_ssm")
    s_b = sem("s_b")
    s_ye = sem("s_ye")
    s_out = sem("s_out")

    block = ctx.enter_context(nc.Block())

    def secount(st, e):
        return st * ne + e + 1

    @block.sync
    def _(sy):
        sy.dma_start(out=Xs[:, :].rearrange("p (nt d) -> p nt d", d=D),
                     in_=x_in.rearrange("(nt p) d -> p nt d", p=128)).then_inc(s_x, 16)

    @block.gpsimd
    def _(g):
        g.dma_start(out=keys[:, :].rearrange("p (e m) -> p e m", m=KC * hh),
                    in_=keyr.rearrange("e p m -> p e m")).then_inc(s_pre, 16)
        g.dma_start(out=vals[:, :].rearrange("p (e d) -> p e d", d=D),
                    in_=valr.rearrange("e p d -> p e d")).then_inc(s_pre, 16)
        g.dma_start(out=esels[:, :], in_=eselr[:, :]).then_inc(s_pre, 16)
        g.memset(epst[:, :], EPS).then_inc(s_gch, 1)
        g.wait_ge(s_gch, 1)
        g.memset(ident[:, :], 0.0).then_inc(s_gch, 1)
        g.wait_ge(s_gch, 2)
        g.affine_select(out=ident[:, :], in_=ident[:, :],
                        compare_op=ALU.not_equal, fill=1.0, base=0,
                        pattern=[[-1, 128]], channel_multiplier=1).then_inc(s_gch, 1)
        g.wait_ge(s_gch, 3)
        g.memset(sel16[:, :], 0.0).then_inc(s_gch, 1)
        g.wait_ge(s_gch, 4)
        g.affine_select(out=sel16[:, :], in_=sel16[:, :],
                        compare_op=ALU.not_equal, fill=0.5, base=0,
                        pattern=[[-1, ne], [0, 128]], channel_multiplier=1).then_inc(s_gch, 1)

    @block.scalar
    def _(s):
        s.wait_ge(s_x, 16)
        for nt in range(NT):
            if nt > 0:
                s.wait_ge(s_sq, nt)
            s.activation(out=Sg[:, 0:D],
                         in_=Xs[:, nt * D:(nt + 1) * D], func=AF.Square,
                         accum_out=sqs[:, nt:nt + 1]).then_inc(s_sq, 1)
        s.wait_ge(s_var, 1)
        s.wait_ge(s_gch, 5)
        s.activation(out=stds[:, :], in_=vars_[:, :], func=AF.Sqrt,
                     bias=epst[:, 0:1]).then_inc(s_std, 1)
        s.wait_ge(s_z, NT)
        s.activation(out=sgm[:, :], in_=arena[:, 1024:1024 + NT * ne],
                     func=AF.Sigmoid).then_inc(s_g, 1)
        for st in range(NST):
            for e in range(ne):
                idx = secount(st, e)
                s.wait_ge(s_s, idx)
                if idx > 2:
                    s.wait_ge(s_ssm, idx - 2)
                sbk = arena[:, 3072 + (e % 2) * 512: 3072 + (e % 2) * 512 + STW]
                s.activation(out=Sg[:, (e % 2) * STW:(e % 2) * STW + STW],
                             in_=sbk, func=AF.Relu).then_inc(s_rel, 1)

    @block.vector
    def _(v):
        v.wait_ge(s_x, 16)
        for nt in range(NT):
            v.reduce_sum(sums[:, nt:nt + 1], Xs[:, nt * D:(nt + 1) * D],
                         axis=AX.X).then_inc(s_sum, 1)
        v.wait_ge(s_sq, NT)
        v.wait_ge(s_sum, NT)
        v.tensor_scalar_mul(means[:, :], sums[:, :], 1.0 / D).then_inc(s_vch, 1)
        v.wait_ge(s_vch, 1)
        v.tensor_mul(m2s[:, :], means[:, :], means[:, :]).then_inc(s_vch, 1)
        v.wait_ge(s_vch, 2)
        v.scalar_tensor_tensor(vars_[:, :], sqs[:, :], 1.0 / D, m2s[:, :],
                               ALU.mult, ALU.subtract).then_inc(s_var, 1)
        v.wait_ge(s_std, 1)
        v.reciprocal(rss[:, :], stds[:, :]).then_inc(s_vch, 1)
        v.wait_ge(s_vch, 3)
        v.scalar_tensor_tensor(negmrs[:, :], means[:, :], -1.0, rss[:, :],
                               ALU.mult, ALU.mult).then_inc(s_rs, 1)
        v.wait_ge(s_rs, 1)
        for nt in range(NT):
            if nt > 0:
                v.wait_ge(s_tp, nt)
            v.tensor_scalar(xlnbf[:, :], Xs[:, nt * D:(nt + 1) * D],
                            rss[:, nt:nt + 1], negmrs[:, nt:nt + 1],
                            ALU.mult, ALU.add).then_inc(s_xln, 1)
            v.wait_ge(s_tp, nt + 1)
            bk = arena[:, (nt % 2) * 512:(nt % 2) * 512 + 512].bitcast(BF16)
            v.tensor_copy(
                XlnT[:, :].rearrange("p (kc t) -> p kc t", t=tok)[:, :, nt * 128:(nt + 1) * 128],
                bk[:, :].rearrange("p (kc t) -> p kc t", t=128),
            ).then_inc(s_xlnT, 1)
        v.wait_ge(s_g, 1)
        v.tensor_copy(sgmb[:, :], sgm[:, :]).then_inc(s_sgb, 1)
        v.wait_ge(s_st, NT)
        v.tensor_copy(selTs[:, :], arena[:, 1536:1536 + tok // 2].bitcast(BF16)[0:ne, :]).then_inc(s_sts, 1)
        for st in range(NST):
            for e in range(ne):
                idx = secount(st, e)
                v.wait_ge(s_selm, idx)
                v.tensor_copy(selbs[:, (e % 2) * STW:(e % 2) * STW + STW],
                              arena[:, 1024:1024 + STW]).then_inc(s_sbc, 1)
                v.wait_ge(s_sbc, idx)
                v.wait_ge(s_rel, idx)
                if st > 0:
                    v.wait_ge(s_b, st)
                v.tensor_tensor(Ssall[:, e * STW:(e + 1) * STW],
                                Sg[:, (e % 2) * STW:(e % 2) * STW + STW],
                                selbs[:, (e % 2) * STW:(e % 2) * STW + STW],
                                ALU.mult).then_inc(s_ssm, 1)
            v.wait_ge(s_b, st + 1)
            v.wait_ge(s_ye, st * NTT)
            for t in range(NTT):
                tk = st * NTT + t
                v.tensor_tensor(Xs[:, tk * D:(tk + 1) * D],
                                arena[:, t * 1024:(t + 1) * 1024],
                                Xs[:, tk * D:(tk + 1) * D],
                                ALU.add).then_inc(s_ye, 1)

    @block.tensor
    def _(t):
        t.wait_ge(s_pre, 48)
        t.wait_ge(s_gch, 5)
        for nt in range(NT):
            t.wait_ge(s_xln, nt + 1)
            if nt >= 2:
                t.wait_ge(s_xlnT, nt - 1)
            bk = arena[:, (nt % 2) * 512:(nt % 2) * 512 + 512].bitcast(BF16)
            for kc in range(KC):
                ins = t.transpose(bk[:, kc * 128:(kc + 1) * 128],
                                  xlnbf[:, kc * 128:(kc + 1) * 128], ident[:, :])
            ins.then_inc(s_tp, 1)
            t.wait_ge(s_xlnT, nt + 1)
            for kc in range(KC):
                ins = t.matmul(arena[:, 1024 + nt * ne:1024 + (nt + 1) * ne],
                               XlnT[:, kc * tok + nt * 128: kc * tok + (nt + 1) * 128],
                               esels[:, kc * ne:(kc + 1) * ne],
                               start=(kc == 0), stop=(kc == KC - 1))
            ins.then_inc(s_z, 1)
        seltb = arena[:, 1536:1536 + tok // 2].bitcast(BF16)
        t.wait_ge(s_sgb, 1)
        for nt in range(NT):
            ins = t.transpose(seltb[0:ne, nt * 128:(nt + 1) * 128],
                              sgmb[:, nt * ne:(nt + 1) * ne], ident[:, :])
            ins.then_inc(s_st, 1)
        t.wait_ge(s_xlnT, NT)
        t.wait_ge(s_sts, 1)
        for st in range(NST):
            if st > 0:
                t.wait_ge(s_ye, st * NTT)
            for e in range(ne):
                idx = secount(st, e)
                if idx > 1:
                    t.wait_ge(s_sbc, idx - 1)
                ins = t.matmul(arena[:, 1024:1024 + STW], sel16[:, e * 128:(e + 1) * 128],
                               selTs[:, st * STW:(st + 1) * STW], start=True, stop=True)
                ins.then_inc(s_selm, 1)
                if idx > 2:
                    t.wait_ge(s_rel, idx - 2)
                if e < 2 and st > 0:
                    t.wait_ge(s_ye, st * NTT)
                sbk = arena[:, 3072 + (e % 2) * 512: 3072 + (e % 2) * 512 + STW]
                for kc in range(KC):
                    ins = t.matmul(sbk, keys[:, e * KC * hh + kc * hh: e * KC * hh + (kc + 1) * hh],
                                   XlnT[:, kc * tok + st * STW: kc * tok + (st + 1) * STW],
                                   start=(kc == 0), stop=(kc == KC - 1))
                ins.then_inc(s_s, 1)
            t.wait_ge(s_sbc, (st + 1) * ne)
            for e in range(ne):
                t.wait_ge(s_ssm, secount(st, e))
                if e == 0:
                    t.wait_ge(s_rel, secount(st, ne - 1))
                first = (e == 0)
                last = (e == ne - 1)
                for tt in range(NTT):
                    hst = Ssall[:, e * STW + tt * 128: e * STW + (tt + 1) * 128]
                    t.matmul(arena[:, tt * 1024: tt * 1024 + 512], hst,
                             vals[:, e * D: e * D + 512], start=first, stop=last)
                    ins = t.matmul(arena[:, tt * 1024 + 512:(tt + 1) * 1024], hst,
                                   vals[:, e * D + 512:(e + 1) * D], start=first, stop=last)
            ins.then_inc(s_b, 1)

    @block.sync
    def _(sy):
        sy.wait_ge(s_ye, NST * NTT)
        sy.dma_start(out=x_out.rearrange("(nt p) d -> p nt d", p=128),
                     in_=Xs[:, :].rearrange("p (nt d) -> p nt d", d=D)).then_inc(s_out, 16)
        sy.wait_ge(s_out, 16)

    ctx.close()
    return nc


def prep_ffn1(esel, keysw, valsw):
    ne, d, hh = keysw.shape
    kc = d // 128
    keyr = np.ascontiguousarray(keysw.reshape(ne, kc, 128, hh).transpose(0, 2, 1, 3).reshape(ne, 128, kc * hh))
    valr = np.ascontiguousarray(valsw)
    eselr = np.ascontiguousarray(esel.T.reshape(kc, 128, ne).transpose(1, 0, 2).reshape(128, kc * ne))
    return {"keyr": keyr, "valr": valr, "eselr": eselr}


def run_ffn1(x_flat, inputs, nc_cache={}):
    if "ffn1" not in nc_cache:
        nc_cache["ffn1"] = build_ffn1()
    nc = nc_cache["ffn1"]
    wmap = prep_ffn1(inputs["L1_ffn_expert_sel"], inputs["L1_ffn_keys"], inputs["L1_ffn_values"])
    in_maps = []
    for c in range(8):
        m = {"x_in": np.ascontiguousarray(x_flat[c * TOK:(c + 1) * TOK])}
        m.update(wmap)
        in_maps.append(m)
    res = run_bass_kernel_spmd(nc, in_maps, core_ids=list(range(8)))
    return np.concatenate([res.results[c]["x_out"] for c in range(8)], axis=0)


def build_attn0(T=4096, nh=8, l1=False):
    NTB = T // 128
    NQT = T // 512
    NP = nh // 2
    HW = nh * 64
    KVW = 64 if l1 else HW
    WTOT = HW + 2 * KVW
    SC = 0.125

    nc = bass.Bass("TRN2", target_bir_lowering=False, debug=False)
    xb = nc.dram_tensor("xb", [T, D], F32, kind="ExternalInput").ap()
    wqkvr = nc.dram_tensor("wqkvr", [128, KC * WTOT], F32, kind="ExternalInput").ap()
    ctabr = nc.dram_tensor("ctabr", [128, NTB * 4 * 32], F32, kind="ExternalInput").ap()
    stabr = nc.dram_tensor("stabr", [128, NTB * 4 * 32], F32, kind="ExternalInput").ap()
    if l1:
        biasr = nc.dram_tensor("biasr", [1, WTOT], F32, kind="ExternalInput").ap()
    o_out = nc.dram_tensor("o_out", [NP, 128, T], BF16, kind="ExternalOutput").ap()

    ctx = ExitStack()
    sb = lambda name, shape, dt: ctx.enter_context(nc.sbuf_tensor(name, shape, dt))
    sem = lambda name: ctx.enter_context(nc.semaphore(name))

    wq = sb("wq", [128, KC * WTOT], BF16)
    crep = sb("crep", [128, NTB * 4 * 32], BF16)
    srep = sb("srep", [128, NTB * 4 * 32], BF16)
    QTs = sb("QTs", [128, NP * T], BF16)
    KTs = sb("KTs", [128, (1 if l1 else NP) * T], BF16)
    Vs = sb("Vs", [128, NTB * ((1 if l1 else nh) * 65)], BF16)
    masks = sb("masks", [128, 4 * 512], BF16)
    ident = sb("ident", [128, 128], BF16)
    onesc = sb("onesc", [1, 128], BF16)
    xtile = sb("xtile", [128, D], F32)
    xlnbf = sb("xlnbf", [128, 2 * D], BF16)
    qrot = sb("qrot", [128, 4 * HW], BF16)
    t14 = sb("t14", [128, 4 * 128], BF16)
    stats = sb("stats", [128, 8], F32)
    epst = sb("epst", [128, 1], F32)
    Pp = sb("Pp", [128, 2 * 1024], BF16)
    zb = sb("zb", [1, 1024], F32)
    rzb = sb("rzb", [1, 1024], BF16)
    rb = sb("rb", [64, 1024], BF16)
    otmp = sb("otmp", [64, 2 * 512], BF16)
    Ob = sb("Ob", [128, NP * T], BF16)
    if l1:
        biass = sb("biass", [1, WTOT], BF16)
    XlnTr = sb("XlnTr", [128, 2 * KC * 128], BF16)

    arena = ctx.enter_context(nc.psum_tensor("arena", [128, 4096], F32))

    s_xs = [sem("s_x0"), sem("s_x1")]
    s_pre = sem("s_pre")
    s_gch = sem("s_gch")
    s_vch = sem("s_vch")
    s_sq = sem("s_sq")
    s_v1 = sem("s_v1")
    s_sd = sem("s_sd")
    s_ln = sem("s_ln")
    s_tp = sem("s_tp")
    s_xlnT = sem("s_xlnT")
    s_qk = sem("s_qk")
    s_rot = sem("s_rot")
    s_qt = sem("s_qt")
    s_qts = sem("s_qts")
    s_S = sem("s_S")
    s_exp = sem("s_exp")
    s_msk = sem("s_msk")
    s_pv = sem("s_pv")
    s_nrm = sem("s_nrm")
    s_bc = sem("s_bc")
    s_on = sem("s_on")
    s_ores = [sem("s_ore0"), sem("s_ore1")]
    s_kd = sem("s_kd")
    s_out = sem("s_out")

    block = ctx.enter_context(nc.Block())

    NW = 3 if not l1 else 3
    def qk_idx(nt, which):
        return nt * 3 + which + 1

    pq_list = [(p, qt) for p in range(NP) for qt in range(NQT)]
    ubase = {}
    acc = 0
    for (p, qt) in pq_list:
        ubase[(p, qt)] = acc
        acc += 4 * qt + 4
    NU = acc
    dbase = {}
    dacc = 0
    for (p, qt) in pq_list:
        dbase[(p, qt)] = dacc
        dacc += 4

    def pqi(p, qt):
        return p * NQT + qt

    @block.sync
    def _(sy):
        for nt in range(NTB):
            if nt >= 1:
                sy.wait_ge(s_ln, nt)
                sy.wait_ge(s_sq, nt)
            sy.dma_start(out=xtile[:, :],
                         in_=xb.rearrange("(nt p) d -> p nt d", p=128)[:, nt, :]).then_inc(s_xs[0], 16)
        if l1:
            sy.wait_ge(s_qts, 2 * NTB)
            sy.dma_start(out=KTs[64:128, :], in_=KTs[0:64, :]).then_inc(s_kd, 16)
        for (p, qt) in pq_list:
            u = pqi(p, qt)
            sy.wait_ge(s_on, 2 * u + 2)
            sy.dma_start(out=Ob[64:128, p * T + qt * 512: p * T + (qt + 1) * 512],
                         in_=otmp[0:64, (u % 2) * 512:(u % 2) * 512 + 512]).then_inc(s_ores[u % 2], 16)
        npq = NP * NQT
        sy.wait_ge(s_ores[0], 16 * ((npq + 1) // 2))
        if npq > 1:
            sy.wait_ge(s_ores[1], 16 * (npq // 2))
        for pp in range(NP):
            sy.dma_start(out=o_out[pp], in_=Ob[:, pp * T:(pp + 1) * T]).then_inc(s_out, 16)
        sy.wait_ge(s_out, 16 * NP)

    @block.gpsimd
    def _(g):
        g.dma_start(out=wq[:, :], in_=wqkvr).then_inc(s_pre, 16)
        g.dma_start(out=crep[:, :], in_=ctabr).then_inc(s_pre, 16)
        g.dma_start(out=srep[:, :], in_=stabr).then_inc(s_pre, 16)
        if l1:
            g.dma_start(out=biass[:, :], in_=biasr).then_inc(s_pre, 16)
        gch = [0]
        def gstep(ins):
            gch[0] += 1
            ins.then_inc(s_gch, 1)
            g.wait_ge(s_gch, gch[0])
        gstep(g.memset(epst[:, :], EPS))
        gstep(g.memset(ident[:, :], 0.0))
        gstep(g.affine_select(out=ident[:, :], in_=ident[:, :],
                              compare_op=ALU.not_equal, fill=1.0, base=0,
                              pattern=[[-1, 128]], channel_multiplier=1))
        gstep(g.memset(onesc[:, :], 1.0))
        nhv = 1 if l1 else nh
        gstep(g.memset(Vs[:, :].rearrange("p (b h c2) -> p b h c2", h=nhv, c2=65)[:, :, :, 64:65], 1.0))
        for b in range(4):
            gstep(g.memset(masks[:, b * 512:(b + 1) * 512], 1.0))
            gstep(g.affine_select(
                out=masks[:, b * 512:(b + 1) * 512],
                in_=masks[:, b * 512:(b + 1) * 512],
                compare_op=ALU.is_ge, fill=0.0, base=-128 * b,
                pattern=[[1, 512]], channel_multiplier=-1))
    NCONST = 13

    @block.scalar
    def _(s):
        for nt in range(NTB):
            s.wait_ge(s_xs[0], 16 * (nt + 1))
            if nt > 0:
                s.wait_ge(s_sq, nt)
                s.wait_ge(s_v1, nt)
            s.activation(out=Pp[:, 0:1024],
                         in_=xtile[:, :], func=AF.Square,
                         accum_out=stats[:, 1:2]).then_inc(s_sq, 1)
            s.wait_ge(s_v1, nt + 1)
            s.wait_ge(s_gch, 1)
            s.activation(out=stats[:, 5:6], in_=stats[:, 4:5], func=AF.Sqrt,
                         bias=epst[:, 0:1]).then_inc(s_sd, 1)
        for (p, qt) in pq_list:
            nkb = 4 * qt + 4
            for kb in range(nkb):
                u = ubase[(p, qt)] + kb + 1
                s.wait_ge(s_S, u)
                if u > 2:
                    s.wait_ge(s_pv, u - 2)
                ub = (u - 1) % 2
                s.activation(out=Pp[:, ub * 1024:(ub + 1) * 1024],
                             in_=arena[:, ub * 1024:(ub + 1) * 1024],
                             func=AF.Exp, scale=SC).then_inc(s_exp, 1)

    @block.vector
    def _(v):
        vch = [0]
        def vstep(ins):
            vch[0] += 1
            ins.then_inc(s_vch, 1)
            v.wait_ge(s_vch, vch[0])
        for nt in range(NTB):
            v.wait_ge(s_xs[0], 16 * (nt + 1))
            vstep(v.reduce_sum(stats[:, 0:1], xtile[:, :], axis=AX.X))
            vstep(v.tensor_scalar_mul(stats[:, 2:3], stats[:, 0:1], 1.0 / D))
            vstep(v.tensor_mul(stats[:, 3:4], stats[:, 2:3], stats[:, 2:3]))
            v.wait_ge(s_sq, nt + 1)
            v.scalar_tensor_tensor(stats[:, 4:5], stats[:, 1:2], 1.0 / D,
                                   stats[:, 3:4], ALU.mult, ALU.subtract).then_inc(s_v1, 1)
            v.wait_ge(s_v1, nt + 1)
            v.wait_ge(s_sd, nt + 1)
            vstep(v.reciprocal(stats[:, 6:7], stats[:, 5:6]))
            vstep(v.scalar_tensor_tensor(stats[:, 7:8], stats[:, 2:3], -1.0,
                                         stats[:, 6:7], ALU.mult, ALU.mult))
            if nt > 0:
                v.wait_ge(s_tp, nt)
            v.tensor_scalar(xlnbf[:, (nt % 2) * D:(nt % 2 + 1) * D],
                            xtile[:, :],
                            stats[:, 6:7], stats[:, 7:8],
                            ALU.mult, ALU.add).then_inc(s_ln, 1)
            v.wait_ge(s_tp, nt + 1)
            bk = arena[:, (nt % 2) * 512:(nt % 2) * 512 + 512].bitcast(BF16)
            slot = nt % 2
            v.tensor_copy(
                XlnTr[:, :].rearrange("p (s kc t) -> p s kc t", kc=KC, t=128)[:, slot],
                bk[:, :].rearrange("p (kc t) -> p kc t", t=128),
            ).then_inc(s_xlnT, 1)
            for which in range(3):
                v.wait_ge(s_qk, qk_idx(nt, which))
                if which < 2 and nt >= 2:
                    v.wait_ge(s_qt, 2 * (nt - 2) + which + 1)
                pb = arena[:, 1024 + which * 512: 1024 + which * 512 + (HW if which == 0 else KVW)]
                if which == 2:
                    nhv = 1 if l1 else nh
                    v.tensor_copy(
                        Vs[:, :].rearrange("p (b h c2) -> p b h c2", h=nhv, c2=65)[:, nt, :, 0:64],
                        pb.rearrange("p (h c) -> p h c", c=64),
                    ).then_inc(s_rot, 1)
                    continue
                nhr = 1 if (l1 and which == 1) else nh
                ngrp = (nhr + 3) // 4
                for grp in range(ngrp):
                    gh = min(4, nhr - grp * 4)
                    wpb = pb[:, grp * 4 * 64: grp * 4 * 64 + gh * 64].rearrange("p (h c) -> p h c", c=64)
                    ev = wpb[:, :, 0:32]
                    od = wpb[:, :, 32:64]
                    ctb = crep[:, nt * 128: nt * 128 + gh * 32].rearrange("p (h c) -> p h c", c=32)
                    stb = srep[:, nt * 128: nt * 128 + gh * 32].rearrange("p (h c) -> p h c", c=32)
                    w32 = gh * 32
                    t1 = t14[:, 0 * w32:1 * w32].rearrange("p (h c) -> p h c", c=32)
                    t2 = t14[:, 1 * w32:2 * w32].rearrange("p (h c) -> p h c", c=32)
                    t3 = t14[:, 2 * w32:3 * w32].rearrange("p (h c) -> p h c", c=32)
                    t4 = t14[:, 3 * w32:4 * w32].rearrange("p (h c) -> p h c", c=32)
                    qr = qrot[:, ((nt % 2) * 2 + which) * HW + grp * 4 * 64:
                              ((nt % 2) * 2 + which) * HW + grp * 4 * 64 + gh * 64].rearrange("p (h c) -> p h c", c=64)
                    vstep(v.tensor_tensor(t1, ev, ctb, ALU.mult))
                    vstep(v.tensor_tensor(t2, od, stb, ALU.mult))
                    vstep(v.tensor_tensor(t3, od, ctb, ALU.mult))
                    vstep(v.tensor_tensor(t4, ev, stb, ALU.mult))
                    last = (grp == ngrp - 1)
                    if not l1:
                        vstep(v.tensor_sub(qr[:, :, 0:32], t1, t2))
                        ins = v.tensor_add(qr[:, :, 32:64], t3, t4)
                    else:
                        vstep(v.tensor_add(qr[:, :, 0:32], t1, t2))
                        ins = v.tensor_sub(qr[:, :, 32:64], t4, t3)
                    if last:
                        ins.then_inc(s_rot, 1)
                    else:
                        ins.then_inc(s_vch, 1)
                        vch[0] += 1
                        v.wait_ge(s_vch, vch[0])
            for which in range(2):
                v.wait_ge(s_qt, nt * 2 + which + 1)
                bkq = arena[:, (5 + which) * 512:(5 + which) * 512 + 512].bitcast(BF16)
                if l1 and which == 1:
                    v.tensor_copy(KTs[0:64, nt * 128:(nt + 1) * 128], bkq[0:64, 0:128]).then_inc(s_qts, 1)
                    continue
                dst = (QTs if which == 0 else KTs)
                v.tensor_copy(
                    dst[:, :].rearrange("p (np t) -> p np t", t=T)[:, :, nt * 128:(nt + 1) * 128],
                    bkq[:, :].rearrange("p (np t) -> p np t", t=128)[:, 0:NP],
                ).then_inc(s_qts, 1)
        for (p, qt) in pq_list:
            u0 = ubase[(p, qt)]
            nkb = 4 * qt + 4
            for kb in range(4 * qt, nkb):
                u = u0 + kb + 1
                b = kb - 4 * qt
                v.wait_ge(s_exp, u)
                ub = (u - 1) % 2
                v.tensor_tensor(Pp[:, ub * 1024:ub * 1024 + 512],
                                Pp[:, ub * 1024:ub * 1024 + 512],
                                masks[:, b * 512:(b + 1) * 512], ALU.mult).then_inc(s_vch, 1)
                v.wait_ge(s_vch, vch[0] + 1)
                vch[0] += 1
                v.tensor_tensor(Pp[:, ub * 1024 + 512:(ub + 1) * 1024],
                                Pp[:, ub * 1024 + 512:(ub + 1) * 1024],
                                masks[:, b * 512:(b + 1) * 512], ALU.mult).then_inc(s_msk, 1)
            pq = pqi(p, qt)
            v.wait_ge(s_pv, u0 + nkb)
            vstep(v.tensor_copy(zb[0:1, 0:512], arena[64:65, 2048:2560]))
            vstep(v.tensor_copy(zb[0:1, 512:1024], arena[64:65, 2560:3072]))
            vstep(v.reciprocal(zb[0:1, :], zb[0:1, :]))
            v.tensor_copy(rzb[0:1, :], zb[0:1, :]).then_inc(s_nrm, 1)
            v.wait_ge(s_bc, pq + 1)
            vstep(v.tensor_copy(rb[0:64, :], arena[0:64, 3072:4096]))
            v.tensor_tensor(Ob[0:64, p * T + qt * 512: p * T + (qt + 1) * 512],
                            arena[0:64, 2048:2560], rb[0:64, 0:512], ALU.mult).then_inc(s_on, 1)
            if pq >= 2:
                v.wait_ge(s_ores[pq % 2], 16 * (pq // 2))
            v.tensor_tensor(otmp[0:64, (pq % 2) * 512:(pq % 2) * 512 + 512],
                            arena[0:64, 2560:3072], rb[0:64, 512:1024], ALU.mult).then_inc(s_on, 1)

    @block.tensor
    def _(t):
        t.wait_ge(s_pre, 16 * (4 if l1 else 3))
        t.wait_ge(s_gch, NCONST)
        for nt in range(NTB):
            t.wait_ge(s_ln, nt + 1)
            if nt >= 2:
                t.wait_ge(s_xlnT, nt - 1)
            bk = arena[:, (nt % 2) * 512:(nt % 2) * 512 + 512].bitcast(BF16)
            for kc in range(KC):
                ins = t.transpose(bk[:, kc * 128:(kc + 1) * 128],
                                  xlnbf[:, (nt % 2) * D + kc * 128:(nt % 2) * D + (kc + 1) * 128],
                                  ident[:, :])
            ins.then_inc(s_tp, 1)
            t.wait_ge(s_xlnT, nt + 1)
            slot = nt % 2
            for which in range(3):
                w = HW if which == 0 else KVW
                idx = qk_idx(nt, which)
                if idx > 3:
                    t.wait_ge(s_rot, idx - 3)
                pb = arena[:, 1024 + which * 512: 1024 + which * 512 + w]
                for kc in range(KC):
                    ins = t.matmul(pb,
                                   XlnTr[:, slot * KC * 128 + kc * 128: slot * KC * 128 + (kc + 1) * 128],
                                   wq[:, kc * WTOT + (0 if which == 0 else HW + (which - 1) * KVW):
                                      kc * WTOT + (0 if which == 0 else HW + (which - 1) * KVW) + w],
                                   start=(kc == 0), stop=(kc == KC - 1 and not l1))
                if l1:
                    ins = t.matmul(pb, onesc[0:1, 0:128],
                                   biass[0:1, (0 if which == 0 else HW + (which - 1) * KVW):
                                         (0 if which == 0 else HW + (which - 1) * KVW) + w],
                                   start=False, stop=True)
                ins.then_inc(s_qk, 1)
            for which in range(2):
                t.wait_ge(s_rot, qk_idx(nt, which))
                if nt >= 2:
                    t.wait_ge(s_qts, 2 * (nt - 1))
                nblk = 1 if (l1 and which == 1) else NP
                bkq = arena[:, (5 + which) * 512:(5 + which) * 512 + 512].bitcast(BF16)
                for i in range(nblk):
                    ins = t.transpose(bkq[:, i * 128:(i + 1) * 128],
                                      qrot[:, ((nt % 2) * 2 + which) * HW + i * 128:((nt % 2) * 2 + which) * HW + (i + 1) * 128],
                                      ident[:, :])
                ins.then_inc(s_qt, 1)
        t.wait_ge(s_qts, 2 * NTB)
        t.wait_ge(s_rot, 3 * NTB)
        if l1:
            t.wait_ge(s_kd, 16)
        for (p, qt) in pq_list:
            pq = pqi(p, qt)
            u0 = ubase[(p, qt)]
            nkb = 4 * qt + 4
            for kb in range(nkb):
                u = u0 + kb + 1
                ub = (u - 1) % 2
                diag = kb >= 4 * qt
                if u > 2:
                    t.wait_ge(s_exp, u - 2)
                kt_off = (0 if l1 else p * T)
                for hh_ in range(2):
                    lo = hh_ * 64
                    ins = t.matmul(arena[:, ub * 1024 + hh_ * 512: ub * 1024 + (hh_ + 1) * 512],
                                   KTs[lo:lo + 64, kt_off + kb * 128: kt_off + (kb + 1) * 128],
                                   QTs[lo:lo + 64, p * T + qt * 512: p * T + (qt + 1) * 512],
                                   start=True, stop=True)
                ins.then_inc(s_S, 1)
                if kb == 0:
                    t.wait_ge(s_on, 2 * pq)
                t.wait_ge(s_msk if diag else s_exp,
                          (dbase[(p, qt)] + kb - 4 * qt + 1) if diag else u)
                nhv = 1 if l1 else nh
                for hh_ in range(2):
                    hv = 0 if l1 else (2 * p + hh_)
                    ins = t.matmul(arena[:, 2048 + hh_ * 512: 2048 + hh_ * 512 + 512][0:65, :],
                                   Vs[:, kb * (nhv * 65) + hv * 65: kb * (nhv * 65) + (hv + 1) * 65],
                                   Pp[:, ub * 1024 + hh_ * 512: ub * 1024 + (hh_ + 1) * 512],
                                   start=(kb == 0), stop=(kb == nkb - 1))
                ins.then_inc(s_pv, 1)
            t.wait_ge(s_nrm, pq + 1)
            t.matmul(arena[0:64, 3072:3584], onesc[0:1, 0:64], rzb[0:1, 0:512],
                     start=True, stop=True)
            ins = t.matmul(arena[0:64, 3584:4096], onesc[0:1, 0:64], rzb[0:1, 512:1024],
                           start=True, stop=True)
            ins.then_inc(s_bc, 1)

    ctx.close()
    return nc


def build_proj(tok=TOK, l1=False):
    NT = tok // 128
    nc = bass.Bass("TRN2", target_bir_lowering=False, debug=False)
    x_in = nc.dram_tensor("x_in", [tok, D], F32, kind="ExternalInput").ap()
    oTr = nc.dram_tensor("oTr", [D, tok], BF16, kind="ExternalInput").ap()
    wor = nc.dram_tensor("wor", [128, KC * D], F32, kind="ExternalInput").ap()
    if l1:
        bor = nc.dram_tensor("bor", [1, D], F32, kind="ExternalInput").ap()
    x_out = nc.dram_tensor("x_out", [tok, D], F32, kind="ExternalOutput").ap()

    ctx = ExitStack()
    sb = lambda name, shape, dt: ctx.enter_context(nc.sbuf_tensor(name, shape, dt))
    sem = lambda name: ctx.enter_context(nc.semaphore(name))

    Xs = sb("Xs", [128, NT * D], F32)
    oTs = sb("oTs", [128, KC * tok], BF16)
    wos = sb("wos", [128, KC * D], BF16)
    onesc = sb("onesc", [1, 128], BF16)
    if l1:
        bos = sb("bos", [1, D], BF16)
    arena = ctx.enter_context(nc.psum_tensor("arena", [128, 4096], F32))

    s_x = sem("s_x")
    s_pre = sem("s_pre")
    s_gch = sem("s_gch")
    s_y = sem("s_y")
    s_ye = sem("s_ye")
    s_out = sem("s_out")

    block = ctx.enter_context(nc.Block())

    @block.sync
    def _(sy):
        sy.dma_start(out=Xs[:, :].rearrange("p (nt d) -> p nt d", d=D),
                     in_=x_in.rearrange("(nt p) d -> p nt d", p=128)).then_inc(s_x, 16)
        sy.dma_start(out=oTs[:, :].rearrange("p (kc t) -> p kc t", t=tok),
                     in_=oTr.rearrange("(kc p) t -> p kc t", p=128)).then_inc(s_x, 16)

    @block.gpsimd
    def _(g):
        g.dma_start(out=wos[:, :], in_=wor).then_inc(s_pre, 16)
        if l1:
            g.dma_start(out=bos[:, :], in_=bor).then_inc(s_pre, 16)
        g.memset(onesc[:, :], 1.0).then_inc(s_gch, 1)

    @block.tensor
    def _(t):
        t.wait_ge(s_x, 32)
        t.wait_ge(s_pre, 32 if l1 else 16)
        t.wait_ge(s_gch, 1)
        for nt in range(NT):
            for dh in range(2):
                idx = nt * 2 + dh + 1
                if idx > 4:
                    t.wait_ge(s_ye, idx - 4)
                pb = arena[:, ((idx - 1) % 4) * 512: ((idx - 1) % 4) * 512 + 512]
                for cc in range(KC):
                    ins = t.matmul(pb,
                                   oTs[:, cc * tok + nt * 128: cc * tok + (nt + 1) * 128],
                                   wos[:, cc * D + dh * 512: cc * D + (dh + 1) * 512],
                                   start=(cc == 0), stop=(cc == KC - 1 and not l1))
                if l1:
                    ins = t.matmul(pb, onesc[0:1, 0:128], bos[0:1, dh * 512:(dh + 1) * 512],
                                   start=False, stop=True)
                ins.then_inc(s_y, 1)

    @block.vector
    def _(v):
        for nt in range(NT):
            for dh in range(2):
                idx = nt * 2 + dh + 1
                v.wait_ge(s_y, idx)
                pb = arena[:, ((idx - 1) % 4) * 512: ((idx - 1) % 4) * 512 + 512]
                v.tensor_tensor(Xs[:, nt * D + dh * 512: nt * D + (dh + 1) * 512],
                                pb,
                                Xs[:, nt * D + dh * 512: nt * D + (dh + 1) * 512],
                                ALU.add).then_inc(s_ye, 1)

    @block.sync
    def _(sy):
        sy.wait_ge(s_ye, NT * 2)
        sy.dma_start(out=x_out.rearrange("(nt p) d -> p nt d", p=128),
                     in_=Xs[:, :].rearrange("p (nt d) -> p nt d", d=D)).then_inc(s_out, 16)
        sy.wait_ge(s_out, 16)

    ctx.close()
    return nc



def _rope_tabs(T, nh):
    inv = 1.0 / (10000.0 ** (np.arange(0, 64, 2, dtype=np.float64) / 64.0))
    pos = np.arange(T, dtype=np.float64)
    ang = np.outer(pos, inv)
    c = np.cos(ang).astype(np.float32)
    s = np.sin(ang).astype(np.float32)
    ntb = T // 128
    crep = np.broadcast_to(c.reshape(ntb, 128, 1, 32), (ntb, 128, nh, 32))
    srep = np.broadcast_to(s.reshape(ntb, 128, 1, 32), (ntb, 128, nh, 32))
    crep = np.ascontiguousarray(crep.transpose(1, 0, 2, 3).reshape(128, ntb * nh * 32))
    srep = np.ascontiguousarray(srep.transpose(1, 0, 2, 3).reshape(128, ntb * nh * 32))
    return crep, srep


def _eo_perm(n=64):
    return np.concatenate([np.arange(0, n, 2), np.arange(1, n, 2)])


def prep_attn0(w_qkv, heads, T=4096):
    nh = len(heads)
    perm = _eo_perm()
    cols = []
    for part in range(3):
        for h in heads:
            blk = w_qkv[:, part * D + h * 64: part * D + (h + 1) * 64]
            cols.append(blk[:, perm] if part < 2 else blk)
    wsl = np.concatenate(cols, axis=1)
    wqkvr = np.ascontiguousarray(
        wsl.reshape(KC, 128, 3 * nh * 64).transpose(1, 0, 2).reshape(128, -1))
    crep, srep = _rope_tabs(T, 4)
    return {"wqkvr": wqkvr, "ctabr": crep, "stabr": srep}


def prep_attn1(Wq, bq, Wk, bk, Wv, bv, heads, T=4096):
    nh = len(heads)
    qcols = np.concatenate([Wq[:, h * 64:(h + 1) * 64] for h in heads], axis=1)
    wsl = np.concatenate([qcols, Wk, Wv], axis=1)
    wqkvr = np.ascontiguousarray(wsl.reshape(KC, 128, wsl.shape[1]).transpose(1, 0, 2).reshape(128, -1))
    bias = np.concatenate([np.concatenate([bq[h * 64:(h + 1) * 64] for h in heads]), bk, bv])
    crep, srep = _rope_tabs(T, 4)
    return {"wqkvr": wqkvr, "ctabr": crep, "stabr": srep, "biasr": bias.reshape(1, -1).astype(np.float32)}


def prep_proj(w_out):
    wor = np.ascontiguousarray(w_out.reshape(KC, 128, D).transpose(1, 0, 2).reshape(128, KC * D))
    return wor


def run_attn_layer(x_flat, inputs, layer, nc_cache={}):
    import ml_dtypes
    key = f"attn{layer}"
    if key not in nc_cache:
        nc_cache[key] = build_attn0(T=4096, nh=8, l1=(layer == 1))
        nc_cache[f"proj{layer}"] = build_proj(l1=(layer == 1))
    nc_a = nc_cache[key]
    nc_p = nc_cache[f"proj{layer}"]
    in_maps = []
    for c in range(8):
        b = c // 2
        heads = list(range(8 * (c % 2), 8 * (c % 2) + 8))
        xb = np.ascontiguousarray(x_flat[b * 4096:(b + 1) * 4096])
        if layer == 0:
            m = prep_attn0(inputs["L0_attn_w_qkv"], heads)
        else:
            m = prep_attn1(inputs["L1_attn_Wq"], inputs["L1_attn_bq"],
                           inputs["L1_attn_Wk"], inputs["L1_attn_bk"],
                           inputs["L1_attn_Wv"], inputs["L1_attn_bv"], heads)
        m["xb"] = xb
        in_maps.append(m)
    res = run_bass_kernel_spmd(nc_a, in_maps, core_ids=list(range(8)))
    wo = prep_proj(inputs["L0_attn_w_out"] if layer == 0 else inputs["L1_attn_Wo"])
    in_maps2 = []
    for c in range(8):
        b = c // 2
        o_lo = res.results[2 * b]["o_out"]
        o_hi = res.results[2 * b + 1]["o_out"]
        oT = np.concatenate([o_lo.reshape(512, 4096), o_hi.reshape(512, 4096)], axis=0)
        oT = np.ascontiguousarray(oT[:, (c % 2) * 2048:(c % 2) * 2048 + 2048])
        m = {"x_in": np.ascontiguousarray(x_flat[c * TOK:(c + 1) * TOK]),
             "oTr": oT, "wor": wo}
        if layer == 1:
            m["bor"] = np.ascontiguousarray(inputs["L1_attn_bo"].reshape(1, D).astype(np.float32))
        in_maps2.append(m)
    res2 = run_bass_kernel_spmd(nc_p, in_maps2, core_ids=list(range(8)))
    return np.concatenate([res2.results[c]["x_out"] for c in range(8)], axis=0)



class FCtx:
    def __init__(self, nc):
        self.nc = nc
        self.sems = {}
        self.cnt = {}
        self.buf = {}
        self.arena = None

    def mksem(self, name, stack):
        s = stack.enter_context(self.nc.semaphore(name))
        self.sems[name] = s
        return s

    def vstep(self, v, ins):
        self.cnt["vch"] = self.cnt.get("vch", 0) + 1
        ins.then_inc(self.sems["vch"], 1)
        v.wait_ge(self.sems["vch"], self.cnt["vch"])

    def gstep(self, g, ins):
        self.cnt["gch"] = self.cnt.get("gch", 0) + 1
        ins.then_inc(self.sems["gch"], 1)
        g.wait_ge(self.sems["gch"], self.cnt["gch"])

    def inc(self, ins, name, amt=1):
        ins.then_inc(self.sems[name], amt)
        return ins


def emit_ln_phase(ctx, xs_ready, xs_tile=None):
    b = ctx.buf
    S = ctx.sems
    C = ctx.cnt
    tok = ctx.tok
    NT = tok // 128
    B = {k: C.get(k, 0) for k in ("ln_sq", "ln_sum", "ln_var", "ln_std", "ln_rs",
                                  "ln_xln", "ln_tp", "ln_xlnT")}

    G = 4 if NT % 4 == 0 else NT
    NGR = NT // G

    def scalar(s):
        def copy_tile(s, nt):
            s.wait_ge(S["ln_tp"], B["ln_tp"] + nt + 1)
            bk = ctx.arena[:, (nt % 2) * 512:(nt % 2) * 512 + 512].bitcast(BF16)
            s.activation(
                out=b["XlnT"][:, :].rearrange("p (kc t) -> p kc t", t=tok)[:, :, nt * 128:(nt + 1) * 128],
                in_=bk[:, :].rearrange("p (kc t) -> p kc t", t=128),
                func=AF.Copy).then_inc(S["ln_xlnT"], 1)
        xs_ready(s)
        for g in range(NGR):
            for k in range(G):
                nt = g * G + k
                if xs_tile is not None:
                    xs_tile(s, nt)
                if nt > 0:
                    s.wait_ge(S["ln_sq"], B["ln_sq"] + nt)
                if B["ln_sq"] > 0 and nt == 0:
                    s.wait_ge(S["ln_var"], B["ln_var"])
                s.activation(out=b["junk"][:, 0:D],
                             in_=b["Xs"][:, nt * D:(nt + 1) * D], func=AF.Square,
                             accum_out=b["sqs"][:, nt:nt + 1]).then_inc(S["ln_sq"], 1)
            if g >= 1:
                for k in range(G):
                    copy_tile(s, (g - 1) * G + k)
            s.wait_ge(S["ln_var"], B["ln_var"] + g + 1)
            s.activation(out=b["stds"][:, g * G:(g + 1) * G], in_=b["vars_"][:, g * G:(g + 1) * G],
                         func=AF.Sqrt, bias=b["epst"][:, 0:1]).then_inc(S["ln_std"], 1)
        for k in range(G):
            copy_tile(s, (NGR - 1) * G + k)

    def vector(v):
        xs_ready(v)
        for g in range(NGR):
            sl = slice(g * G, (g + 1) * G)
            for k in range(G):
                nt = g * G + k
                if xs_tile is not None:
                    xs_tile(v, nt)
                v.reduce_sum(b["sums"][:, nt:nt + 1], b["Xs"][:, nt * D:(nt + 1) * D],
                             axis=AX.X).then_inc(S["ln_sum"], 1)
            v.wait_ge(S["ln_sq"], B["ln_sq"] + (g + 1) * G)
            v.wait_ge(S["ln_sum"], B["ln_sum"] + (g + 1) * G)
            ctx.vstep(v, v.tensor_scalar_mul(b["means"][:, sl], b["sums"][:, sl], 1.0 / D))
            ctx.vstep(v, v.tensor_mul(b["m2s"][:, sl], b["means"][:, sl], b["means"][:, sl]))
            v.scalar_tensor_tensor(b["vars_"][:, sl], b["sqs"][:, sl], 1.0 / D,
                                   b["m2s"][:, sl], ALU.mult, ALU.subtract).then_inc(S["ln_var"], 1)
            v.wait_ge(S["ln_std"], B["ln_std"] + g + 1)
            ctx.vstep(v, v.reciprocal(b["rss"][:, sl], b["stds"][:, sl]))
            v.scalar_tensor_tensor(b["negmrs"][:, sl], b["means"][:, sl], -1.0,
                                   b["rss"][:, sl], ALU.mult, ALU.mult).then_inc(S["ln_rs"], 1)
            v.wait_ge(S["ln_rs"], B["ln_rs"] + g + 1)
            for k in range(G):
                nt = g * G + k
                if nt > 0:
                    v.wait_ge(S["ln_tp"], B["ln_tp"] + nt)
                v.tensor_scalar(b["xlnbf"][:, :], b["Xs"][:, nt * D:(nt + 1) * D],
                                b["rss"][:, nt:nt + 1], b["negmrs"][:, nt:nt + 1],
                                ALU.mult, ALU.add).then_inc(S["ln_xln"], 1)

    def tensor(t):
        xs_ready(t)
        for nt in range(NT):
            t.wait_ge(S["ln_xln"], B["ln_xln"] + nt + 1)
            if nt >= 2:
                t.wait_ge(S["ln_xlnT"], B["ln_xlnT"] + nt - 1)
            elif B["ln_xlnT"] > 0:
                t.wait_ge(S["ln_xlnT"], B["ln_xlnT"])
            bk = ctx.arena[:, (nt % 2) * 512:(nt % 2) * 512 + 512].bitcast(BF16)
            for kc in range(KC):
                ins = t.transpose(bk[:, kc * 128:(kc + 1) * 128],
                                  b["xlnbf"][:, kc * 128:(kc + 1) * 128], b["ident"][:, :])
            ins.then_inc(S["ln_tp"], 1)

    for k, dv in dict(ln_sq=NT, ln_sum=NT, ln_var=NGR, ln_std=NGR, ln_rs=NGR,
                      ln_xln=NT, ln_tp=NT, ln_xlnT=NT).items():
        C[k] = B[k] + dv
    return {"scalar": scalar, "vector": vector, "tensor": tensor}


def emit_ffn0(ctx, widx, guard=None):
    b = ctx.buf
    S = ctx.sems
    C = ctx.cnt
    tok = ctx.tok
    ne, nj = ctx.ne0, ctx.nj0
    NST = max(tok // 512, 1)
    STW = tok // NST
    NTT = STW // 128
    NJH = 2
    JH = nj // NJH
    NRND = ne * NJH
    GT = min(4, JH)
    NG = JH // GT
    RING = max(NG, 2)
    sc = b["scratch"]
    w1ring = sc[:, 0:16384]
    w2ring = sc[:, 16384:32768]
    Hss = sc[:, 32768:40960]
    Hgel = sc[:, 40960:41984]
    cwbs = sc[:, 41984:43008]
    arena = ctx.arena
    names = ("z", "g", "cw", "cwt", "cwts", "h", "gel", "cwm", "selm", "cwbc",
             "fb", "ye") + tuple(f"w1_{i}" for i in range(RING)) + tuple(f"w2_{i}" for i in range(RING))
    B = {k: C.get(k, 0) for k in names}

    def hcount(rnd, st, j):
        return (rnd * NST + st) * JH + j + 1

    def fbcount(rnd, st, tti, j):
        return ((rnd * NST + st) * NTT + tti) * JH + j + 1

    def evacs_before(rnd, st):
        return (rnd * NST + st) * NTT

    def hbank(j):
        return arena[:, 3072 + (j % 2) * 512: 3072 + (j % 2) * 512 + STW]

    def gpsimd(g):
        if guard is not None:
            for (gs_, gv_) in guard:
                g.wait_ge(S[gs_], gv_)
        for rnd in range(NRND):
            e, jh = divmod(rnd, NJH)
            for q in range(NG):
                for which in (0, 1):
                    gidx = rnd * NG + q
                    slot = gidx % RING
                    relname = ("h" if which == 0 else "fb")
                    if gidx >= RING:
                        prnd, pq_ = divmod(gidx - RING, NG)
                        if which == 0:
                            g.wait_ge(S["h"], B["h"] + hcount(prnd, NST - 1, pq_ * GT + GT - 1))
                        else:
                            g.wait_ge(S["fb"], B["fb"] + fbcount(prnd, NST - 1, NTT - 1, pq_ * GT + GT - 1))
                    elif widx > 0 or B[relname] > 0:
                        g.wait_ge(S[relname], B[relname])
                    gg = (jh * JH + q * GT) // GT
                    ring = w1ring if which == 0 else w2ring
                    wsname = f"w{which + 1}_{slot}"
                    g.dma_start(out=ring[:, slot * GT * D:(slot + 1) * GT * D],
                                in_=(b["w1r"] if which == 0 else b["w2r"])[e, gg]
                                ).then_inc(S[wsname], 16)

    def scalar(s):
        s.wait_ge(S["z"], B["z"] + (tok // 128))
        s.activation(out=b["gsm"][:, 0:(tok // 128) * ne], in_=arena[:, 1024:1024 + (tok // 128) * ne],
                     func=AF.Exp).then_inc(S["g"], 1)
        for rnd in range(NRND):
            e, jh = divmod(rnd, NJH)
            for st in range(NST):
                for j in range(JH):
                    idx = hcount(rnd, st, j)
                    s.wait_ge(S["h"], B["h"] + idx)
                    if idx > 2:
                        s.wait_ge(S["cwm"], B["cwm"] + idx - 2)
                    elif B["cwm"] > 0:
                        s.wait_ge(S["cwm"], B["cwm"])
                    s.activation(out=Hgel[:, (j % 2) * STW:(j % 2) * STW + STW],
                                 in_=hbank(j), func=AF.Gelu,
                                 bias=b["b1s"][:, e * nj + jh * JH + j: e * nj + jh * JH + j + 1],
                                 ).then_inc(S["gel"], 1)

    def vector(v):
        NT = tok // 128
        v.wait_ge(S["g"], B["g"] + 1)
        for nt in range(NT):
            gt = b["gsm"][:, nt * ne:(nt + 1) * ne]
            ctx.vstep(v, v.reduce_sum(b["r1"][:, :], gt, axis=AX.X))
            ctx.vstep(v, v.reciprocal(b["r2"][:, :], b["r1"][:, :]))
            ctx.vstep(v, v.tensor_scalar(b["gs1"][:, 0:ne], gt, b["r2"][:, 0:1], None, ALU.mult))
            ctx.vstep(v, v.reduce_max(b["r1"][:, :], b["gs1"][:, 0:ne], axis=AX.X))
            ctx.vstep(v, v.tensor_scalar(b["gs2"][:, 0:ne], b["gs1"][:, 0:ne], b["r1"][:, 0:1], None, ALU.is_ge))
            ctx.vstep(v, v.scalar_tensor_tensor(b["gs3"][:, 0:ne], b["gs2"][:, 0:ne], -1e30,
                                                b["gs1"][:, 0:ne], ALU.mult, ALU.add))
            ctx.vstep(v, v.reduce_max(b["r1"][:, :], b["gs3"][:, 0:ne], axis=AX.X))
            ctx.vstep(v, v.tensor_scalar(b["gs2"][:, 0:ne], b["gs1"][:, 0:ne], b["r1"][:, 0:1], None, ALU.is_ge))
            ctx.vstep(v, v.scalar_tensor_tensor(b["gs3"][:, 0:ne], b["gs2"][:, 0:ne], 0.5,
                                                b["gs1"][:, 0:ne], ALU.mult, ALU.mult))
            v.tensor_copy(b["cwb_all"][:, nt * ne:(nt + 1) * ne], b["gs3"][:, 0:ne]).then_inc(S["cw"], 1)
            v.wait_ge(S["cw"], B["cw"] + nt + 1)
        v.wait_ge(S["cwt"], B["cwt"] + NT)
        v.tensor_copy(b["cwTs"][0:ne, 0:tok],
                      arena[:, 1536:1536 + tok // 2].bitcast(BF16)[0:ne, :]).then_inc(S["cwts"], 1)
        for rnd in range(NRND):
            for st in range(NST):
                prevs = rnd * NST + st - 1
                v.wait_ge(S["selm"], B["selm"] + rnd * NST + st + 1)
                v.tensor_copy(cwbs[:, (st % 2) * STW:(st % 2) * STW + STW],
                              arena[:, 2560:2560 + STW]).then_inc(S["cwbc"], 1)
                v.wait_ge(S["cwbc"], B["cwbc"] + rnd * NST + st + 1)
                for j in range(JH):
                    idx = hcount(rnd, st, j)
                    v.wait_ge(S["gel"], B["gel"] + idx)
                    if prevs >= 0:
                        v.wait_ge(S["fb"], B["fb"] + fbcount(rnd, st - 1, NTT - 1, j) if st > 0
                                  else B["fb"] + fbcount(rnd - 1, NST - 1, NTT - 1, j))
                    elif B["fb"] > 0:
                        v.wait_ge(S["fb"], B["fb"])
                    v.tensor_tensor(Hss[:, j * STW:(j + 1) * STW],
                                    Hgel[:, (j % 2) * STW:(j % 2) * STW + STW],
                                    cwbs[:, (st % 2) * STW:(st % 2) * STW + STW],
                                    ALU.mult).then_inc(S["cwm"], 1)
                for tti in range(NTT):
                    tt = NTT - 1 - tti
                    tk = st * NTT + tt
                    v.wait_ge(S["fb"], B["fb"] + fbcount(rnd, st, tti, JH - 1))
                    v.tensor_tensor(b["Xs"][:, tk * D:(tk + 1) * D],
                                    arena[:, tt * 1024:(tt + 1) * 1024],
                                    b["Xs"][:, tk * D:(tk + 1) * D],
                                    ALU.add).then_inc(S["ye"], 1)

    def tensor(t):
        NT = tok // 128
        for nt in range(NT):
            t.wait_ge(S["ln_xlnT"], C["ln_xlnT"] - NT + nt + 1)
            if nt == 0 and B["g"] > 0:
                t.wait_ge(S["g"], B["g"])
            for kc in range(KC):
                ins = t.matmul(arena[:, 1024 + nt * ne:1024 + (nt + 1) * ne],
                               b["XlnT"][:, kc * tok + nt * 128: kc * tok + (nt + 1) * 128],
                               b["wgs"][:, kc * ne:(kc + 1) * ne],
                               start=(kc == 0), stop=(kc == KC - 1))
            ins.then_inc(S["z"], 1)
        cwtb = arena[:, 1536:1536 + tok // 2].bitcast(BF16)

        def aphase(rnd, st, jr=None):
            gidx_base = rnd * NG
            prevs = rnd * NST + st - 1
            for j in (range(JH) if jr is None else jr):
                q = j // GT
                gidx = gidx_base + q
                t.wait_ge(S[f"w1_{gidx % RING}"], B[f"w1_{gidx % RING}"] + 16 * (gidx // RING + 1))
                idx = hcount(rnd, st, j)
                if j < 2 and prevs >= 0:
                    t.wait_ge(S["ye"], B["ye"] + prevs * NTT + 1)
                if idx > 2:
                    t.wait_ge(S["gel"], B["gel"] + idx - 2)
                elif B["gel"] > 0 and idx == 1:
                    t.wait_ge(S["gel"], B["gel"])
                slot = gidx % RING
                w1t = w1ring[:, slot * GT * D + (j % GT) * D: slot * GT * D + (j % GT + 1) * D]
                for kc in range(KC):
                    ins = t.matmul(hbank(j), w1t[:, kc * 128:(kc + 1) * 128],
                                   b["XlnT"][:, kc * tok + st * STW: kc * tok + (st + 1) * STW],
                                   start=(kc == 0), stop=(kc == KC - 1))
                ins.then_inc(S["h"], 1)

        PRE = min(JH, 4)
        aphase(0, 0, range(PRE))
        for nt in range(NT):
            t.wait_ge(S["cw"], B["cw"] + nt + 1)
            if nt == 0 and B["cwts"] > 0:
                t.wait_ge(S["cwts"], B["cwts"])
            ins = t.transpose(cwtb[0:ne, nt * 128:(nt + 1) * 128],
                              b["cwb_all"][:, nt * ne:(nt + 1) * ne], b["ident"][:, :])
            ins.then_inc(S["cwt"], 1)
        t.wait_ge(S["g"], B["g"] + 1)
        t.wait_ge(S["cwts"], B["cwts"] + 1)
        for rnd in range(NRND):
            e, jh = divmod(rnd, NJH)
            for st in range(NST):
                gidx_base = rnd * NG
                prevs = rnd * NST + st - 1
                if prevs >= 0:
                    t.wait_ge(S["ye"], B["ye"] + prevs * NTT + 2)
                t.wait_ge(S["cwbc"], B["cwbc"] + rnd * NST + st)
                ins = t.matmul(arena[:, 2560:2560 + STW], b["sel"][0:ne, e * 128:(e + 1) * 128],
                               b["cwTs"][0:ne, st * STW:(st + 1) * STW], start=True, stop=True)
                ins.then_inc(S["selm"], 1)
                if (rnd, st) != (0, 0):
                    aphase(rnd, st)
                else:
                    aphase(0, 0, range(PRE, JH))
                for tti in range(NTT):
                    tt = NTT - 1 - tti
                    for j in range(JH):
                        q = j // GT
                        gidx = gidx_base + q
                        t.wait_ge(S[f"w2_{gidx % RING}"], B[f"w2_{gidx % RING}"] + 16 * (gidx // RING + 1))
                        t.wait_ge(S["cwm"], B["cwm"] + hcount(rnd, st, j))
                        if j == 0:
                            if tt == NTT - 1:
                                t.wait_ge(S["gel"], B["gel"] + hcount(rnd, st, JH - 1))
                            if prevs >= 0:
                                t.wait_ge(S["ye"], B["ye"] + prevs * NTT + tti + 1)
                            if tt == 2 or NTT == 1:
                                t.wait_ge(S["cwbc"], B["cwbc"] + rnd * NST + st + 1)
                        slot = gidx % RING
                        w2t = w2ring[:, slot * GT * D + (j % GT) * D: slot * GT * D + (j % GT + 1) * D]
                        first = (j == 0)
                        last = (j == JH - 1)
                        if first and rnd == 0:
                            t.matmul(arena[:, tt * 1024: tt * 1024 + 512],
                                     b["cwTs"][0:ne, (st * NTT + tt) * 128:(st * NTT + tt + 1) * 128],
                                     b["b2s"][0:ne, 0:512], start=True, stop=False)
                            t.matmul(arena[:, tt * 1024 + 512:(tt + 1) * 1024],
                                     b["cwTs"][0:ne, (st * NTT + tt) * 128:(st * NTT + tt + 1) * 128],
                                     b["b2s"][0:ne, 512:1024], start=True, stop=False)
                        hst = Hss[:, j * STW + tt * 128: j * STW + (tt + 1) * 128]
                        st0 = first and rnd > 0
                        t.matmul(arena[:, tt * 1024: tt * 1024 + 512], hst,
                                 w2t[:, 0:512], start=st0, stop=last)
                        ins = t.matmul(arena[:, tt * 1024 + 512:(tt + 1) * 1024], hst,
                                       w2t[:, 512:1024], start=st0, stop=last)
                        ins.then_inc(S["fb"], 1)

    tot = dict(z=tok // 128, g=1, cw=tok // 128, cwt=tok // 128, cwts=1,
               h=NRND * NST * JH, gel=NRND * NST * JH, cwm=NRND * NST * JH,
               selm=NRND * NST, cwbc=NRND * NST, fb=NRND * NST * NTT * JH,
               ye=NRND * NST * NTT)
    for i in range(RING):
        tot[f"w1_{i}"] = 16 * (NRND * NG // RING)
        tot[f"w2_{i}"] = 16 * (NRND * NG // RING)
    for k, dv in tot.items():
        C[k] = B[k] + dv
    return {"gpsimd": gpsimd, "scalar": scalar, "vector": vector, "tensor": tensor}


def emit_ffn1(ctx, guard=None):
    b = ctx.buf
    S = ctx.sems
    C = ctx.cnt
    tok = ctx.tok
    ne, hh = ctx.ne1, 128
    NT = tok // 128
    NST = max(tok // 512, 1)
    STW = tok // NST
    NTT = STW // 128
    sc = b["scratch"]
    keys = sc[:, 0:16384]
    vals = sc[:, 16384:32768]
    Ssall = sc[:, 32768:40960]
    Sg = sc[:, 40960:41984]
    selbs = sc[:, 41984:43008]
    arena = ctx.arena
    names = ("z", "g", "cw", "cwt", "cwts", "h", "gel", "cwm", "selm", "cwbc",
             "fb", "ye", "w1_0", "w1_1", "w1_2")
    B = {k: C.get(k, 0) for k in names}

    def secount(st, e):
        return st * ne + e + 1

    def gpsimd(g):
        if guard is not None:
            for (gs_, gv_) in guard:
                g.wait_ge(S[gs_], gv_)
        g.dma_start(out=keys[:, 0:ne * KC * hh].rearrange("p (e m) -> p e m", m=KC * hh),
                    in_=b["keyr"].rearrange("e p m -> p e m")).then_inc(S["w1_0"], 16)
        g.dma_start(out=vals[:, 0:ne * D].rearrange("p (e d) -> p e d", d=D),
                    in_=b["valr"].rearrange("e p d -> p e d")).then_inc(S["w1_1"], 16)

    def scalar(s):
        s.wait_ge(S["z"], B["z"] + NT)
        s.activation(out=b["gsm"][:, 0:NT * ne], in_=arena[:, 1024:1024 + NT * ne],
                     func=AF.Sigmoid).then_inc(S["g"], 1)
        for st in range(NST):
            for e in range(ne):
                idx = secount(st, e)
                s.wait_ge(S["h"], B["h"] + idx)
                if idx > 2:
                    s.wait_ge(S["cwm"], B["cwm"] + idx - 2)
                elif B["cwm"] > 0:
                    s.wait_ge(S["cwm"], B["cwm"])
                sbk = arena[:, 3072 + (e % 2) * 512: 3072 + (e % 2) * 512 + STW]
                s.activation(out=Sg[:, (e % 2) * STW:(e % 2) * STW + STW],
                             in_=sbk, func=AF.Relu).then_inc(S["gel"], 1)

    def vector(v):
        v.wait_ge(S["g"], B["g"] + 1)
        ctx.vstep(v, v.tensor_scalar_mul(b["cwb_all"][:, 0:NT * ne], b["gsm"][:, 0:NT * ne], 0.5))
        ctx.ffn1_sgb_mark = ctx.cnt["vch"]
        v.wait_ge(S["cwt"], B["cwt"] + NT)
        v.tensor_copy(b["cwTs"][0:ne, 0:tok],
                      arena[:, 1536:1536 + tok // 2].bitcast(BF16)[0:ne, :]).then_inc(S["cwts"], 1)
        for st in range(NST):
            for e in range(ne):
                idx = secount(st, e)
                v.wait_ge(S["selm"], B["selm"] + idx)
                v.tensor_copy(selbs[:, (e % 2) * STW:(e % 2) * STW + STW],
                              arena[:, 1024:1024 + STW]).then_inc(S["cwbc"], 1)
                v.wait_ge(S["cwbc"], B["cwbc"] + idx)
                v.wait_ge(S["gel"], B["gel"] + idx)
                if st > 0:
                    v.wait_ge(S["fb"], B["fb"] + st)
                elif B["fb"] > 0:
                    v.wait_ge(S["fb"], B["fb"])
                v.tensor_tensor(Ssall[:, e * STW:(e + 1) * STW],
                                Sg[:, (e % 2) * STW:(e % 2) * STW + STW],
                                selbs[:, (e % 2) * STW:(e % 2) * STW + STW],
                                ALU.mult).then_inc(S["cwm"], 1)
            v.wait_ge(S["fb"], B["fb"] + st + 1)
            v.wait_ge(S["ye"], B["ye"] + st * NTT)
            for t in range(NTT):
                tk = st * NTT + t
                v.tensor_tensor(b["Xs"][:, tk * D:(tk + 1) * D],
                                arena[:, t * 1024:(t + 1) * 1024],
                                b["Xs"][:, tk * D:(tk + 1) * D],
                                ALU.add).then_inc(S["ye"], 1)

    def tensor(t):
        t.wait_ge(S["w1_0"], B["w1_0"] + 16)
        t.wait_ge(S["w1_1"], B["w1_1"] + 16)
        for nt in range(NT):
            t.wait_ge(S["ln_xlnT"], C["ln_xlnT"] - NT + nt + 1)
            if nt == 0 and B["g"] > 0:
                t.wait_ge(S["g"], B["g"])
            for kc in range(KC):
                ins = t.matmul(arena[:, 1024 + nt * ne:1024 + (nt + 1) * ne],
                               b["XlnT"][:, kc * tok + nt * 128: kc * tok + (nt + 1) * 128],
                               b["esels"][:, kc * ne:(kc + 1) * ne],
                               start=(kc == 0), stop=(kc == KC - 1))
            ins.then_inc(S["z"], 1)
        seltb = arena[:, 1536:1536 + tok // 2].bitcast(BF16)
        t.wait_ge(S["vch"], ctx.ffn1_sgb_mark)
        for nt in range(NT):
            if nt == 0 and B["cwts"] > 0:
                t.wait_ge(S["cwts"], B["cwts"])
            ins = t.transpose(seltb[0:ne, nt * 128:(nt + 1) * 128],
                              b["cwb_all"][:, nt * ne:(nt + 1) * ne], b["ident"][:, :])
            ins.then_inc(S["cwt"], 1)
        t.wait_ge(S["cwts"], B["cwts"] + 1)
        for st in range(NST):
            if st > 0:
                t.wait_ge(S["ye"], B["ye"] + st * NTT)
            elif B["ye"] > 0:
                t.wait_ge(S["ye"], B["ye"])
            for e in range(ne):
                idx = secount(st, e)
                if idx > 1:
                    t.wait_ge(S["cwbc"], B["cwbc"] + idx - 1)
                ins = t.matmul(arena[:, 1024:1024 + STW], b["sel16"][0:ne, e * 128:(e + 1) * 128],
                               b["cwTs"][0:ne, st * STW:(st + 1) * STW], start=True, stop=True)
                ins.then_inc(S["selm"], 1)
                if idx > 2:
                    t.wait_ge(S["gel"], B["gel"] + idx - 2)
                elif B["gel"] > 0 and idx == 1:
                    t.wait_ge(S["gel"], B["gel"])
                sbk = arena[:, 3072 + (e % 2) * 512: 3072 + (e % 2) * 512 + STW]
                for kc in range(KC):
                    ins = t.matmul(sbk, keys[:, e * KC * hh + kc * hh: e * KC * hh + (kc + 1) * hh],
                                   b["XlnT"][:, kc * tok + st * STW: kc * tok + (st + 1) * STW],
                                   start=(kc == 0), stop=(kc == KC - 1))
                ins.then_inc(S["h"], 1)
            t.wait_ge(S["cwbc"], B["cwbc"] + (st + 1) * ne)
            for e in range(ne):
                t.wait_ge(S["cwm"], B["cwm"] + secount(st, e))
                if e == 0:
                    t.wait_ge(S["gel"], B["gel"] + secount(st, ne - 1))
                first = (e == 0)
                last = (e == ne - 1)
                for tt in range(NTT):
                    hst = Ssall[:, e * STW + tt * 128: e * STW + (tt + 1) * 128]
                    t.matmul(arena[:, tt * 1024: tt * 1024 + 512], hst,
                             vals[:, e * D: e * D + 512], start=first, stop=last)
                    ins = t.matmul(arena[:, tt * 1024 + 512:(tt + 1) * 1024], hst,
                                   vals[:, e * D + 512:(e + 1) * D], start=first, stop=last)
            ins.then_inc(S["fb"], 1)

    tot = dict(z=NT, g=1, cwt=NT, cwts=1, h=NST * ne, gel=NST * ne, cwm=NST * ne,
               selm=NST * ne, cwbc=NST * ne, fb=NST, ye=NST * NTT,
               w1_0=16, w1_1=16)
    for k, dv in tot.items():
        C[k] = B[k] + dv
    return {"gpsimd": gpsimd, "scalar": scalar, "vector": vector, "tensor": tensor}


def emit_attn(ctx, layer, groups, guard):
    b = ctx.buf
    S = ctx.sems
    C = ctx.cnt
    l1 = (layer == 1)
    T = 2 * ctx.tok
    NTB = T // 128
    NTG = NTB // 4
    NQT = T // 512
    NP = 4
    sc = b["scratch"]
    wqbuf = sc[:, 0:6144]
    xring = sc[:, 6144:10240]
    QT = sc[:, 10240:14336]
    KT = sc[:, 14336:18432]
    Vaug = sc[:, 18432:22592]
    qrot = sc[:, 22592:24640]
    Pp = sc[:, 24896:26944]
    otmp = sc[:, 26944:27968]
    rb = sc[:, 27968:29024]
    zbA = sc[:, 26944:27968].bitcast(F32)[64:65, 0:512]
    zbB = sc[:, 27968:29024].bitcast(F32)[64:65, 0:512]
    wos = sc[:, 29024:33120]
    ybuf = sc[:, 33120:37216]
    tsc = [sc[:, 37216 + i * 256: 37216 + (i + 1) * 256] for i in range(4)]
    Ob = b["XlnT"]
    arena = ctx.arena
    SCC = 0.125

    names = ("ago_0", "ago_1", "cc", "wq_0", "wq_1", "wq_w", "xr_0", "xr_1", "qk", "rot", "qt", "qts",
             "S", "exp", "msk", "pv", "nrm", "bc", "on", "rbc", "ore_0", "ore_1",
             "kd", "pj", "yp", "yb", "yr_0", "yr_1", "ya", "yv", "ln_xlnT", "gch")
    B = {k: C.get(k, 0) for k in names}

    def kvw(p):
        return (128, 128) if not l1 else ((64, 64) if p == 0 else (0, 0))

    def pw(p):
        return 384 if not l1 else (256 if p == 0 else 128)

    nrot, nqt, nqts = [], [], []
    for p in range(NP):
        kw, vw = kvw(p)
        for tg in range(NTG):
            nrot.append(1 + (1 if kw else 0) + (1 if vw else 0))
            nqt.append(1 + (1 if kw else 0))
            nqts.append(1 + (1 if kw else 0))
    def pref(a, i):
        return sum(a[:i + 1])
    ROT_T, QT_T, QTS_T = sum(nrot), sum(nqt), sum(nqts)

    pq_list = [(p, qt) for p in range(NP) for qt in range(NQT)]

    def pqi(p, qt):
        return p * NQT + qt

    assert NQT % 2 == 0
    SCHED = []
    ug_ = 0
    dcum_ = 0
    for p in range(NP):
        for qtp in range(NQT // 2):
            qa, qb = 2 * qtp, 2 * qtp + 1
            na, nb = 4 * qa + 4, 4 * qb + 4
            order = []
            for k in range(nb):
                if k < na:
                    order.append((qa, k))
                order.append((qb, k))
            ents = []
            lastpar = {0: 0, 1: 0}
            for idx, (qt_, kb_) in enumerate(order):
                diag = kb_ >= 4 * qt_
                f2par = (1 if ug_ % 2 == 0 else 2) if idx < 2 else 0
                ents.append(("u", p, qt_, kb_, ug_, dcum_, f2par, diag))
                lastpar[ug_ % 2] = ug_
                ug_ += 1
                if diag:
                    dcum_ += 1
            lastA = max(i for i, e in enumerate(ents) if e[2] == qa)
            ents.insert(lastA + 1, ("nv", p, qa))
            ents.append(("nv", p, qb))
            ents.append(("nf", p, qa))
            ents.append(("nf", p, qb))
            ents.append(("nt", p, qa, lastpar[0] + 1))
            ents.append(("nt", p, qb, lastpar[1] + 1))
            SCHED.extend(ents)
    SCHED2 = []
    pend_nt = []
    ucount_since = 0
    for e in SCHED:
        if e[0] == "nt":
            pend_nt.append(e)
            continue
        if e[0] == "u" and pend_nt and e[1] == pend_nt[0][1]:
            SCHED2.append(e)
            ucount_since += 1
            if ucount_since >= 1:
                SCHED2.extend(pend_nt)
                pend_nt = []
                ucount_since = 0
            continue
        if pend_nt and (e[0] != "u"):
            SCHED2.append(e)
            continue
        if pend_nt:
            SCHED2.extend(pend_nt)
            pend_nt = []
            ucount_since = 0
        SCHED2.append(e)
    SCHED2.extend(pend_nt)
    SCHED = SCHED2
    NU = ug_
    ND = dcum_

    def uend(p):
        c = 0
        for e in SCHED:
            if e[0] == "u" and e[1] <= p:
                c += 1
        return c

    pv_at_norm = {}
    _c = 0
    for e in SCHED:
        if e[0] == "u":
            _c += 1
            pv_at_norm[(e[1], e[2])] = _c

    NTBH = ctx.tok // 256
    def sync(sy):
        for tb_ in range(NTBH):
            sy.wait_ge(S["ln_xlnT"], C["ln_xlnT"] - (ctx.tok // 128) + 2 * tb_ + 2)
            agd = b["ag_in_a"] if tb_ < NTBH // 2 else b["ag_in_b"]
            tbl = tb_ if tb_ < NTBH // 2 else tb_ - NTBH // 2
            sy.dma_start(
                out=agd[:, tbl * 2048:(tbl + 1) * 2048],
                in_=b["XlnT"][:, :].rearrange("p (kc tb dt) -> p tb kc dt",
                                              tb=NTBH, dt=256)[:, tb_]
            ).then_inc(S[f"ago_{0 if tb_ < NTBH // 2 else 1}"], 16)
        NG2 = NTB // 2
        for p in range(NP):
            for g2l in range(NG2):
                g2 = p * NG2 + g2l
                nt0 = g2l * 2
                half = (nt0 * 128) // ctx.tok
                tc = (nt0 * 128) % ctx.tok
                ccw = 1 if (tc // 256) < NTBH // 2 else 2
                sy.wait_ge(S["cc"], B["cc"] + ccw)
                if g2 > 1:
                    sy.wait_ge(S["qk"], B["qk"] + 2 * (g2 - 1))
                slot = g2 % 2
                tb = tc // 256
                ago_src = b["ag_out_a"] if tb < NTBH // 2 else b["ag_out_b"]
                tbl2 = tb if tb < NTBH // 2 else tb - NTBH // 2
                sy.dma_start(
                    out=xring[:, slot * 2048:(slot + 1) * 2048],
                    in_=ago_src[half * 128:(half + 1) * 128,
                                tbl2 * 2048:(tbl2 + 1) * 2048]).then_inc(S[f"xr_{slot}"], 16)
            if l1 and p == 0:
                sy.wait_ge(S["qts"], B["qts"] + pref(nqts, NTG - 1))
                sy.dma_start(out=KT[64:128, 0:T], in_=KT[0:64, 0:T]).then_inc(S["kd"], 16)
        NTS = T // 512
        for dch in range(KC):
            sy.wait_ge(S["yp"], B["yp"] + NTS * (dch + 1))
            for r in range(2):
                sy.dma_start(
                    out=b["rs_in"][r * D + dch * 128: r * D + (dch + 1) * 128, :],
                    in_=ybuf[:, r * ctx.tok:(r + 1) * ctx.tok]).then_inc(S["yb"], 16)
        sy.wait_ge(S["cc"], B["cc"] + 3)
        for q_ in range(2):
            qtk = ctx.tok // 2
            sy.dma_start(
                out=sc[:, 0:KC * ctx.tok].rearrange("p (kc t) -> p kc t", t=ctx.tok)[:, :, q_ * qtk:(q_ + 1) * qtk],
                in_=b["rs_out"].rearrange("(kc p) t -> p kc t", p=128)[:, :, q_ * qtk:(q_ + 1) * qtk]
            ).then_inc(S[f"yr_{q_}"], 16)

    def gpsimd(g):
        for (gs, gv) in guard:
            g.wait_ge(S[gs], gv)
        g.wait_ge(S["ln_xlnT"], C["ln_xlnT"])
        ctx.gstep(g, g.memset(
            Vaug.rearrange("p (bl h c2) -> p bl h c2", h=2, c2=65)[:, :, :, 64:65], 1.0))
        for bb in range(4):
            ctx.gstep(g, g.memset(b["masks"][:, bb * 512:(bb + 1) * 512], 0.0))
            ctx.gstep(g, g.affine_select(
                out=b["masks"][:, bb * 512:(bb + 1) * 512],
                in_=b["masks"][:, bb * 512:(bb + 1) * 512],
                compare_op=ALU.is_ge, fill=-1.0e9, base=-128 * bb,
                pattern=[[1, 512]], channel_multiplier=-1))
        ctx.attn_gch_mark = ctx.cnt["gch"]
        g.dma_start(out=wos, in_=b["wor0" if not l1 else "wor1"]).then_inc(S["wq_w"], 16)
        if l1:
            g.dma_start(out=b["bqs"][0:1, 0:1024], in_=b["bqr"]).then_inc(S["wq_w"], 16)
        g.wait_ge(S["ago_0"], B["ago_0"] + 16 * (NTBH // 2))
        g.collective_compute("AllGather", ALU.bypass, replica_groups=groups,
                             ins=[b["ag_in_ta"].ap().opt()],
                             outs=[b["ag_out_ta"].ap().opt()]).then_inc(S["cc"], 1)
        g.wait_ge(S["ago_1"], B["ago_1"] + 16 * (NTBH - NTBH // 2))
        g.collective_compute("AllGather", ALU.bypass, replica_groups=groups,
                             ins=[b["ag_in_tb"].ap().opt()],
                             outs=[b["ag_out_tb"].ap().opt()]).then_inc(S["cc"], 1)
        def reloc(p_):
            for qt_ in range(NQT):
                u = pqi(p_, qt_)
                g.wait_ge(S["on"], B["on"] + 2 * u + 2)
                g.dma_start(out=Ob[64:128, p_ * T + qt_ * 512: p_ * T + (qt_ + 1) * 512],
                            in_=otmp[0:64, (u % 2) * 512:(u % 2) * 512 + 512]).then_inc(S[f"ore_{u % 2}"], 16)
        for p in range(NP):
            slot = p % 2
            if p >= 2:
                g.wait_ge(S["qk"], B["qk"] + 4 * NTG * (p - 1))
            g.dma_start(out=wqbuf[:, slot * 3072: slot * 3072 + KC * pw(p)],
                        in_=b[f"wqp{layer}_{p}"]).then_inc(S[f"wq_{slot}"], 16)
            if p >= 1:
                reloc(p - 1)
        reloc(NP - 1)
        g.wait_ge(S["yb"], B["yb"] + 16 * 16)
        g.collective_compute("ReduceScatter", ALU.add, replica_groups=groups,
                             ins=[b["rs_in_t"].ap().opt()],
                             outs=[b["rs_out_t"].ap().opt()]).then_inc(S["cc"], 1)

    def scalar(s):
        for e in SCHED:
            if e[0] != "u":
                continue
            (_, p, qt, kb, ug, dcum, f2, diag) = e
            u = ug + 1
            s.wait_ge(S["S"], B["S"] + u)
            if u > 2:
                s.wait_ge(S["pv"], B["pv"] + u - 2)
            elif B["pv"] > 0:
                s.wait_ge(S["pv"], B["pv"])
            ub = ug % 2
            s.activation(out=Pp[:, ub * 1024:(ub + 1) * 1024],
                         in_=arena[:, ub * 1024:(ub + 1) * 1024],
                         func=AF.Exp, scale=SCC).then_inc(S["exp"], 1)

    def vector(v):
        for p in range(NP):
            kw, vw = kvw(p)
            for tg in range(NTG):
                gi = p * NTG + tg
                v.wait_ge(S["qk"], B["qk"] + 4 * (gi + 1))
                nr = 0
                for which in range(2):
                    w64 = 128 if which == 0 else kw
                    if w64 == 0:
                        continue
                    if gi >= 2:
                        v.wait_ge(S["qt"], B["qt"] + pref(nqt, gi - 3) + which + 1)
                    nh2 = w64 // 64
                    base_c = 0 if which == 0 else 128
                    pr = arena[:, 0:2048].rearrange("p (tl c) -> p tl c", c=512)[:, :, base_c:base_c + w64] \
                        .rearrange("p tl (h c) -> p tl h c", c=64)
                    ev = pr[:, :, :, 0:32]
                    od = pr[:, :, :, 32:64]
                    ctb = b["crep"][:, tg * 256:(tg + 1) * 256].rearrange("p (tl r c) -> p tl r c", r=2, c=32)[:, :, 0:nh2, :]
                    stb = b["srep"][:, tg * 256:(tg + 1) * 256].rearrange("p (tl r c) -> p tl r c", r=2, c=32)[:, :, 0:nh2, :]
                    t1 = tsc[0].rearrange("p (tl h c) -> p tl h c", h=2, c=32)[:, :, 0:nh2, :]
                    t2 = tsc[1].rearrange("p (tl h c) -> p tl h c", h=2, c=32)[:, :, 0:nh2, :]
                    t3 = tsc[2].rearrange("p (tl h c) -> p tl h c", h=2, c=32)[:, :, 0:nh2, :]
                    t4 = tsc[3].rearrange("p (tl h c) -> p tl h c", h=2, c=32)[:, :, 0:nh2, :]
                    qr = qrot[:, (gi % 2) * 1024 + which * 512: (gi % 2) * 1024 + which * 512 + 512] \
                        .rearrange("p (tl h c) -> p tl h c", h=2, c=64)[:, :, 0:nh2, :]
                    ctx.vstep(v, v.tensor_tensor(t1, ev, ctb, ALU.mult))
                    ctx.vstep(v, v.tensor_tensor(t2, od, stb, ALU.mult))
                    ctx.vstep(v, v.tensor_tensor(t3, od, ctb, ALU.mult))
                    ctx.vstep(v, v.tensor_tensor(t4, ev, stb, ALU.mult))
                    if not l1:
                        ctx.vstep(v, v.tensor_sub(qr[:, :, :, 0:32], t1, t2))
                        ins = v.tensor_add(qr[:, :, :, 32:64], t3, t4)
                    else:
                        ctx.vstep(v, v.tensor_add(qr[:, :, :, 0:32], t1, t2))
                        ins = v.tensor_sub(qr[:, :, :, 32:64], t4, t3)
                    ins.then_inc(S["rot"], 1)
                    nr += 1
                if vw > 0:
                    if p > 0 or B["pv"] > 0:
                        v.wait_ge(S["pv"], B["pv"] + (uend(p - 1) if p > 0 else 0))
                    nhv = 1 if l1 else 2
                    vbase = 128 + kw
                    vsrc = arena[:, 0:2048].rearrange("p (tl c) -> p tl c", c=512)[:, :, vbase:vbase + vw] \
                        .rearrange("p tl (h c) -> p tl h c", c=64)
                    vdst = Vaug[:, tg * 4 * 130:(tg + 1) * 4 * 130].rearrange("p (tl x) -> p tl x", x=130) \
                        .rearrange("p tl (h c) -> p tl h c", c=65)[:, :, 0:nhv, 0:64]
                    v.tensor_copy(vdst, vsrc).then_inc(S["rot"], 1)
                for which in range(2):
                    w64 = 128 if which == 0 else kw
                    if w64 == 0:
                        continue
                    v.wait_ge(S["qt"], B["qt"] + pref(nqt, gi - 1) + which + 1)
                    if p > 0 and tg == 0:
                        v.wait_ge(S["S"], B["S"] + uend(p - 1))
                    bkq = arena[:, (4 + which) * 512:(4 + which) * 512 + 512].bitcast(BF16)
                    dst = (QT if which == 0 else KT)
                    nrow = 128 if (which == 0 or not l1) else 64
                    v.tensor_copy(
                        dst[0:nrow, tg * 512:(tg + 1) * 512].rearrange("p (tl c) -> p tl c", c=128),
                        bkq[0:nrow, 0:512].rearrange("p (tl c) -> p tl c", c=128)).then_inc(S["qts"], 1)
            for e in SCHED:
                if e[1] != p:
                    continue
                if e[0] == "u":
                    continue
                elif e[0] == "nv":
                    qt = e[2]
                    pq = pqi(p, qt)
                    ob_ = 2048 + (qt % 2) * 1024
                    v.wait_ge(S["pv"], B["pv"] + pv_at_norm[(p, qt)])
                    if pq == 0:
                        v.wait_ge(S["ago_0"], B["ago_0"] + 16 * (NTBH // 2))
                        v.wait_ge(S["ago_1"], B["ago_1"] + 16 * (NTBH - NTBH // 2))
                    if pq > 0:
                        v.wait_ge(S["bc"], B["bc"] + pq)
                    with ctx.nc.allow_low_precision(reason="recip f32->bf16 store, same as f32 recip then cast"):
                        v.reciprocal(b["rzb"][0:1, 0:1024],
                                     arena[64:65, ob_:ob_ + 1024]).then_inc(S["nrm"], 1)
                elif e[0] == "nf":
                    qt = e[2]
                    pq = pqi(p, qt)
                    strm = qt % 2
                    ob_ = 2048 + strm * 1024
                    rbrow = strm * 64
                    v.wait_ge(S["bc"], B["bc"] + pq + 1)
                    v.tensor_copy(rb[rbrow:rbrow + 64, 0:1024],
                                  arena[64:128, ob_:ob_ + 1024]).then_inc(S["rbc"], 1)
                    v.wait_ge(S["rbc"], B["rbc"] + pq + 1)
                    v.tensor_tensor(Ob[0:64, p * T + qt * 512: p * T + (qt + 1) * 512],
                                    arena[0:64, ob_:ob_ + 512], rb[rbrow:rbrow + 64, 0:512],
                                    ALU.mult).then_inc(S["on"], 1)
                    if pq >= 2:
                        v.wait_ge(S[f"ore_{pq % 2}"], B[f"ore_{pq % 2}"] + 16 * (pq // 2))
                    v.tensor_tensor(otmp[0:64, (pq % 2) * 512:(pq % 2) * 512 + 512],
                                    arena[0:64, ob_ + 512:ob_ + 1024], rb[rbrow:rbrow + 64, 512:1024],
                                    ALU.mult).then_inc(S["on"], 1)
        NTS = T // 512
        for dch in range(KC):
            for tst in range(NTS):
                idx = dch * NTS + tst + 1
                v.wait_ge(S["pj"], B["pj"] + idx)
                if dch >= 1 and tst == 0:
                    v.wait_ge(S["yb"], B["yb"] + 32 * dch)
                pb = arena[:, ((idx - 1) % 4) * 512:((idx - 1) % 4) * 512 + 512]
                if l1:
                    ins = v.tensor_scalar(ybuf[:, tst * 512:(tst + 1) * 512], pb,
                                          b["bos2"][:, dch:dch + 1], None, ALU.add)
                else:
                    ins = v.tensor_copy(ybuf[:, tst * 512:(tst + 1) * 512], pb)
                ins.then_inc(S["yp"], 1)
        for nt in range(ctx.tok // 128):
            v.wait_ge(S["ya"], B["ya"] + nt + 1)
            pbk = arena[:, (nt % 2) * 512:(nt % 2) * 512 + 512].bitcast(BF16)
            v.tensor_tensor(b["Xs"][:, nt * D:(nt + 1) * D],
                            pbk[:, 0:1024],
                            b["Xs"][:, nt * D:(nt + 1) * D], ALU.add).then_inc(S["yv"], 1)

    def tensor(t):
        t.wait_ge(S["gch"], ctx.attn_gch_mark)
        t.wait_ge(S["wq_w"], B["wq_w"] + (32 if l1 else 16))
        for p in range(NP):
            kw, vw = kvw(p)
            slot = p % 2
            t.wait_ge(S[f"wq_{slot}"], B[f"wq_{slot}"] + 16 * (p // 2 + 1))
            for tg in range(NTG):
                gi = p * NTG + tg
                if gi > 0:
                    t.wait_ge(S["rot"], B["rot"] + pref(nrot, gi - 1))
                for sub in range(2):
                    g2 = gi * 2 + sub
                    xslot = g2 % 2
                    t.wait_ge(S[f"xr_{xslot}"], B[f"xr_{xslot}"] + 16 * (g2 // 2 + 1))
                    for tl2 in range(2):
                        tl = sub * 2 + tl2
                        for kc in range(KC):
                            ins = t.matmul(arena[:, tl * 512: tl * 512 + pw(p)],
                                           xring[:, xslot * 2048 + kc * 256 + tl2 * 128: xslot * 2048 + kc * 256 + (tl2 + 1) * 128],
                                           wqbuf[:, slot * 3072 + kc * pw(p): slot * 3072 + (kc + 1) * pw(p)],
                                           start=(kc == 0), stop=(kc == KC - 1 and not l1))
                        if l1:
                            ins = t.matmul(arena[:, tl * 512: tl * 512 + pw(p)],
                                           b["onesc"][0:1, 0:128],
                                           b["bqs"][0:1, p * 256: p * 256 + pw(p)],
                                           start=False, stop=True)
                        ctx.inc(ins, "qk")
                for which in range(2):
                    w64 = 128 if which == 0 else kw
                    if w64 == 0:
                        continue
                    t.wait_ge(S["rot"], B["rot"] + pref(nrot, gi - 1) + which + 1)
                    if gi >= 1:
                        t.wait_ge(S["qts"], B["qts"] + pref(nqts, gi - 1))
                    if p > 0 and tg == 0 and which == 0:
                        t.wait_ge(S["on"], B["on"] + 2 * p * NQT)
                    nblk = 4
                    bkq = arena[:, (4 + which) * 512:(4 + which) * 512 + 512].bitcast(BF16)
                    for i in range(nblk):
                        ins = t.transpose(bkq[:, i * 128:(i + 1) * 128],
                                          qrot[:, (gi % 2) * 1024 + which * 512 + i * 128:
                                               (gi % 2) * 1024 + which * 512 + (i + 1) * 128],
                                          b["ident"][:, :])
                    ins.then_inc(S["qt"], 1)
            t.wait_ge(S["qts"], B["qts"] + pref(nqts, p * NTG + NTG - 1))
            t.wait_ge(S["rot"], B["rot"] + pref(nrot, p * NTG + NTG - 1))
            if l1:
                t.wait_ge(S["kd"], B["kd"] + 16)
            def emit_S(e):
                (_, _, qt, kb, ug, dcum, f2par, diag) = e
                pq = pqi(p, qt)
                u = ug + 1
                ub = ug % 2
                if u > 2:
                    t.wait_ge(S["exp"], B["exp"] + u - 2)
                elif B["exp"] > 0:
                    t.wait_ge(S["exp"], B["exp"])
                for hh_ in range(2):
                    lo = hh_ * 64
                    ins = t.matmul(arena[:, ub * 1024 + hh_ * 512: ub * 1024 + (hh_ + 1) * 512],
                                   KT[lo:lo + 64, kb * 128:(kb + 1) * 128],
                                   QT[lo:lo + 64, qt * 512:(qt + 1) * 512],
                                   start=True, stop=not diag)
                if diag:
                    bb = kb - 4 * qt
                    for hh_ in range(2):
                        ins = t.matmul(arena[:, ub * 1024 + hh_ * 512: ub * 1024 + (hh_ + 1) * 512],
                                       b["ident"][:, :], b["masks"][:, bb * 512:(bb + 1) * 512],
                                       start=False, stop=True)
                ins.then_inc(S["S"], 1)

            def emit_PV(e):
                (_, _, qt, kb, ug, dcum, f2par, diag) = e
                pq = pqi(p, qt)
                u = ug + 1
                ub = ug % 2
                nkb = 4 * qt + 4
                if kb == 0:
                    if pq >= 2:
                        t.wait_ge(S["on"], B["on"] + 2 * pq - 2)
                        t.wait_ge(S["rbc"], B["rbc"] + pq - 1)
                    else:
                        if B["on"] > 0:
                            t.wait_ge(S["on"], B["on"])
                        if B["rbc"] > 0:
                            t.wait_ge(S["rbc"], B["rbc"])
                t.wait_ge(S["exp"], B["exp"] + u)
                ob_ = 2048 + (qt % 2) * 1024
                for hh_ in range(2):
                    hv = 0 if l1 else hh_
                    ins = t.matmul(arena[:, ob_ + hh_ * 512: ob_ + (hh_ + 1) * 512][0:65, :],
                                   Vaug[:, kb * 130 + hv * 65: kb * 130 + (hv + 1) * 65],
                                   Pp[:, ub * 1024 + hh_ * 512: ub * 1024 + (hh_ + 1) * 512],
                                   start=(kb == 0), stop=(kb == nkb - 1))
                ins.then_inc(S["pv"], 1)

            pend = None
            for e in SCHED:
                if e[1] != p:
                    continue
                if e[0] == "u":
                    emit_S(e)
                    if pend is not None:
                        emit_PV(pend)
                    pend = e
                elif e[0] == "nt":
                    if pend is not None and pqi(p, pend[2]) // 2 == pqi(p, e[2]) // 2:
                        emit_PV(pend)
                        pend = None
                    qt = e[2]
                    pq = pqi(p, qt)
                    bco = 2048 + (qt % 2) * 1024
                    t.wait_ge(S["nrm"], B["nrm"] + pq + 1)
                    t.matmul(arena[64:128, bco:bco + 512], b["onesc"][0:1, 0:64], b["rzb"][0:1, 0:512],
                             start=True, stop=True)
                    ins = t.matmul(arena[64:128, bco + 512:bco + 1024], b["onesc"][0:1, 0:64],
                                   b["rzb"][0:1, 512:1024], start=True, stop=True)
                    ins.then_inc(S["bc"], 1)
            if pend is not None:
                emit_PV(pend)
        npq_ = NP * NQT
        t.wait_ge(S["on"], B["on"] + 2 * npq_)
        t.wait_ge(S["rbc"], B["rbc"] + npq_)
        t.wait_ge(S["ore_0"], B["ore_0"] + 16 * ((npq_ + 1) // 2))
        t.wait_ge(S["ore_1"], B["ore_1"] + 16 * (npq_ // 2))
        NTS = T // 512
        for dch in range(KC):
            for tst in range(NTS):
                idx = dch * NTS + tst + 1
                if idx > 4:
                    t.wait_ge(S["yp"], B["yp"] + idx - 4)
                pb = arena[:, ((idx - 1) % 4) * 512:((idx - 1) % 4) * 512 + 512]
                for cc in range(4):
                    ins = t.matmul(pb,
                                   wos[:, cc * 1024 + dch * 128:cc * 1024 + (dch + 1) * 128],
                                   Ob[:, cc * T + tst * 512: cc * T + (tst + 1) * 512],
                                   start=(cc == 0), stop=(cc == 3))
                ctx.inc(ins, "pj")
        for nt in range(ctx.tok // 128):
            t.wait_ge(S[f"yr_{0 if nt < ctx.tok // 256 else 1}"],
                      B["yr_0" if nt < ctx.tok // 256 else "yr_1"] + 16)
            if nt >= 2:
                t.wait_ge(S["yv"], B["yv"] + nt - 1)
            pbk = arena[:, (nt % 2) * 512:(nt % 2) * 512 + 512].bitcast(BF16)
            for kc in range(KC):
                ins = t.transpose(pbk[:, kc * 128:(kc + 1) * 128],
                                  sc[:, kc * ctx.tok + nt * 128: kc * ctx.tok + (nt + 1) * 128],
                                  b["ident"][:, :])
            ins.then_inc(S["ya"], 1)

    npq = NP * NQT
    NTS = T // 512
    tot = dict(ago_0=16 * (ctx.tok // 512), ago_1=16 * (ctx.tok // 256 - ctx.tok // 512), cc=3, xr_0=16 * NP * NTG, xr_1=16 * NP * NTG,
               qk=4 * NP * NTG, rot=ROT_T, qt=QT_T, qts=QTS_T,
               S=NU, exp=NU, pv=NU, nrm=npq, bc=npq, on=2 * npq,
               rbc=npq, ore_0=16 * ((npq + 1) // 2), ore_1=16 * (npq // 2),
               pj=KC * NTS, yp=KC * NTS, yb=16 * 16, yr_0=16, yr_1=16, ya=ctx.tok // 128, yv=ctx.tok // 128,
               wq_0=16 * 2, wq_1=16 * 2, wq_w=16 * (2 if l1 else 1))
    if l1:
        tot["kd"] = 16
    for k, dv in tot.items():
        C[k] = B[k] + dv
    return {"sync": sync, "gpsimd": gpsimd, "scalar": scalar, "vector": vector, "tensor": tensor}


def build_fused(groups=None, n_cores=8, tok=TOK, ne0=NE, nj0=NJ, ne1=16):
    if groups is None:
        groups = [[2 * i, 2 * i + 1] for i in range(n_cores // 2)]
    T2 = 2 * tok
    gt0 = min(4, nj0 // 2)
    nc = bass.Bass("TRN2", target_bir_lowering=False, debug=False)
    dram = lambda n, s, dt, k: nc.dram_tensor(n, s, dt, kind=k)
    x_in = dram("x_in", [tok, D], F32, "ExternalInput").ap()
    w1r = dram("w1r", [ne0, nj0 // gt0, 128, gt0 * D], F32, "ExternalInput").ap()
    w2r = dram("w2r", [ne0, nj0 // gt0, 128, gt0 * D], F32, "ExternalInput").ap()
    b1r = dram("b1r", [128, ne0 * nj0], F32, "ExternalInput").ap()
    b2r = dram("b2r", [ne0, D], F32, "ExternalInput").ap()
    wgr = dram("wgr", [128, KC * ne0], F32, "ExternalInput").ap()
    keyr = dram("keyr", [ne1, 128, KC * 128], F32, "ExternalInput").ap()
    valr = dram("valr", [ne1, 128, D], F32, "ExternalInput").ap()
    eselr = dram("eselr", [128, KC * ne1], F32, "ExternalInput").ap()
    ctabr = dram("ctabr", [128, T2 // 2], F32, "ExternalInput").ap()
    stabr = dram("stabr", [128, T2 // 2], F32, "ExternalInput").ap()
    wqp = {}
    for p in range(4):
        wqp[(0, p)] = dram(f"wqp0_{p}", [128, KC * 384], F32, "ExternalInput").ap()
        pwl1 = 256 if p == 0 else 128
        wqp[(1, p)] = dram(f"wqp1_{p}", [128, KC * pwl1], F32, "ExternalInput").ap()
    wor0 = dram("wor0", [128, 4 * D], F32, "ExternalInput").ap()
    wor1 = dram("wor1", [128, 4 * D], F32, "ExternalInput").ap()
    bqr = dram("bqr", [1, 1024], F32, "ExternalInput").ap()
    bor = dram("bor", [128, KC], F32, "ExternalInput").ap()
    x_out = dram("x_out", [tok, D], F32, "ExternalOutput").ap()
    ag_in_a = nc.dram_tensor("ag_in_a", [128, KC * tok // 2], BF16)
    ag_in_b = nc.dram_tensor("ag_in_b", [128, KC * tok // 2], BF16)
    ag_out_a = nc.dram_tensor("ag_out_a", [256, KC * tok // 2], BF16)
    ag_out_b = nc.dram_tensor("ag_out_b", [256, KC * tok // 2], BF16)
    rs_in_t = nc.dram_tensor("rs_in", [2 * D, tok], BF16)
    rs_out_t = nc.dram_tensor("rs_out", [D, tok], BF16)

    ctx = FCtx(nc)
    ctx.tok, ctx.ne0, ctx.nj0, ctx.ne1 = tok, ne0, nj0, ne1
    stack = ExitStack()
    sb = lambda n, s, dt: stack.enter_context(nc.sbuf_tensor(n, s, dt))
    b = ctx.buf
    b["Xs"] = sb("Xs", [128, (tok // 128) * D], F32)
    b["XlnT"] = sb("XlnT", [128, KC * tok], BF16)
    b["scratch"] = sb("scratch", [128, 43008], BF16)
    b["junk"] = sb("junk", [128, D], BF16)
    b["xlnbf"] = sb("xlnbf", [128, D], BF16)
    for n in ("sums", "sqs", "means", "m2s", "vars_", "stds", "rss", "negmrs"):
        b[n] = sb(n, [128, 16], F32)
    b["gsm"] = sb("gsm", [128, 256], F32)
    b["cwb_all"] = sb("cwb_all", [128, 256], BF16)
    b["gs1"] = sb("gs1", [128, NE], F32)
    b["gs2"] = sb("gs2", [128, NE], F32)
    b["gs3"] = sb("gs3", [128, NE], F32)
    b["r1"] = sb("r1", [128, 1], F32)
    b["r2"] = sb("r2", [128, 1], F32)
    b["epst"] = sb("epst", [128, 1], F32)
    b["cwTs"] = sb("cwTs", [16, tok], BF16)
    b["sel"] = sb("sel", [16, 16 * 128], BF16)
    b["sel16"] = b["sel"]
    b["ident"] = sb("ident", [128, 128], BF16)
    b["onesc"] = sb("onesc", [1, 128], BF16)
    b["masks"] = b["scratch"][:, 38240:40288]
    b["crep"] = sb("crep", [128, T2 // 2], BF16)
    b["srep"] = sb("srep", [128, T2 // 2], BF16)
    b["b1s"] = sb("b1s", [128, ne0 * nj0], F32)
    b["b2s"] = sb("b2s", [ne0, D], BF16)
    b["wgs"] = sb("wgs", [128, KC * ne0], BF16)
    b["esels"] = sb("esels", [128, KC * ne1], BF16)

    b["rzb"] = b["scratch"][:, 40288:41312]
    b["bqs"] = b["scratch"][:, 41312:42336]
    b["bors"] = sb("bors", [128, KC], F32)
    b["bos2"] = sb("bos2", [128, KC], F32)
    b["w1r"], b["w2r"] = w1r, w2r
    b["keyr"], b["valr"] = keyr, valr
    b["ag_in_a"], b["ag_out_a"] = ag_in_a.ap(), ag_out_a.ap()
    b["ag_in_b"], b["ag_out_b"] = ag_in_b.ap(), ag_out_b.ap()
    b["rs_in"], b["rs_out"] = rs_in_t.ap(), rs_out_t.ap()
    b["ag_in_ta"], b["ag_out_ta"] = ag_in_a, ag_out_a
    b["ag_in_tb"], b["ag_out_tb"] = ag_in_b, ag_out_b
    b["rs_in_t"], b["rs_out_t"] = rs_in_t, rs_out_t
    b["wor0"], b["wor1"], b["bqr"] = wor0, wor1, bqr
    for p in range(4):
        b[f"wqp0_{p}"] = wqp[(0, p)]
        b[f"wqp1_{p}"] = wqp[(1, p)]
    ctx.arena = stack.enter_context(nc.psum_tensor("arena", [128, 4096], F32))

    semnames = (["x", "pre", "preh", "gch", "vch",
                 "ln_sq", "ln_sum", "ln_var", "ln_std", "ln_rs", "ln_xln", "ln_tp", "ln_xlnT",
                 "z", "g", "cw", "cwt", "cwts", "h", "gel", "cwm", "selm", "cwbc", "fb", "ye",
                 "ago_0", "ago_1", "cc", "wq_0", "wq_1", "xr_0", "qk", "rot", "qt", "qts",
                 "S", "exp", "msk", "pv", "nrm", "bc", "on", "ore_0", "ore_1",
                 "kd", "pj", "yp", "yb", "yr_0", "yr_1", "ya", "yv", "wq_w", "rbc", "xr_1", "out"]
                + [f"w1_{i}" for i in range(4)] + [f"w2_{i}" for i in range(4)])
    for n in semnames:
        ctx.mksem(n, stack)
    S = ctx.sems
    C = ctx.cnt
    for n in semnames:
        C[n] = 0

    block = stack.enter_context(nc.Block())

    @block.sync
    def _(sy):
        sy.dma_start(out=b["Xs"][:, :].rearrange("p (nt d) -> p nt d", d=D),
                     in_=x_in.rearrange("(nt p) d -> p nt d", p=128)).then_inc(S["x"], 16)
        sy.dma_start(out=b["b1s"][:, :], in_=b1r).then_inc(S["preh"], 16)
        sy.dma_start(out=b["bors"][:, :], in_=bor).then_inc(S["preh"], 16)

    @block.gpsimd
    def _(g):
        g.dma_start(out=b["b2s"][:, :], in_=b2r).then_inc(S["pre"], 16)
        g.dma_start(out=b["wgs"][:, :], in_=wgr).then_inc(S["pre"], 16)
        g.dma_start(out=b["esels"][:, :], in_=eselr).then_inc(S["pre"], 16)
        g.dma_start(out=b["crep"][:, :], in_=ctabr).then_inc(S["pre"], 16)
        g.dma_start(out=b["srep"][:, :], in_=stabr).then_inc(S["pre"], 16)
        ctx.gstep(g, g.memset(b["epst"][:, :], EPS))
        ctx.gstep(g, g.memset(b["ident"][:, :], 0.0))
        ctx.gstep(g, g.affine_select(out=b["ident"][:, :], in_=b["ident"][:, :],
                                     compare_op=ALU.not_equal, fill=1.0, base=0,
                                     pattern=[[-1, 128]], channel_multiplier=1))
        ctx.gstep(g, g.memset(b["onesc"][:, :], 1.0))
        ctx.gstep(g, g.memset(b["sel"][:, :], 0.0))
        ctx.gstep(g, g.affine_select(out=b["sel"][:, :], in_=b["sel"][:, :],
                                     compare_op=ALU.not_equal, fill=1.0, base=0,
                                     pattern=[[-1, 16], [0, 128]], channel_multiplier=1))
    NCONST = ctx.cnt["gch"]

    @block.vector
    def _(v):
        v.wait_ge(S["preh"], 32)
        ctx.vstep(v, v.tensor_scalar_mul(b["bos2"][:, :], b["bors"][:, :], 0.5))

    def xs_ready_initial(eng):
        eng.wait_ge(S["x"], 16)
        eng.wait_ge(S["pre"], 80)
        eng.wait_ge(S["preh"], 32)
        eng.wait_ge(S["gch"], NCONST)

    def mk_xs_ready(semname):
        val = C[semname]
        def f(eng):
            eng.wait_ge(S[semname], val)
        return f

    def mk_noop():
        def f(eng):
            pass
        return f

    ntt_ = tok // 128

    def mk_tile_ffn():
        base = C["ye"]
        def f(eng, nt):
            eng.wait_ge(S["ye"], base - ((ntt_ // 4 - 1 - nt // 4) * 4 + nt % 4))
        return f

    def mk_tile_attn():
        base = C["yv"]
        def f(eng, nt):
            eng.wait_ge(S["yv"], base - (ntt_ - nt - 1))
        return f

    def run(ph):
        for en in ("sync", "gpsimd", "scalar", "vector", "tensor"):
            if en in ph:
                getattr(block, en)(ph[en])

    run(emit_ln_phase(ctx, xs_ready_initial))
    run(emit_ffn0(ctx, 0))
    run(emit_ln_phase(ctx, mk_noop(), xs_tile=mk_tile_ffn()))
    run(emit_attn(ctx, 0, groups, guard=[("ye", C["ye"])]))
    run(emit_ln_phase(ctx, mk_noop(), xs_tile=mk_tile_attn()))
    run(emit_ffn0(ctx, 1, guard=[("ya", C["ya"]), ("yb", C["yb"])]))
    run(emit_ln_phase(ctx, mk_noop(), xs_tile=mk_tile_ffn()))
    run(emit_ffn1(ctx, guard=[("ye", C["ye"])]))
    run(emit_ln_phase(ctx, mk_noop(), xs_tile=mk_tile_ffn()))
    run(emit_attn(ctx, 1, groups, guard=[("ye", C["ye"])]))
    run(emit_ln_phase(ctx, mk_noop(), xs_tile=mk_tile_attn()))
    run(emit_ffn1(ctx, guard=[("ya", C["ya"]), ("yb", C["yb"])]))

    final_ye = C["ye"]

    @block.sync
    def _(sy):
        sy.wait_ge(S["ye"], final_ye)
        sy.dma_start(out=x_out.rearrange("(nt p) d -> p nt d", p=128),
                     in_=b["Xs"][:, :].rearrange("p (nt d) -> p nt d", d=D)).then_inc(S["out"], 16)
        sy.wait_ge(S["out"], 16)

    stack.close()
    return nc


def prep_fused_core(inputs, c, tok=TOK):
    hbase = 8 * (c % 2)
    heads = list(range(hbase, hbase + 8))
    m = {}
    m.update(prep_ffn0(inputs["L0_ffn_W1"], inputs["L0_ffn_W2"], inputs["L0_ffn_b1"],
                       inputs["L0_ffn_b2"], inputs["L0_ffn_w_gate"]))
    f1 = prep_ffn1(inputs["L1_ffn_expert_sel"], inputs["L1_ffn_keys"], inputs["L1_ffn_values"])
    m["keyr"], m["valr"], m["eselr"] = f1["keyr"], f1["valr"], f1["eselr"]
    crep, srep = _rope_tabs(2 * tok, 2)
    m["ctabr"], m["stabr"] = crep, srep
    w_qkv = inputs["L0_attn_w_qkv"]
    perm = _eo_perm()
    for p in range(4):
        h0, h1 = heads[2 * p], heads[2 * p + 1]
        cols = []
        for part in (0, 1):
            for h in (h0, h1):
                cols.append(w_qkv[:, part * D + h * 64: part * D + (h + 1) * 64][:, perm])
        for h in (h0, h1):
            cols.append(w_qkv[:, 2 * D + h * 64: 2 * D + (h + 1) * 64])
        wsl = np.concatenate(cols, axis=1)
        m[f"wqp0_{p}"] = np.ascontiguousarray(
            wsl.reshape(KC, 128, 384).transpose(1, 0, 2).reshape(128, KC * 384))
    Wq, bq = inputs["L1_attn_Wq"], inputs["L1_attn_bq"]
    Wk, bk = inputs["L1_attn_Wk"], inputs["L1_attn_bk"]
    Wv, bv = inputs["L1_attn_Wv"], inputs["L1_attn_bv"]
    bqarr = np.zeros((1, 1024), np.float32)
    for p in range(4):
        h0, h1 = heads[2 * p], heads[2 * p + 1]
        cols = [Wq[:, h0 * 64:(h0 + 1) * 64], Wq[:, h1 * 64:(h1 + 1) * 64]]
        bias = [bq[h0 * 64:(h0 + 1) * 64], bq[h1 * 64:(h1 + 1) * 64]]
        if p == 0:
            cols += [Wk, Wv]
            bias += [bk, bv]
        wsl = np.concatenate(cols, axis=1)
        pwl = wsl.shape[1]
        m[f"wqp1_{p}"] = np.ascontiguousarray(
            wsl.reshape(KC, 128, pwl).transpose(1, 0, 2).reshape(128, KC * pwl))
        bqarr[0, p * 256: p * 256 + pwl] = np.concatenate(bias)
    m["bqr"] = bqarr
    for li, wo in ((0, inputs["L0_attn_w_out"]), (1, inputs["L1_attn_Wo"])):
        sl = wo[hbase * 64:(hbase + 8) * 64]
        m[f"wor{li}"] = np.ascontiguousarray(
            sl.reshape(4, 128, D).transpose(1, 0, 2).reshape(128, 4 * D))
    m["bor"] = np.ascontiguousarray(
        inputs["L1_attn_bo"].reshape(KC, 128).T.astype(np.float32))
    return m


def kernel_fused(**inputs):
    inputs = {k: np.asarray(v) for k, v in inputs.items()}
    x = np.ascontiguousarray(inputs["L0_attn_x"].astype(np.float32).reshape(16384, D))
    nc = build_fused()
    in_maps = []
    for c in range(8):
        m = prep_fused_core(inputs, c)
        m["x_in"] = np.ascontiguousarray(x[c * TOK:(c + 1) * TOK])
        in_maps.append(m)
    res = run_bass_kernel_spmd(nc, in_maps, core_ids=list(range(8)))
    out = np.concatenate([res.results[c]["x_out"] for c in range(8)], axis=0)
    return np.ascontiguousarray(out.reshape(4, 4096, D))


# version 23
# speedup vs baseline: 1.0077x; 1.0077x over previous
import numpy as np
from contextlib import ExitStack

import concourse.bass as bass
import concourse.mybir as mybir
from concourse.bass_utils import run_bass_kernel_spmd

F32 = mybir.dt.float32
BF16 = mybir.dt.bfloat16
AF = mybir.ActivationFunctionType
ALU = mybir.AluOpType
AX = mybir.AxisListType

TOK = 2048
D = 1024
KC = D // 128
NE = 8
NJ = 32
EPS = 1e-5


def _ident(g, ident):
    g.memset(ident[:, :], 0.0)
    return g.affine_select(out=ident[:, :], in_=ident[:, :],
                           compare_op=ALU.not_equal, fill=1.0, base=0,
                           pattern=[[-1, ident.shape[1]]], channel_multiplier=1)


def _sel1(g, sel, ne):
    g.memset(sel[:, :], 0.0)
    return g.affine_select(out=sel[:, :], in_=sel[:, :],
                           compare_op=ALU.not_equal, fill=1.0, base=0,
                           pattern=[[-1, ne], [0, 128]], channel_multiplier=1)


def build_ffn0(tok=TOK, ne=NE, nj=NJ):
    NT = tok // 128
    NST = max(tok // 512, 1)
    STW = tok // NST
    NTT = STW // 128
    NJH = 2
    JH = nj // NJH
    NRND = ne * NJH
    GT = min(4, JH)
    NG = JH // GT
    RING = max(NG, 2)

    nc = bass.Bass("TRN2", target_bir_lowering=False, debug=False)
    x_in = nc.dram_tensor("x_in", [tok, D], F32, kind="ExternalInput").ap()
    w1r = nc.dram_tensor("w1r", [ne, nj // GT, 128, GT * D], F32, kind="ExternalInput").ap()
    w2r = nc.dram_tensor("w2r", [ne, nj // GT, 128, GT * D], F32, kind="ExternalInput").ap()
    b1r = nc.dram_tensor("b1r", [128, ne * nj], F32, kind="ExternalInput").ap()
    b2r = nc.dram_tensor("b2r", [ne, D], F32, kind="ExternalInput").ap()
    wgr = nc.dram_tensor("wgr", [128, KC * ne], F32, kind="ExternalInput").ap()
    x_out = nc.dram_tensor("x_out", [tok, D], F32, kind="ExternalOutput").ap()

    ctx = ExitStack()
    sb = lambda name, shape, dt: ctx.enter_context(nc.sbuf_tensor(name, shape, dt))
    sem = lambda name: ctx.enter_context(nc.semaphore(name))

    Xs = sb("Xs", [128, NT * D], F32)
    XlnT = sb("XlnT", [128, KC * tok], BF16)
    Hss = sb("Hss", [128, JH * STW], BF16)
    w1ring = sb("w1ring", [128, RING * GT * D], BF16)
    w2ring = sb("w2ring", [128, RING * GT * D], BF16)
    Hgel = sb("Hgel", [128, max(2 * STW, D)], BF16)
    cwbs = sb("cwbs", [128, 2 * STW], BF16)
    b1s = sb("b1s", [128, ne * nj], F32)
    b2s = sb("b2s", [ne, D], BF16)
    wgs = sb("wgs", [128, KC * ne], BF16)
    cwTs = sb("cwTs", [ne, tok], BF16)
    sel = sb("sel", [ne, ne * 128], BF16)
    ident = sb("ident", [128, 128], BF16)
    xlnbf = sb("xlnbf", [128, D], BF16)
    sums = sb("sums", [128, NT], F32)
    sqs = sb("sqs", [128, NT], F32)
    means = sb("means", [128, NT], F32)
    m2s = sb("m2s", [128, NT], F32)
    vars_ = sb("vars_", [128, NT], F32)
    stds = sb("stds", [128, NT], F32)
    rss = sb("rss", [128, NT], F32)
    negmrs = sb("negmrs", [128, NT], F32)
    gsm = sb("gsm", [128, NT * ne], F32)
    gs1 = sb("gs1", [128, ne], F32)
    gs2 = sb("gs2", [128, ne], F32)
    gs3 = sb("gs3", [128, ne], F32)
    r1 = sb("r1", [128, 1], F32)
    r2 = sb("r2", [128, 1], F32)
    epst = sb("epst", [128, 1], F32)
    cwb_all = sb("cwb_all", [128, NT * ne], BF16)

    arena = ctx.enter_context(nc.psum_tensor("arena", [128, 4096], F32))

    s_x = sem("s_x")
    s_pre = sem("s_pre")
    s_preh = sem("s_preh")
    s_sq = sem("s_sq")
    s_sum = sem("s_sum")
    s_var = sem("s_var")
    s_std = sem("s_std")
    s_rs = sem("s_rs")
    s_xln = sem("s_xln")
    s_tp = sem("s_tp")
    s_xlnT = sem("s_xlnT")
    s_z = sem("s_z")
    s_g = sem("s_g")
    s_cw = sem("s_cw")
    s_cwt = sem("s_cwt")
    s_cwts = sem("s_cwts")
    s_w1s = [sem(f"s_w1_{i}") for i in range(RING)]
    s_w2s = [sem(f"s_w2_{i}") for i in range(RING)]
    s_h = sem("s_h")
    s_gel = sem("s_gel")
    s_cwm = sem("s_cwm")
    s_selm = sem("s_selm")
    s_cwbc = sem("s_cwbc")
    s_b = sem("s_b")
    s_ye = sem("s_ye")
    s_out = sem("s_out")
    s_gch = sem("s_gch")
    s_vch = sem("s_vch")

    block = ctx.enter_context(nc.Block())

    def hcount(rnd, st, j):
        return (rnd * NST + st) * JH + j + 1

    def evacs_before(rnd, st):
        return (rnd * NST + st) * NTT

    def hbank(j):
        return arena[:, 3072 + (j % 2) * 512: 3072 + (j % 2) * 512 + STW]

    @block.sync
    def _(sy):
        sy.dma_start(out=Xs[:, :].rearrange("p (nt d) -> p nt d", d=D),
                     in_=x_in.rearrange("(nt p) d -> p nt d", p=128)).then_inc(s_x, 16)
        sy.dma_start(out=b1s[:, :], in_=b1r[:, :]).then_inc(s_preh, 16)

    @block.gpsimd
    def _(g):
        g.dma_start(out=b2s[:, :], in_=b2r[:, :]).then_inc(s_pre, 16)
        g.dma_start(out=wgs[:, :], in_=wgr[:, :]).then_inc(s_pre, 16)
        g.memset(epst[:, :], EPS).then_inc(s_gch, 1)
        g.wait_ge(s_gch, 1)
        g.memset(ident[:, :], 0.0).then_inc(s_gch, 1)
        g.wait_ge(s_gch, 2)
        g.affine_select(out=ident[:, :], in_=ident[:, :],
                        compare_op=ALU.not_equal, fill=1.0, base=0,
                        pattern=[[-1, 128]], channel_multiplier=1).then_inc(s_gch, 1)
        g.wait_ge(s_gch, 3)
        g.memset(sel[:, :], 0.0).then_inc(s_gch, 1)
        g.wait_ge(s_gch, 4)
        g.affine_select(out=sel[:, :], in_=sel[:, :],
                        compare_op=ALU.not_equal, fill=1.0, base=0,
                        pattern=[[-1, ne], [0, 128]], channel_multiplier=1).then_inc(s_gch, 1)
        for rnd in range(NRND):
            e, jh = divmod(rnd, NJH)
            pairs = [(q, which) for q in range(NG) for which in (0, 1)]
            if rnd == 0:
                pairs = [(q, 0) for q in range(NG)] + [(q, 1) for q in range(NG)]
            for (q, which) in pairs:
                if True:
                    gidx = rnd * NG + q
                    slot = gidx % RING
                    if gidx >= RING:
                        prnd, pq = divmod(gidx - RING, NG)
                        rel = hcount(prnd, NST - 1, pq * GT + (GT - 1))
                        g.wait_ge(s_h if which == 0 else s_b, rel)
                    gg = (jh * JH + q * GT) // GT
                    if which == 0:
                        g.dma_start(
                            out=w1ring[:, slot * GT * D:(slot + 1) * GT * D],
                            in_=w1r[e, gg],
                        ).then_inc(s_w1s[slot], 16)
                    else:
                        g.dma_start(
                            out=w2ring[:, slot * GT * D:(slot + 1) * GT * D],
                            in_=w2r[e, gg],
                        ).then_inc(s_w2s[slot], 16)

    @block.scalar
    def _(s):
        s.wait_ge(s_x, 16)
        for nt in range(NT):
            if nt > 0:
                s.wait_ge(s_sq, nt)
            s.activation(out=Hgel[:, 0:D],
                         in_=Xs[:, nt * D:(nt + 1) * D], func=AF.Square,
                         accum_out=sqs[:, nt:nt + 1]).then_inc(s_sq, 1)
        s.wait_ge(s_var, 1)
        s.wait_ge(s_pre, 32)
        s.wait_ge(s_preh, 16)
        s.wait_ge(s_gch, 5)
        s.activation(out=stds[:, :], in_=vars_[:, :], func=AF.Sqrt,
                     bias=epst[:, 0:1]).then_inc(s_std, 1)
        s.wait_ge(s_z, NT)
        s.activation(out=gsm[:, :], in_=arena[:, 1024:1024 + NT * ne],
                     func=AF.Exp).then_inc(s_g, 1)
        for rnd in range(NRND):
            e, jh = divmod(rnd, NJH)
            for st in range(NST):
                for j in range(JH):
                    idx = hcount(rnd, st, j)
                    s.wait_ge(s_h, idx)
                    if idx > 2:
                        s.wait_ge(s_cwm, idx - 2)
                    s.activation(out=Hgel[:, (j % 2) * STW:(j % 2) * STW + STW],
                                 in_=hbank(j), func=AF.Gelu,
                                 bias=b1s[:, e * nj + jh * JH + j: e * nj + jh * JH + j + 1],
                                 ).then_inc(s_gel, 1)

    @block.vector
    def _(v):
        v.wait_ge(s_x, 16)
        for nt in range(NT):
            v.reduce_sum(sums[:, nt:nt + 1], Xs[:, nt * D:(nt + 1) * D],
                         axis=AX.X).then_inc(s_sum, 1)
        v.wait_ge(s_sq, NT)
        v.wait_ge(s_sum, NT)
        v.tensor_scalar_mul(means[:, :], sums[:, :], 1.0 / D).then_inc(s_vch, 1)
        v.wait_ge(s_vch, 1)
        v.tensor_mul(m2s[:, :], means[:, :], means[:, :]).then_inc(s_vch, 1)
        v.wait_ge(s_vch, 2)
        v.scalar_tensor_tensor(vars_[:, :], sqs[:, :], 1.0 / D, m2s[:, :],
                               ALU.mult, ALU.subtract).then_inc(s_var, 1)
        v.wait_ge(s_std, 1)
        v.reciprocal(rss[:, :], stds[:, :]).then_inc(s_vch, 1)
        v.wait_ge(s_vch, 3)
        v.scalar_tensor_tensor(negmrs[:, :], means[:, :], -1.0, rss[:, :],
                               ALU.mult, ALU.mult).then_inc(s_rs, 1)
        v.wait_ge(s_rs, 1)
        for nt in range(NT):
            if nt > 0:
                v.wait_ge(s_tp, nt)
            v.tensor_scalar(xlnbf[:, :], Xs[:, nt * D:(nt + 1) * D],
                            rss[:, nt:nt + 1], negmrs[:, nt:nt + 1],
                            ALU.mult, ALU.add).then_inc(s_xln, 1)
            v.wait_ge(s_tp, nt + 1)
            bk = arena[:, (nt % 2) * 512:(nt % 2) * 512 + 512].bitcast(BF16)
            v.tensor_copy(
                XlnT[:, :].rearrange("p (kc t) -> p kc t", t=tok)[:, :, nt * 128:(nt + 1) * 128],
                bk[:, :].rearrange("p (kc t) -> p kc t", t=128),
            ).then_inc(s_xlnT, 1)
        v.wait_ge(s_g, 1)
        vchn = [3]
        def vstep(ins):
            vchn[0] += 1
            ins.then_inc(s_vch, 1)
            v.wait_ge(s_vch, vchn[0])
        for nt in range(NT):
            gt = gsm[:, nt * ne:(nt + 1) * ne]
            vstep(v.reduce_sum(r1[:, :], gt, axis=AX.X))
            vstep(v.reciprocal(r2[:, :], r1[:, :]))
            vstep(v.tensor_scalar(gs1[:, :], gt, r2[:, 0:1], None, ALU.mult))
            vstep(v.reduce_max(r1[:, :], gs1[:, :], axis=AX.X))
            vstep(v.tensor_scalar(gs2[:, :], gs1[:, :], r1[:, 0:1], None, ALU.is_ge))
            vstep(v.scalar_tensor_tensor(gs3[:, :], gs2[:, :], -1e30, gs1[:, :],
                                         ALU.mult, ALU.add))
            vstep(v.reduce_max(r1[:, :], gs3[:, :], axis=AX.X))
            vstep(v.tensor_scalar(gs2[:, :], gs1[:, :], r1[:, 0:1], None, ALU.is_ge))
            vstep(v.scalar_tensor_tensor(gs3[:, :], gs2[:, :], 0.5, gs1[:, :],
                                         ALU.mult, ALU.mult))
            v.tensor_copy(cwb_all[:, nt * ne:(nt + 1) * ne], gs3[:, :]).then_inc(s_cw, 1)
            v.wait_ge(s_cw, nt + 1)
        v.wait_ge(s_cwt, NT)
        v.tensor_copy(cwTs[:, :], arena[:, 1536:1536 + tok // 2].bitcast(BF16)[0:ne, :]).then_inc(s_cwts, 1)
        for rnd in range(NRND):
            for st in range(NST):
                v.wait_ge(s_selm, rnd * NST + st + 1)
                v.tensor_copy(cwbs[:, (st % 2) * STW:(st % 2) * STW + STW],
                              arena[:, 0:STW]).then_inc(s_cwbc, 1)
                v.wait_ge(s_cwbc, rnd * NST + st + 1)
                for j in range(JH):
                    idx = hcount(rnd, st, j)
                    v.wait_ge(s_gel, idx)
                    if rnd * NST + st > 0:
                        v.wait_ge(s_b, hcount(rnd, st - 1, j) if st > 0
                                  else hcount(rnd - 1, NST - 1, j))
                    v.tensor_tensor(Hss[:, j * STW:(j + 1) * STW],
                                    Hgel[:, (j % 2) * STW:(j % 2) * STW + STW],
                                    cwbs[:, (st % 2) * STW:(st % 2) * STW + STW],
                                    ALU.mult).then_inc(s_cwm, 1)
                v.wait_ge(s_b, hcount(rnd, st, JH - 1))
                v.wait_ge(s_ye, evacs_before(rnd, st))
                for t in range(NTT):
                    tk = st * NTT + t
                    v.tensor_tensor(Xs[:, tk * D:(tk + 1) * D],
                                    arena[:, t * 1024:(t + 1) * 1024],
                                    Xs[:, tk * D:(tk + 1) * D],
                                    ALU.add).then_inc(s_ye, 1)

    @block.tensor
    def _(t):
        t.wait_ge(s_pre, 32)
        t.wait_ge(s_preh, 16)
        t.wait_ge(s_gch, 5)
        for nt in range(NT):
            t.wait_ge(s_xln, nt + 1)
            if nt >= 2:
                t.wait_ge(s_xlnT, nt - 1)
            bk = arena[:, (nt % 2) * 512:(nt % 2) * 512 + 512].bitcast(BF16)
            for kc in range(KC):
                ins = t.transpose(bk[:, kc * 128:(kc + 1) * 128],
                                  xlnbf[:, kc * 128:(kc + 1) * 128], ident[:, :])
            ins.then_inc(s_tp, 1)
            t.wait_ge(s_xlnT, nt + 1)
            for kc in range(KC):
                ins = t.matmul(arena[:, 1024 + nt * ne:1024 + (nt + 1) * ne],
                               XlnT[:, kc * tok + nt * 128: kc * tok + (nt + 1) * 128],
                               wgs[:, kc * ne:(kc + 1) * ne],
                               start=(kc == 0), stop=(kc == KC - 1))
            ins.then_inc(s_z, 1)
        cwtb = arena[:, 1536:1536 + tok // 2].bitcast(BF16)
        for nt in range(NT):
            t.wait_ge(s_cw, nt + 1)
            ins = t.transpose(cwtb[0:ne, nt * 128:(nt + 1) * 128],
                              cwb_all[:, nt * ne:(nt + 1) * ne], ident[:, :])
            ins.then_inc(s_cwt, 1)
        t.wait_ge(s_xlnT, NT)
        t.wait_ge(s_g, 1)
        t.wait_ge(s_cwts, 1)
        for rnd in range(NRND):
            e, jh = divmod(rnd, NJH)
            for st in range(NST):
                gidx_base = rnd * NG
                t.wait_ge(s_ye, evacs_before(rnd, st))
                t.wait_ge(s_cwbc, rnd * NST + st)
                ins = t.matmul(arena[:, 0:STW], sel[:, e * 128:(e + 1) * 128],
                               cwTs[:, st * STW:(st + 1) * STW], start=True, stop=True)
                ins.then_inc(s_selm, 1)
                for j in range(JH):
                    q = j // GT
                    gidx = gidx_base + q
                    t.wait_ge(s_w1s[gidx % RING], 16 * (gidx // RING + 1))
                    idx = hcount(rnd, st, j)
                    if idx > 2:
                        t.wait_ge(s_gel, idx - 2)
                    slot = (gidx_base + q) % RING
                    w1t = w1ring[:, slot * GT * D + (j % GT) * D: slot * GT * D + (j % GT + 1) * D]
                    for kc in range(KC):
                        ins = t.matmul(hbank(j), w1t[:, kc * 128:(kc + 1) * 128],
                                       XlnT[:, kc * tok + st * STW: kc * tok + (st + 1) * STW],
                                       start=(kc == 0), stop=(kc == KC - 1))
                    ins.then_inc(s_h, 1)
                for j in range(JH):
                    q = j // GT
                    gidx = gidx_base + q
                    t.wait_ge(s_w2s[gidx % RING], 16 * (gidx // RING + 1))
                    t.wait_ge(s_cwm, hcount(rnd, st, j))
                    if j == 0 and NTT == 4:
                        t.wait_ge(s_gel, hcount(rnd, st, JH - 1))
                    slot = (gidx_base + q) % RING
                    w2t = w2ring[:, slot * GT * D + (j % GT) * D: slot * GT * D + (j % GT + 1) * D]
                    first = (j == 0)
                    last = (j == JH - 1)
                    for tt in range(NTT):
                        if first and rnd == 0:
                            t.matmul(arena[:, tt * 1024: tt * 1024 + 512],
                                     cwTs[:, (st * NTT + tt) * 128:(st * NTT + tt + 1) * 128],
                                     b2s[:, 0:512], start=True, stop=False)
                            t.matmul(arena[:, tt * 1024 + 512:(tt + 1) * 1024],
                                     cwTs[:, (st * NTT + tt) * 128:(st * NTT + tt + 1) * 128],
                                     b2s[:, 512:1024], start=True, stop=False)
                        hst = Hss[:, j * STW + tt * 128: j * STW + (tt + 1) * 128]
                        st0 = first and rnd > 0
                        t.matmul(arena[:, tt * 1024: tt * 1024 + 512], hst,
                                 w2t[:, 0:512], start=st0, stop=last)
                        ins = t.matmul(arena[:, tt * 1024 + 512:(tt + 1) * 1024], hst,
                                       w2t[:, 512:1024], start=st0, stop=last)
                    ins.then_inc(s_b, 1)

    @block.sync
    def _(sy):
        sy.wait_ge(s_ye, NRND * NST * NTT)
        sy.dma_start(out=x_out.rearrange("(nt p) d -> p nt d", p=128),
                     in_=Xs[:, :].rearrange("p (nt d) -> p nt d", d=D)).then_inc(s_out, 16)
        sy.wait_ge(s_out, 16)

    ctx.close()
    return nc



def prep_ffn0(W1, W2, b1, b2, wg):
    ne, d, hid = W1.shape
    nj = hid // 128
    kc = d // 128
    gt = min(4, nj // 2)
    w1r = np.ascontiguousarray(
        W1.reshape(ne, kc, 128, nj // gt, gt, 128)
          .transpose(0, 3, 2, 4, 1, 5).reshape(ne, nj // gt, 128, gt * kc * 128))
    w2r = np.ascontiguousarray(
        W2.reshape(ne, nj // gt, gt, 128, d)
          .transpose(0, 1, 3, 2, 4).reshape(ne, nj // gt, 128, gt * d))
    b1r = np.ascontiguousarray(b1.reshape(ne, nj, 128).transpose(2, 0, 1).reshape(128, ne * nj))
    b2r = np.ascontiguousarray(b2)
    wgr = np.ascontiguousarray(wg.T.reshape(d // 128, 128, ne).transpose(1, 0, 2).reshape(128, (d // 128) * ne))
    return {"w1r": w1r, "w2r": w2r, "b1r": b1r, "b2r": b2r, "wgr": wgr}


def run_ffn0(x_flat, inputs, nc_cache={}):
    if "ffn0" not in nc_cache:
        nc_cache["ffn0"] = build_ffn0()
    nc = nc_cache["ffn0"]
    wmap = prep_ffn0(inputs["L0_ffn_W1"], inputs["L0_ffn_W2"], inputs["L0_ffn_b1"],
                     inputs["L0_ffn_b2"], inputs["L0_ffn_w_gate"])
    in_maps = []
    for c in range(8):
        m = {"x_in": np.ascontiguousarray(x_flat[c * TOK:(c + 1) * TOK])}
        m.update(wmap)
        in_maps.append(m)
    res = run_bass_kernel_spmd(nc, in_maps, core_ids=list(range(8)))
    return np.concatenate([res.results[c]["x_out"] for c in range(8)], axis=0)


def kernel_unfused(**inputs):
    inputs = {k: np.asarray(v) for k, v in inputs.items()}
    x = np.ascontiguousarray(inputs["L0_attn_x"].astype(np.float32).reshape(16384, D))
    x = run_ffn0(x, inputs)
    x = run_attn_layer(x, inputs, 0)
    x = run_ffn0(x, inputs)
    x = run_ffn1(x, inputs)
    x = run_attn_layer(x, inputs, 1)
    x = run_ffn1(x, inputs)
    return np.ascontiguousarray(x.reshape(4, 4096, D))


def kernel(**inputs):
    return kernel_fused(**inputs)


def build_ffn1(tok=TOK, ne=16, hh=128):
    NT = tok // 128
    NST = max(tok // 512, 1)
    STW = tok // NST
    NTT = STW // 128

    nc = bass.Bass("TRN2", target_bir_lowering=False, debug=False)
    x_in = nc.dram_tensor("x_in", [tok, D], F32, kind="ExternalInput").ap()
    keyr = nc.dram_tensor("keyr", [ne, 128, KC * hh], F32, kind="ExternalInput").ap()
    valr = nc.dram_tensor("valr", [ne, 128, D], F32, kind="ExternalInput").ap()
    eselr = nc.dram_tensor("eselr", [128, KC * ne], F32, kind="ExternalInput").ap()
    x_out = nc.dram_tensor("x_out", [tok, D], F32, kind="ExternalOutput").ap()

    ctx = ExitStack()
    sb = lambda name, shape, dt: ctx.enter_context(nc.sbuf_tensor(name, shape, dt))
    sem = lambda name: ctx.enter_context(nc.semaphore(name))

    Xs = sb("Xs", [128, NT * D], F32)
    XlnT = sb("XlnT", [128, KC * tok], BF16)
    keys = sb("keys", [128, ne * KC * hh], BF16)
    vals = sb("vals", [128, ne * D], BF16)
    esels = sb("esels", [128, KC * ne], BF16)
    Ssall = sb("Ssall", [128, ne * STW], BF16)
    Sg = sb("Sg", [128, 2 * STW], BF16)
    selbs = sb("selbs", [128, 2 * STW], BF16)
    selTs = sb("selTs", [ne, tok], BF16)
    sel16 = sb("sel16", [ne, ne * 128], BF16)
    ident = sb("ident", [128, 128], BF16)
    xlnbf = sb("xlnbf", [128, D], BF16)
    sums = sb("sums", [128, NT], F32)
    sqs = sb("sqs", [128, NT], F32)
    means = sb("means", [128, NT], F32)
    m2s = sb("m2s", [128, NT], F32)
    vars_ = sb("vars_", [128, NT], F32)
    stds = sb("stds", [128, NT], F32)
    rss = sb("rss", [128, NT], F32)
    negmrs = sb("negmrs", [128, NT], F32)
    sgm = sb("sgm", [128, NT * ne], F32)
    sgmb = sb("sgmb", [128, NT * ne], BF16)
    epst = sb("epst", [128, 1], F32)

    arena = ctx.enter_context(nc.psum_tensor("arena", [128, 4096], F32))

    s_x = sem("s_x")
    s_pre = sem("s_pre")
    s_gch = sem("s_gch")
    s_vch = sem("s_vch")
    s_sq = sem("s_sq")
    s_sum = sem("s_sum")
    s_var = sem("s_var")
    s_std = sem("s_std")
    s_rs = sem("s_rs")
    s_xln = sem("s_xln")
    s_tp = sem("s_tp")
    s_xlnT = sem("s_xlnT")
    s_z = sem("s_z")
    s_g = sem("s_g")
    s_sgb = sem("s_sgb")
    s_st = sem("s_st")
    s_sts = sem("s_sts")
    s_selm = sem("s_selm")
    s_sbc = sem("s_sbc")
    s_s = sem("s_s")
    s_rel = sem("s_rel")
    s_ssm = sem("s_ssm")
    s_b = sem("s_b")
    s_ye = sem("s_ye")
    s_out = sem("s_out")

    block = ctx.enter_context(nc.Block())

    def secount(st, e):
        return st * ne + e + 1

    @block.sync
    def _(sy):
        sy.dma_start(out=Xs[:, :].rearrange("p (nt d) -> p nt d", d=D),
                     in_=x_in.rearrange("(nt p) d -> p nt d", p=128)).then_inc(s_x, 16)

    @block.gpsimd
    def _(g):
        g.dma_start(out=keys[:, :].rearrange("p (e m) -> p e m", m=KC * hh),
                    in_=keyr.rearrange("e p m -> p e m")).then_inc(s_pre, 16)
        g.dma_start(out=vals[:, :].rearrange("p (e d) -> p e d", d=D),
                    in_=valr.rearrange("e p d -> p e d")).then_inc(s_pre, 16)
        g.dma_start(out=esels[:, :], in_=eselr[:, :]).then_inc(s_pre, 16)
        g.memset(epst[:, :], EPS).then_inc(s_gch, 1)
        g.wait_ge(s_gch, 1)
        g.memset(ident[:, :], 0.0).then_inc(s_gch, 1)
        g.wait_ge(s_gch, 2)
        g.affine_select(out=ident[:, :], in_=ident[:, :],
                        compare_op=ALU.not_equal, fill=1.0, base=0,
                        pattern=[[-1, 128]], channel_multiplier=1).then_inc(s_gch, 1)
        g.wait_ge(s_gch, 3)
        g.memset(sel16[:, :], 0.0).then_inc(s_gch, 1)
        g.wait_ge(s_gch, 4)
        g.affine_select(out=sel16[:, :], in_=sel16[:, :],
                        compare_op=ALU.not_equal, fill=0.5, base=0,
                        pattern=[[-1, ne], [0, 128]], channel_multiplier=1).then_inc(s_gch, 1)

    @block.scalar
    def _(s):
        s.wait_ge(s_x, 16)
        for nt in range(NT):
            if nt > 0:
                s.wait_ge(s_sq, nt)
            s.activation(out=Sg[:, 0:D],
                         in_=Xs[:, nt * D:(nt + 1) * D], func=AF.Square,
                         accum_out=sqs[:, nt:nt + 1]).then_inc(s_sq, 1)
        s.wait_ge(s_var, 1)
        s.wait_ge(s_gch, 5)
        s.activation(out=stds[:, :], in_=vars_[:, :], func=AF.Sqrt,
                     bias=epst[:, 0:1]).then_inc(s_std, 1)
        s.wait_ge(s_z, NT)
        s.activation(out=sgm[:, :], in_=arena[:, 1024:1024 + NT * ne],
                     func=AF.Sigmoid).then_inc(s_g, 1)
        for st in range(NST):
            for e in range(ne):
                idx = secount(st, e)
                s.wait_ge(s_s, idx)
                if idx > 2:
                    s.wait_ge(s_ssm, idx - 2)
                sbk = arena[:, 3072 + (e % 2) * 512: 3072 + (e % 2) * 512 + STW]
                s.activation(out=Sg[:, (e % 2) * STW:(e % 2) * STW + STW],
                             in_=sbk, func=AF.Relu).then_inc(s_rel, 1)

    @block.vector
    def _(v):
        v.wait_ge(s_x, 16)
        for nt in range(NT):
            v.reduce_sum(sums[:, nt:nt + 1], Xs[:, nt * D:(nt + 1) * D],
                         axis=AX.X).then_inc(s_sum, 1)
        v.wait_ge(s_sq, NT)
        v.wait_ge(s_sum, NT)
        v.tensor_scalar_mul(means[:, :], sums[:, :], 1.0 / D).then_inc(s_vch, 1)
        v.wait_ge(s_vch, 1)
        v.tensor_mul(m2s[:, :], means[:, :], means[:, :]).then_inc(s_vch, 1)
        v.wait_ge(s_vch, 2)
        v.scalar_tensor_tensor(vars_[:, :], sqs[:, :], 1.0 / D, m2s[:, :],
                               ALU.mult, ALU.subtract).then_inc(s_var, 1)
        v.wait_ge(s_std, 1)
        v.reciprocal(rss[:, :], stds[:, :]).then_inc(s_vch, 1)
        v.wait_ge(s_vch, 3)
        v.scalar_tensor_tensor(negmrs[:, :], means[:, :], -1.0, rss[:, :],
                               ALU.mult, ALU.mult).then_inc(s_rs, 1)
        v.wait_ge(s_rs, 1)
        for nt in range(NT):
            if nt > 0:
                v.wait_ge(s_tp, nt)
            v.tensor_scalar(xlnbf[:, :], Xs[:, nt * D:(nt + 1) * D],
                            rss[:, nt:nt + 1], negmrs[:, nt:nt + 1],
                            ALU.mult, ALU.add).then_inc(s_xln, 1)
            v.wait_ge(s_tp, nt + 1)
            bk = arena[:, (nt % 2) * 512:(nt % 2) * 512 + 512].bitcast(BF16)
            v.tensor_copy(
                XlnT[:, :].rearrange("p (kc t) -> p kc t", t=tok)[:, :, nt * 128:(nt + 1) * 128],
                bk[:, :].rearrange("p (kc t) -> p kc t", t=128),
            ).then_inc(s_xlnT, 1)
        v.wait_ge(s_g, 1)
        v.tensor_copy(sgmb[:, :], sgm[:, :]).then_inc(s_sgb, 1)
        v.wait_ge(s_st, NT)
        v.tensor_copy(selTs[:, :], arena[:, 1536:1536 + tok // 2].bitcast(BF16)[0:ne, :]).then_inc(s_sts, 1)
        for st in range(NST):
            for e in range(ne):
                idx = secount(st, e)
                v.wait_ge(s_selm, idx)
                v.tensor_copy(selbs[:, (e % 2) * STW:(e % 2) * STW + STW],
                              arena[:, 1024:1024 + STW]).then_inc(s_sbc, 1)
                v.wait_ge(s_sbc, idx)
                v.wait_ge(s_rel, idx)
                if st > 0:
                    v.wait_ge(s_b, st)
                v.tensor_tensor(Ssall[:, e * STW:(e + 1) * STW],
                                Sg[:, (e % 2) * STW:(e % 2) * STW + STW],
                                selbs[:, (e % 2) * STW:(e % 2) * STW + STW],
                                ALU.mult).then_inc(s_ssm, 1)
            v.wait_ge(s_b, st + 1)
            v.wait_ge(s_ye, st * NTT)
            for t in range(NTT):
                tk = st * NTT + t
                v.tensor_tensor(Xs[:, tk * D:(tk + 1) * D],
                                arena[:, t * 1024:(t + 1) * 1024],
                                Xs[:, tk * D:(tk + 1) * D],
                                ALU.add).then_inc(s_ye, 1)

    @block.tensor
    def _(t):
        t.wait_ge(s_pre, 48)
        t.wait_ge(s_gch, 5)
        for nt in range(NT):
            t.wait_ge(s_xln, nt + 1)
            if nt >= 2:
                t.wait_ge(s_xlnT, nt - 1)
            bk = arena[:, (nt % 2) * 512:(nt % 2) * 512 + 512].bitcast(BF16)
            for kc in range(KC):
                ins = t.transpose(bk[:, kc * 128:(kc + 1) * 128],
                                  xlnbf[:, kc * 128:(kc + 1) * 128], ident[:, :])
            ins.then_inc(s_tp, 1)
            t.wait_ge(s_xlnT, nt + 1)
            for kc in range(KC):
                ins = t.matmul(arena[:, 1024 + nt * ne:1024 + (nt + 1) * ne],
                               XlnT[:, kc * tok + nt * 128: kc * tok + (nt + 1) * 128],
                               esels[:, kc * ne:(kc + 1) * ne],
                               start=(kc == 0), stop=(kc == KC - 1))
            ins.then_inc(s_z, 1)
        seltb = arena[:, 1536:1536 + tok // 2].bitcast(BF16)
        t.wait_ge(s_sgb, 1)
        for nt in range(NT):
            ins = t.transpose(seltb[0:ne, nt * 128:(nt + 1) * 128],
                              sgmb[:, nt * ne:(nt + 1) * ne], ident[:, :])
            ins.then_inc(s_st, 1)
        t.wait_ge(s_xlnT, NT)
        t.wait_ge(s_sts, 1)
        for st in range(NST):
            if st > 0:
                t.wait_ge(s_ye, st * NTT)
            for e in range(ne):
                idx = secount(st, e)
                if idx > 1:
                    t.wait_ge(s_sbc, idx - 1)
                ins = t.matmul(arena[:, 1024:1024 + STW], sel16[:, e * 128:(e + 1) * 128],
                               selTs[:, st * STW:(st + 1) * STW], start=True, stop=True)
                ins.then_inc(s_selm, 1)
                if idx > 2:
                    t.wait_ge(s_rel, idx - 2)
                if e < 2 and st > 0:
                    t.wait_ge(s_ye, st * NTT)
                sbk = arena[:, 3072 + (e % 2) * 512: 3072 + (e % 2) * 512 + STW]
                for kc in range(KC):
                    ins = t.matmul(sbk, keys[:, e * KC * hh + kc * hh: e * KC * hh + (kc + 1) * hh],
                                   XlnT[:, kc * tok + st * STW: kc * tok + (st + 1) * STW],
                                   start=(kc == 0), stop=(kc == KC - 1))
                ins.then_inc(s_s, 1)
            t.wait_ge(s_sbc, (st + 1) * ne)
            for e in range(ne):
                t.wait_ge(s_ssm, secount(st, e))
                if e == 0:
                    t.wait_ge(s_rel, secount(st, ne - 1))
                first = (e == 0)
                last = (e == ne - 1)
                for tt in range(NTT):
                    hst = Ssall[:, e * STW + tt * 128: e * STW + (tt + 1) * 128]
                    t.matmul(arena[:, tt * 1024: tt * 1024 + 512], hst,
                             vals[:, e * D: e * D + 512], start=first, stop=last)
                    ins = t.matmul(arena[:, tt * 1024 + 512:(tt + 1) * 1024], hst,
                                   vals[:, e * D + 512:(e + 1) * D], start=first, stop=last)
            ins.then_inc(s_b, 1)

    @block.sync
    def _(sy):
        sy.wait_ge(s_ye, NST * NTT)
        sy.dma_start(out=x_out.rearrange("(nt p) d -> p nt d", p=128),
                     in_=Xs[:, :].rearrange("p (nt d) -> p nt d", d=D)).then_inc(s_out, 16)
        sy.wait_ge(s_out, 16)

    ctx.close()
    return nc


def prep_ffn1(esel, keysw, valsw):
    ne, d, hh = keysw.shape
    kc = d // 128
    keyr = np.ascontiguousarray(keysw.reshape(ne, kc, 128, hh).transpose(0, 2, 1, 3).reshape(ne, 128, kc * hh))
    valr = np.ascontiguousarray(valsw)
    eselr = np.ascontiguousarray(esel.T.reshape(kc, 128, ne).transpose(1, 0, 2).reshape(128, kc * ne))
    return {"keyr": keyr, "valr": valr, "eselr": eselr}


def run_ffn1(x_flat, inputs, nc_cache={}):
    if "ffn1" not in nc_cache:
        nc_cache["ffn1"] = build_ffn1()
    nc = nc_cache["ffn1"]
    wmap = prep_ffn1(inputs["L1_ffn_expert_sel"], inputs["L1_ffn_keys"], inputs["L1_ffn_values"])
    in_maps = []
    for c in range(8):
        m = {"x_in": np.ascontiguousarray(x_flat[c * TOK:(c + 1) * TOK])}
        m.update(wmap)
        in_maps.append(m)
    res = run_bass_kernel_spmd(nc, in_maps, core_ids=list(range(8)))
    return np.concatenate([res.results[c]["x_out"] for c in range(8)], axis=0)


def build_attn0(T=4096, nh=8, l1=False):
    NTB = T // 128
    NQT = T // 512
    NP = nh // 2
    HW = nh * 64
    KVW = 64 if l1 else HW
    WTOT = HW + 2 * KVW
    SC = 0.125

    nc = bass.Bass("TRN2", target_bir_lowering=False, debug=False)
    xb = nc.dram_tensor("xb", [T, D], F32, kind="ExternalInput").ap()
    wqkvr = nc.dram_tensor("wqkvr", [128, KC * WTOT], F32, kind="ExternalInput").ap()
    ctabr = nc.dram_tensor("ctabr", [128, NTB * 4 * 32], F32, kind="ExternalInput").ap()
    stabr = nc.dram_tensor("stabr", [128, NTB * 4 * 32], F32, kind="ExternalInput").ap()
    if l1:
        biasr = nc.dram_tensor("biasr", [1, WTOT], F32, kind="ExternalInput").ap()
    o_out = nc.dram_tensor("o_out", [NP, 128, T], BF16, kind="ExternalOutput").ap()

    ctx = ExitStack()
    sb = lambda name, shape, dt: ctx.enter_context(nc.sbuf_tensor(name, shape, dt))
    sem = lambda name: ctx.enter_context(nc.semaphore(name))

    wq = sb("wq", [128, KC * WTOT], BF16)
    crep = sb("crep", [128, NTB * 4 * 32], BF16)
    srep = sb("srep", [128, NTB * 4 * 32], BF16)
    QTs = sb("QTs", [128, NP * T], BF16)
    KTs = sb("KTs", [128, (1 if l1 else NP) * T], BF16)
    Vs = sb("Vs", [128, NTB * ((1 if l1 else nh) * 65)], BF16)
    masks = sb("masks", [128, 4 * 512], BF16)
    ident = sb("ident", [128, 128], BF16)
    onesc = sb("onesc", [1, 128], BF16)
    xtile = sb("xtile", [128, D], F32)
    xlnbf = sb("xlnbf", [128, 2 * D], BF16)
    qrot = sb("qrot", [128, 4 * HW], BF16)
    t14 = sb("t14", [128, 4 * 128], BF16)
    stats = sb("stats", [128, 8], F32)
    epst = sb("epst", [128, 1], F32)
    Pp = sb("Pp", [128, 2 * 1024], BF16)
    zb = sb("zb", [1, 1024], F32)
    rzb = sb("rzb", [1, 1024], BF16)
    rb = sb("rb", [64, 1024], BF16)
    otmp = sb("otmp", [64, 2 * 512], BF16)
    Ob = sb("Ob", [128, NP * T], BF16)
    if l1:
        biass = sb("biass", [1, WTOT], BF16)
    XlnTr = sb("XlnTr", [128, 2 * KC * 128], BF16)

    arena = ctx.enter_context(nc.psum_tensor("arena", [128, 4096], F32))

    s_xs = [sem("s_x0"), sem("s_x1")]
    s_pre = sem("s_pre")
    s_gch = sem("s_gch")
    s_vch = sem("s_vch")
    s_sq = sem("s_sq")
    s_v1 = sem("s_v1")
    s_sd = sem("s_sd")
    s_ln = sem("s_ln")
    s_tp = sem("s_tp")
    s_xlnT = sem("s_xlnT")
    s_qk = sem("s_qk")
    s_rot = sem("s_rot")
    s_qt = sem("s_qt")
    s_qts = sem("s_qts")
    s_S = sem("s_S")
    s_exp = sem("s_exp")
    s_msk = sem("s_msk")
    s_pv = sem("s_pv")
    s_nrm = sem("s_nrm")
    s_bc = sem("s_bc")
    s_on = sem("s_on")
    s_ores = [sem("s_ore0"), sem("s_ore1")]
    s_kd = sem("s_kd")
    s_out = sem("s_out")

    block = ctx.enter_context(nc.Block())

    NW = 3 if not l1 else 3
    def qk_idx(nt, which):
        return nt * 3 + which + 1

    pq_list = [(p, qt) for p in range(NP) for qt in range(NQT)]
    ubase = {}
    acc = 0
    for (p, qt) in pq_list:
        ubase[(p, qt)] = acc
        acc += 4 * qt + 4
    NU = acc
    dbase = {}
    dacc = 0
    for (p, qt) in pq_list:
        dbase[(p, qt)] = dacc
        dacc += 4

    def pqi(p, qt):
        return p * NQT + qt

    @block.sync
    def _(sy):
        for nt in range(NTB):
            if nt >= 1:
                sy.wait_ge(s_ln, nt)
                sy.wait_ge(s_sq, nt)
            sy.dma_start(out=xtile[:, :],
                         in_=xb.rearrange("(nt p) d -> p nt d", p=128)[:, nt, :]).then_inc(s_xs[0], 16)
        if l1:
            sy.wait_ge(s_qts, 2 * NTB)
            sy.dma_start(out=KTs[64:128, :], in_=KTs[0:64, :]).then_inc(s_kd, 16)
        for (p, qt) in pq_list:
            u = pqi(p, qt)
            sy.wait_ge(s_on, 2 * u + 2)
            sy.dma_start(out=Ob[64:128, p * T + qt * 512: p * T + (qt + 1) * 512],
                         in_=otmp[0:64, (u % 2) * 512:(u % 2) * 512 + 512]).then_inc(s_ores[u % 2], 16)
        npq = NP * NQT
        sy.wait_ge(s_ores[0], 16 * ((npq + 1) // 2))
        if npq > 1:
            sy.wait_ge(s_ores[1], 16 * (npq // 2))
        for pp in range(NP):
            sy.dma_start(out=o_out[pp], in_=Ob[:, pp * T:(pp + 1) * T]).then_inc(s_out, 16)
        sy.wait_ge(s_out, 16 * NP)

    @block.gpsimd
    def _(g):
        g.dma_start(out=wq[:, :], in_=wqkvr).then_inc(s_pre, 16)
        g.dma_start(out=crep[:, :], in_=ctabr).then_inc(s_pre, 16)
        g.dma_start(out=srep[:, :], in_=stabr).then_inc(s_pre, 16)
        if l1:
            g.dma_start(out=biass[:, :], in_=biasr).then_inc(s_pre, 16)
        gch = [0]
        def gstep(ins):
            gch[0] += 1
            ins.then_inc(s_gch, 1)
            g.wait_ge(s_gch, gch[0])
        gstep(g.memset(epst[:, :], EPS))
        gstep(g.memset(ident[:, :], 0.0))
        gstep(g.affine_select(out=ident[:, :], in_=ident[:, :],
                              compare_op=ALU.not_equal, fill=1.0, base=0,
                              pattern=[[-1, 128]], channel_multiplier=1))
        gstep(g.memset(onesc[:, :], 1.0))
        nhv = 1 if l1 else nh
        gstep(g.memset(Vs[:, :].rearrange("p (b h c2) -> p b h c2", h=nhv, c2=65)[:, :, :, 64:65], 1.0))
        for b in range(4):
            gstep(g.memset(masks[:, b * 512:(b + 1) * 512], 1.0))
            gstep(g.affine_select(
                out=masks[:, b * 512:(b + 1) * 512],
                in_=masks[:, b * 512:(b + 1) * 512],
                compare_op=ALU.is_ge, fill=0.0, base=-128 * b,
                pattern=[[1, 512]], channel_multiplier=-1))
    NCONST = 13

    @block.scalar
    def _(s):
        for nt in range(NTB):
            s.wait_ge(s_xs[0], 16 * (nt + 1))
            if nt > 0:
                s.wait_ge(s_sq, nt)
                s.wait_ge(s_v1, nt)
            s.activation(out=Pp[:, 0:1024],
                         in_=xtile[:, :], func=AF.Square,
                         accum_out=stats[:, 1:2]).then_inc(s_sq, 1)
            s.wait_ge(s_v1, nt + 1)
            s.wait_ge(s_gch, 1)
            s.activation(out=stats[:, 5:6], in_=stats[:, 4:5], func=AF.Sqrt,
                         bias=epst[:, 0:1]).then_inc(s_sd, 1)
        for (p, qt) in pq_list:
            nkb = 4 * qt + 4
            for kb in range(nkb):
                u = ubase[(p, qt)] + kb + 1
                s.wait_ge(s_S, u)
                if u > 2:
                    s.wait_ge(s_pv, u - 2)
                ub = (u - 1) % 2
                s.activation(out=Pp[:, ub * 1024:(ub + 1) * 1024],
                             in_=arena[:, ub * 1024:(ub + 1) * 1024],
                             func=AF.Exp, scale=SC).then_inc(s_exp, 1)

    @block.vector
    def _(v):
        vch = [0]
        def vstep(ins):
            vch[0] += 1
            ins.then_inc(s_vch, 1)
            v.wait_ge(s_vch, vch[0])
        for nt in range(NTB):
            v.wait_ge(s_xs[0], 16 * (nt + 1))
            vstep(v.reduce_sum(stats[:, 0:1], xtile[:, :], axis=AX.X))
            vstep(v.tensor_scalar_mul(stats[:, 2:3], stats[:, 0:1], 1.0 / D))
            vstep(v.tensor_mul(stats[:, 3:4], stats[:, 2:3], stats[:, 2:3]))
            v.wait_ge(s_sq, nt + 1)
            v.scalar_tensor_tensor(stats[:, 4:5], stats[:, 1:2], 1.0 / D,
                                   stats[:, 3:4], ALU.mult, ALU.subtract).then_inc(s_v1, 1)
            v.wait_ge(s_v1, nt + 1)
            v.wait_ge(s_sd, nt + 1)
            vstep(v.reciprocal(stats[:, 6:7], stats[:, 5:6]))
            vstep(v.scalar_tensor_tensor(stats[:, 7:8], stats[:, 2:3], -1.0,
                                         stats[:, 6:7], ALU.mult, ALU.mult))
            if nt > 0:
                v.wait_ge(s_tp, nt)
            v.tensor_scalar(xlnbf[:, (nt % 2) * D:(nt % 2 + 1) * D],
                            xtile[:, :],
                            stats[:, 6:7], stats[:, 7:8],
                            ALU.mult, ALU.add).then_inc(s_ln, 1)
            v.wait_ge(s_tp, nt + 1)
            bk = arena[:, (nt % 2) * 512:(nt % 2) * 512 + 512].bitcast(BF16)
            slot = nt % 2
            v.tensor_copy(
                XlnTr[:, :].rearrange("p (s kc t) -> p s kc t", kc=KC, t=128)[:, slot],
                bk[:, :].rearrange("p (kc t) -> p kc t", t=128),
            ).then_inc(s_xlnT, 1)
            for which in range(3):
                v.wait_ge(s_qk, qk_idx(nt, which))
                if which < 2 and nt >= 2:
                    v.wait_ge(s_qt, 2 * (nt - 2) + which + 1)
                pb = arena[:, 1024 + which * 512: 1024 + which * 512 + (HW if which == 0 else KVW)]
                if which == 2:
                    nhv = 1 if l1 else nh
                    v.tensor_copy(
                        Vs[:, :].rearrange("p (b h c2) -> p b h c2", h=nhv, c2=65)[:, nt, :, 0:64],
                        pb.rearrange("p (h c) -> p h c", c=64),
                    ).then_inc(s_rot, 1)
                    continue
                nhr = 1 if (l1 and which == 1) else nh
                ngrp = (nhr + 3) // 4
                for grp in range(ngrp):
                    gh = min(4, nhr - grp * 4)
                    wpb = pb[:, grp * 4 * 64: grp * 4 * 64 + gh * 64].rearrange("p (h c) -> p h c", c=64)
                    ev = wpb[:, :, 0:32]
                    od = wpb[:, :, 32:64]
                    ctb = crep[:, nt * 128: nt * 128 + gh * 32].rearrange("p (h c) -> p h c", c=32)
                    stb = srep[:, nt * 128: nt * 128 + gh * 32].rearrange("p (h c) -> p h c", c=32)
                    w32 = gh * 32
                    t1 = t14[:, 0 * w32:1 * w32].rearrange("p (h c) -> p h c", c=32)
                    t2 = t14[:, 1 * w32:2 * w32].rearrange("p (h c) -> p h c", c=32)
                    t3 = t14[:, 2 * w32:3 * w32].rearrange("p (h c) -> p h c", c=32)
                    t4 = t14[:, 3 * w32:4 * w32].rearrange("p (h c) -> p h c", c=32)
                    qr = qrot[:, ((nt % 2) * 2 + which) * HW + grp * 4 * 64:
                              ((nt % 2) * 2 + which) * HW + grp * 4 * 64 + gh * 64].rearrange("p (h c) -> p h c", c=64)
                    vstep(v.tensor_tensor(t1, ev, ctb, ALU.mult))
                    vstep(v.tensor_tensor(t2, od, stb, ALU.mult))
                    vstep(v.tensor_tensor(t3, od, ctb, ALU.mult))
                    vstep(v.tensor_tensor(t4, ev, stb, ALU.mult))
                    last = (grp == ngrp - 1)
                    if not l1:
                        vstep(v.tensor_sub(qr[:, :, 0:32], t1, t2))
                        ins = v.tensor_add(qr[:, :, 32:64], t3, t4)
                    else:
                        vstep(v.tensor_add(qr[:, :, 0:32], t1, t2))
                        ins = v.tensor_sub(qr[:, :, 32:64], t4, t3)
                    if last:
                        ins.then_inc(s_rot, 1)
                    else:
                        ins.then_inc(s_vch, 1)
                        vch[0] += 1
                        v.wait_ge(s_vch, vch[0])
            for which in range(2):
                v.wait_ge(s_qt, nt * 2 + which + 1)
                bkq = arena[:, (5 + which) * 512:(5 + which) * 512 + 512].bitcast(BF16)
                if l1 and which == 1:
                    v.tensor_copy(KTs[0:64, nt * 128:(nt + 1) * 128], bkq[0:64, 0:128]).then_inc(s_qts, 1)
                    continue
                dst = (QTs if which == 0 else KTs)
                v.tensor_copy(
                    dst[:, :].rearrange("p (np t) -> p np t", t=T)[:, :, nt * 128:(nt + 1) * 128],
                    bkq[:, :].rearrange("p (np t) -> p np t", t=128)[:, 0:NP],
                ).then_inc(s_qts, 1)
        for (p, qt) in pq_list:
            u0 = ubase[(p, qt)]
            nkb = 4 * qt + 4
            for kb in range(4 * qt, nkb):
                u = u0 + kb + 1
                b = kb - 4 * qt
                v.wait_ge(s_exp, u)
                ub = (u - 1) % 2
                v.tensor_tensor(Pp[:, ub * 1024:ub * 1024 + 512],
                                Pp[:, ub * 1024:ub * 1024 + 512],
                                masks[:, b * 512:(b + 1) * 512], ALU.mult).then_inc(s_vch, 1)
                v.wait_ge(s_vch, vch[0] + 1)
                vch[0] += 1
                v.tensor_tensor(Pp[:, ub * 1024 + 512:(ub + 1) * 1024],
                                Pp[:, ub * 1024 + 512:(ub + 1) * 1024],
                                masks[:, b * 512:(b + 1) * 512], ALU.mult).then_inc(s_msk, 1)
            pq = pqi(p, qt)
            v.wait_ge(s_pv, u0 + nkb)
            vstep(v.tensor_copy(zb[0:1, 0:512], arena[64:65, 2048:2560]))
            vstep(v.tensor_copy(zb[0:1, 512:1024], arena[64:65, 2560:3072]))
            vstep(v.reciprocal(zb[0:1, :], zb[0:1, :]))
            v.tensor_copy(rzb[0:1, :], zb[0:1, :]).then_inc(s_nrm, 1)
            v.wait_ge(s_bc, pq + 1)
            vstep(v.tensor_copy(rb[0:64, :], arena[0:64, 3072:4096]))
            v.tensor_tensor(Ob[0:64, p * T + qt * 512: p * T + (qt + 1) * 512],
                            arena[0:64, 2048:2560], rb[0:64, 0:512], ALU.mult).then_inc(s_on, 1)
            if pq >= 2:
                v.wait_ge(s_ores[pq % 2], 16 * (pq // 2))
            v.tensor_tensor(otmp[0:64, (pq % 2) * 512:(pq % 2) * 512 + 512],
                            arena[0:64, 2560:3072], rb[0:64, 512:1024], ALU.mult).then_inc(s_on, 1)

    @block.tensor
    def _(t):
        t.wait_ge(s_pre, 16 * (4 if l1 else 3))
        t.wait_ge(s_gch, NCONST)
        for nt in range(NTB):
            t.wait_ge(s_ln, nt + 1)
            if nt >= 2:
                t.wait_ge(s_xlnT, nt - 1)
            bk = arena[:, (nt % 2) * 512:(nt % 2) * 512 + 512].bitcast(BF16)
            for kc in range(KC):
                ins = t.transpose(bk[:, kc * 128:(kc + 1) * 128],
                                  xlnbf[:, (nt % 2) * D + kc * 128:(nt % 2) * D + (kc + 1) * 128],
                                  ident[:, :])
            ins.then_inc(s_tp, 1)
            t.wait_ge(s_xlnT, nt + 1)
            slot = nt % 2
            for which in range(3):
                w = HW if which == 0 else KVW
                idx = qk_idx(nt, which)
                if idx > 3:
                    t.wait_ge(s_rot, idx - 3)
                pb = arena[:, 1024 + which * 512: 1024 + which * 512 + w]
                for kc in range(KC):
                    ins = t.matmul(pb,
                                   XlnTr[:, slot * KC * 128 + kc * 128: slot * KC * 128 + (kc + 1) * 128],
                                   wq[:, kc * WTOT + (0 if which == 0 else HW + (which - 1) * KVW):
                                      kc * WTOT + (0 if which == 0 else HW + (which - 1) * KVW) + w],
                                   start=(kc == 0), stop=(kc == KC - 1 and not l1))
                if l1:
                    ins = t.matmul(pb, onesc[0:1, 0:128],
                                   biass[0:1, (0 if which == 0 else HW + (which - 1) * KVW):
                                         (0 if which == 0 else HW + (which - 1) * KVW) + w],
                                   start=False, stop=True)
                ins.then_inc(s_qk, 1)
            for which in range(2):
                t.wait_ge(s_rot, qk_idx(nt, which))
                if nt >= 2:
                    t.wait_ge(s_qts, 2 * (nt - 1))
                nblk = 1 if (l1 and which == 1) else NP
                bkq = arena[:, (5 + which) * 512:(5 + which) * 512 + 512].bitcast(BF16)
                for i in range(nblk):
                    ins = t.transpose(bkq[:, i * 128:(i + 1) * 128],
                                      qrot[:, ((nt % 2) * 2 + which) * HW + i * 128:((nt % 2) * 2 + which) * HW + (i + 1) * 128],
                                      ident[:, :])
                ins.then_inc(s_qt, 1)
        t.wait_ge(s_qts, 2 * NTB)
        t.wait_ge(s_rot, 3 * NTB)
        if l1:
            t.wait_ge(s_kd, 16)
        for (p, qt) in pq_list:
            pq = pqi(p, qt)
            u0 = ubase[(p, qt)]
            nkb = 4 * qt + 4
            for kb in range(nkb):
                u = u0 + kb + 1
                ub = (u - 1) % 2
                diag = kb >= 4 * qt
                if u > 2:
                    t.wait_ge(s_exp, u - 2)
                kt_off = (0 if l1 else p * T)
                for hh_ in range(2):
                    lo = hh_ * 64
                    ins = t.matmul(arena[:, ub * 1024 + hh_ * 512: ub * 1024 + (hh_ + 1) * 512],
                                   KTs[lo:lo + 64, kt_off + kb * 128: kt_off + (kb + 1) * 128],
                                   QTs[lo:lo + 64, p * T + qt * 512: p * T + (qt + 1) * 512],
                                   start=True, stop=True)
                ins.then_inc(s_S, 1)
                if kb == 0:
                    t.wait_ge(s_on, 2 * pq)
                t.wait_ge(s_msk if diag else s_exp,
                          (dbase[(p, qt)] + kb - 4 * qt + 1) if diag else u)
                nhv = 1 if l1 else nh
                for hh_ in range(2):
                    hv = 0 if l1 else (2 * p + hh_)
                    ins = t.matmul(arena[:, 2048 + hh_ * 512: 2048 + hh_ * 512 + 512][0:65, :],
                                   Vs[:, kb * (nhv * 65) + hv * 65: kb * (nhv * 65) + (hv + 1) * 65],
                                   Pp[:, ub * 1024 + hh_ * 512: ub * 1024 + (hh_ + 1) * 512],
                                   start=(kb == 0), stop=(kb == nkb - 1))
                ins.then_inc(s_pv, 1)
            t.wait_ge(s_nrm, pq + 1)
            t.matmul(arena[0:64, 3072:3584], onesc[0:1, 0:64], rzb[0:1, 0:512],
                     start=True, stop=True)
            ins = t.matmul(arena[0:64, 3584:4096], onesc[0:1, 0:64], rzb[0:1, 512:1024],
                           start=True, stop=True)
            ins.then_inc(s_bc, 1)

    ctx.close()
    return nc


def build_proj(tok=TOK, l1=False):
    NT = tok // 128
    nc = bass.Bass("TRN2", target_bir_lowering=False, debug=False)
    x_in = nc.dram_tensor("x_in", [tok, D], F32, kind="ExternalInput").ap()
    oTr = nc.dram_tensor("oTr", [D, tok], BF16, kind="ExternalInput").ap()
    wor = nc.dram_tensor("wor", [128, KC * D], F32, kind="ExternalInput").ap()
    if l1:
        bor = nc.dram_tensor("bor", [1, D], F32, kind="ExternalInput").ap()
    x_out = nc.dram_tensor("x_out", [tok, D], F32, kind="ExternalOutput").ap()

    ctx = ExitStack()
    sb = lambda name, shape, dt: ctx.enter_context(nc.sbuf_tensor(name, shape, dt))
    sem = lambda name: ctx.enter_context(nc.semaphore(name))

    Xs = sb("Xs", [128, NT * D], F32)
    oTs = sb("oTs", [128, KC * tok], BF16)
    wos = sb("wos", [128, KC * D], BF16)
    onesc = sb("onesc", [1, 128], BF16)
    if l1:
        bos = sb("bos", [1, D], BF16)
    arena = ctx.enter_context(nc.psum_tensor("arena", [128, 4096], F32))

    s_x = sem("s_x")
    s_pre = sem("s_pre")
    s_gch = sem("s_gch")
    s_y = sem("s_y")
    s_ye = sem("s_ye")
    s_out = sem("s_out")

    block = ctx.enter_context(nc.Block())

    @block.sync
    def _(sy):
        sy.dma_start(out=Xs[:, :].rearrange("p (nt d) -> p nt d", d=D),
                     in_=x_in.rearrange("(nt p) d -> p nt d", p=128)).then_inc(s_x, 16)
        sy.dma_start(out=oTs[:, :].rearrange("p (kc t) -> p kc t", t=tok),
                     in_=oTr.rearrange("(kc p) t -> p kc t", p=128)).then_inc(s_x, 16)

    @block.gpsimd
    def _(g):
        g.dma_start(out=wos[:, :], in_=wor).then_inc(s_pre, 16)
        if l1:
            g.dma_start(out=bos[:, :], in_=bor).then_inc(s_pre, 16)
        g.memset(onesc[:, :], 1.0).then_inc(s_gch, 1)

    @block.tensor
    def _(t):
        t.wait_ge(s_x, 32)
        t.wait_ge(s_pre, 32 if l1 else 16)
        t.wait_ge(s_gch, 1)
        for nt in range(NT):
            for dh in range(2):
                idx = nt * 2 + dh + 1
                if idx > 4:
                    t.wait_ge(s_ye, idx - 4)
                pb = arena[:, ((idx - 1) % 4) * 512: ((idx - 1) % 4) * 512 + 512]
                for cc in range(KC):
                    ins = t.matmul(pb,
                                   oTs[:, cc * tok + nt * 128: cc * tok + (nt + 1) * 128],
                                   wos[:, cc * D + dh * 512: cc * D + (dh + 1) * 512],
                                   start=(cc == 0), stop=(cc == KC - 1 and not l1))
                if l1:
                    ins = t.matmul(pb, onesc[0:1, 0:128], bos[0:1, dh * 512:(dh + 1) * 512],
                                   start=False, stop=True)
                ins.then_inc(s_y, 1)

    @block.vector
    def _(v):
        for nt in range(NT):
            for dh in range(2):
                idx = nt * 2 + dh + 1
                v.wait_ge(s_y, idx)
                pb = arena[:, ((idx - 1) % 4) * 512: ((idx - 1) % 4) * 512 + 512]
                v.tensor_tensor(Xs[:, nt * D + dh * 512: nt * D + (dh + 1) * 512],
                                pb,
                                Xs[:, nt * D + dh * 512: nt * D + (dh + 1) * 512],
                                ALU.add).then_inc(s_ye, 1)

    @block.sync
    def _(sy):
        sy.wait_ge(s_ye, NT * 2)
        sy.dma_start(out=x_out.rearrange("(nt p) d -> p nt d", p=128),
                     in_=Xs[:, :].rearrange("p (nt d) -> p nt d", d=D)).then_inc(s_out, 16)
        sy.wait_ge(s_out, 16)

    ctx.close()
    return nc



def _rope_tabs(T, nh):
    inv = 1.0 / (10000.0 ** (np.arange(0, 64, 2, dtype=np.float64) / 64.0))
    pos = np.arange(T, dtype=np.float64)
    ang = np.outer(pos, inv)
    c = np.cos(ang).astype(np.float32)
    s = np.sin(ang).astype(np.float32)
    ntb = T // 128
    crep = np.broadcast_to(c.reshape(ntb, 128, 1, 32), (ntb, 128, nh, 32))
    srep = np.broadcast_to(s.reshape(ntb, 128, 1, 32), (ntb, 128, nh, 32))
    crep = np.ascontiguousarray(crep.transpose(1, 0, 2, 3).reshape(128, ntb * nh * 32))
    srep = np.ascontiguousarray(srep.transpose(1, 0, 2, 3).reshape(128, ntb * nh * 32))
    return crep, srep


def _eo_perm(n=64):
    return np.concatenate([np.arange(0, n, 2), np.arange(1, n, 2)])


def prep_attn0(w_qkv, heads, T=4096):
    nh = len(heads)
    perm = _eo_perm()
    cols = []
    for part in range(3):
        for h in heads:
            blk = w_qkv[:, part * D + h * 64: part * D + (h + 1) * 64]
            cols.append(blk[:, perm] if part < 2 else blk)
    wsl = np.concatenate(cols, axis=1)
    wqkvr = np.ascontiguousarray(
        wsl.reshape(KC, 128, 3 * nh * 64).transpose(1, 0, 2).reshape(128, -1))
    crep, srep = _rope_tabs(T, 4)
    return {"wqkvr": wqkvr, "ctabr": crep, "stabr": srep}


def prep_attn1(Wq, bq, Wk, bk, Wv, bv, heads, T=4096):
    nh = len(heads)
    qcols = np.concatenate([Wq[:, h * 64:(h + 1) * 64] for h in heads], axis=1)
    wsl = np.concatenate([qcols, Wk, Wv], axis=1)
    wqkvr = np.ascontiguousarray(wsl.reshape(KC, 128, wsl.shape[1]).transpose(1, 0, 2).reshape(128, -1))
    bias = np.concatenate([np.concatenate([bq[h * 64:(h + 1) * 64] for h in heads]), bk, bv])
    crep, srep = _rope_tabs(T, 4)
    return {"wqkvr": wqkvr, "ctabr": crep, "stabr": srep, "biasr": bias.reshape(1, -1).astype(np.float32)}


def prep_proj(w_out):
    wor = np.ascontiguousarray(w_out.reshape(KC, 128, D).transpose(1, 0, 2).reshape(128, KC * D))
    return wor


def run_attn_layer(x_flat, inputs, layer, nc_cache={}):
    import ml_dtypes
    key = f"attn{layer}"
    if key not in nc_cache:
        nc_cache[key] = build_attn0(T=4096, nh=8, l1=(layer == 1))
        nc_cache[f"proj{layer}"] = build_proj(l1=(layer == 1))
    nc_a = nc_cache[key]
    nc_p = nc_cache[f"proj{layer}"]
    in_maps = []
    for c in range(8):
        b = c // 2
        heads = list(range(8 * (c % 2), 8 * (c % 2) + 8))
        xb = np.ascontiguousarray(x_flat[b * 4096:(b + 1) * 4096])
        if layer == 0:
            m = prep_attn0(inputs["L0_attn_w_qkv"], heads)
        else:
            m = prep_attn1(inputs["L1_attn_Wq"], inputs["L1_attn_bq"],
                           inputs["L1_attn_Wk"], inputs["L1_attn_bk"],
                           inputs["L1_attn_Wv"], inputs["L1_attn_bv"], heads)
        m["xb"] = xb
        in_maps.append(m)
    res = run_bass_kernel_spmd(nc_a, in_maps, core_ids=list(range(8)))
    wo = prep_proj(inputs["L0_attn_w_out"] if layer == 0 else inputs["L1_attn_Wo"])
    in_maps2 = []
    for c in range(8):
        b = c // 2
        o_lo = res.results[2 * b]["o_out"]
        o_hi = res.results[2 * b + 1]["o_out"]
        oT = np.concatenate([o_lo.reshape(512, 4096), o_hi.reshape(512, 4096)], axis=0)
        oT = np.ascontiguousarray(oT[:, (c % 2) * 2048:(c % 2) * 2048 + 2048])
        m = {"x_in": np.ascontiguousarray(x_flat[c * TOK:(c + 1) * TOK]),
             "oTr": oT, "wor": wo}
        if layer == 1:
            m["bor"] = np.ascontiguousarray(inputs["L1_attn_bo"].reshape(1, D).astype(np.float32))
        in_maps2.append(m)
    res2 = run_bass_kernel_spmd(nc_p, in_maps2, core_ids=list(range(8)))
    return np.concatenate([res2.results[c]["x_out"] for c in range(8)], axis=0)



class FCtx:
    def __init__(self, nc):
        self.nc = nc
        self.sems = {}
        self.cnt = {}
        self.buf = {}
        self.arena = None

    def mksem(self, name, stack):
        s = stack.enter_context(self.nc.semaphore(name))
        self.sems[name] = s
        return s

    def vstep(self, v, ins):
        self.cnt["vch"] = self.cnt.get("vch", 0) + 1
        ins.then_inc(self.sems["vch"], 1)
        v.wait_ge(self.sems["vch"], self.cnt["vch"])

    def gstep(self, g, ins):
        self.cnt["gch"] = self.cnt.get("gch", 0) + 1
        ins.then_inc(self.sems["gch"], 1)
        g.wait_ge(self.sems["gch"], self.cnt["gch"])

    def inc(self, ins, name, amt=1):
        ins.then_inc(self.sems[name], amt)
        return ins


def emit_ln_phase(ctx, xs_ready, xs_tile=None):
    b = ctx.buf
    S = ctx.sems
    C = ctx.cnt
    tok = ctx.tok
    NT = tok // 128
    B = {k: C.get(k, 0) for k in ("ln_sq", "ln_sum", "ln_var", "ln_std", "ln_rs",
                                  "ln_xln", "ln_tp", "ln_xlnT")}

    G = 4 if NT % 4 == 0 else NT
    NGR = NT // G

    def scalar(s):
        def copy_tile(s, nt):
            s.wait_ge(S["ln_tp"], B["ln_tp"] + nt + 1)
            bk = ctx.arena[:, (nt % 2) * 512:(nt % 2) * 512 + 512].bitcast(BF16)
            s.activation(
                out=b["XlnT"][:, :].rearrange("p (kc t) -> p kc t", t=tok)[:, :, nt * 128:(nt + 1) * 128],
                in_=bk[:, :].rearrange("p (kc t) -> p kc t", t=128),
                func=AF.Copy).then_inc(S["ln_xlnT"], 1)
        xs_ready(s)
        for g in range(NGR):
            for k in range(G):
                nt = g * G + k
                if xs_tile is not None:
                    xs_tile(s, nt)
                if nt > 0:
                    s.wait_ge(S["ln_sq"], B["ln_sq"] + nt)
                if B["ln_sq"] > 0 and nt == 0:
                    s.wait_ge(S["ln_var"], B["ln_var"])
                s.activation(out=b["junk"][:, 0:D],
                             in_=b["Xs"][:, nt * D:(nt + 1) * D], func=AF.Square,
                             accum_out=b["sqs"][:, nt:nt + 1]).then_inc(S["ln_sq"], 1)
            if g >= 1:
                for k in range(G):
                    copy_tile(s, (g - 1) * G + k)
            s.wait_ge(S["ln_var"], B["ln_var"] + g + 1)
            s.activation(out=b["stds"][:, g * G:(g + 1) * G], in_=b["vars_"][:, g * G:(g + 1) * G],
                         func=AF.Sqrt, bias=b["epst"][:, 0:1]).then_inc(S["ln_std"], 1)
        for k in range(G):
            copy_tile(s, (NGR - 1) * G + k)

    def vector(v):
        xs_ready(v)
        for g in range(NGR):
            sl = slice(g * G, (g + 1) * G)
            for k in range(G):
                nt = g * G + k
                if xs_tile is not None:
                    xs_tile(v, nt)
                v.reduce_sum(b["sums"][:, nt:nt + 1], b["Xs"][:, nt * D:(nt + 1) * D],
                             axis=AX.X).then_inc(S["ln_sum"], 1)
            v.wait_ge(S["ln_sq"], B["ln_sq"] + (g + 1) * G)
            v.wait_ge(S["ln_sum"], B["ln_sum"] + (g + 1) * G)
            ctx.vstep(v, v.tensor_scalar_mul(b["means"][:, sl], b["sums"][:, sl], 1.0 / D))
            ctx.vstep(v, v.tensor_mul(b["m2s"][:, sl], b["means"][:, sl], b["means"][:, sl]))
            v.scalar_tensor_tensor(b["vars_"][:, sl], b["sqs"][:, sl], 1.0 / D,
                                   b["m2s"][:, sl], ALU.mult, ALU.subtract).then_inc(S["ln_var"], 1)
            v.wait_ge(S["ln_std"], B["ln_std"] + g + 1)
            ctx.vstep(v, v.reciprocal(b["rss"][:, sl], b["stds"][:, sl]))
            v.scalar_tensor_tensor(b["negmrs"][:, sl], b["means"][:, sl], -1.0,
                                   b["rss"][:, sl], ALU.mult, ALU.mult).then_inc(S["ln_rs"], 1)
            v.wait_ge(S["ln_rs"], B["ln_rs"] + g + 1)
            for k in range(G):
                nt = g * G + k
                if nt > 0:
                    v.wait_ge(S["ln_tp"], B["ln_tp"] + nt)
                v.tensor_scalar(b["xlnbf"][:, :], b["Xs"][:, nt * D:(nt + 1) * D],
                                b["rss"][:, nt:nt + 1], b["negmrs"][:, nt:nt + 1],
                                ALU.mult, ALU.add).then_inc(S["ln_xln"], 1)

    def tensor(t):
        xs_ready(t)
        for nt in range(NT):
            t.wait_ge(S["ln_xln"], B["ln_xln"] + nt + 1)
            if nt >= 2:
                t.wait_ge(S["ln_xlnT"], B["ln_xlnT"] + nt - 1)
            elif B["ln_xlnT"] > 0:
                t.wait_ge(S["ln_xlnT"], B["ln_xlnT"])
            bk = ctx.arena[:, (nt % 2) * 512:(nt % 2) * 512 + 512].bitcast(BF16)
            for kc in range(KC):
                ins = t.transpose(bk[:, kc * 128:(kc + 1) * 128],
                                  b["xlnbf"][:, kc * 128:(kc + 1) * 128], b["ident"][:, :])
            ins.then_inc(S["ln_tp"], 1)

    for k, dv in dict(ln_sq=NT, ln_sum=NT, ln_var=NGR, ln_std=NGR, ln_rs=NGR,
                      ln_xln=NT, ln_tp=NT, ln_xlnT=NT).items():
        C[k] = B[k] + dv
    return {"scalar": scalar, "vector": vector, "tensor": tensor}


def emit_ffn0(ctx, widx, guard=None):
    b = ctx.buf
    S = ctx.sems
    C = ctx.cnt
    tok = ctx.tok
    ne, nj = ctx.ne0, ctx.nj0
    NST = max(tok // 512, 1)
    STW = tok // NST
    NTT = STW // 128
    NJH = 2
    JH = nj // NJH
    NRND = ne * NJH
    GT = min(4, JH)
    NG = JH // GT
    RING = max(NG, 2)
    sc = b["scratch"]
    w1ring = sc[:, 0:16384]
    w2ring = sc[:, 16384:32768]
    Hss = sc[:, 32768:40960]
    Hgel = sc[:, 40960:41984]
    cwbs = sc[:, 41984:43008]
    arena = ctx.arena
    names = ("z", "g", "cw", "cwt", "cwts", "h", "gel", "cwm", "selm", "cwbc",
             "fb", "ye") + tuple(f"w1_{i}" for i in range(RING)) + tuple(f"w2_{i}" for i in range(RING))
    B = {k: C.get(k, 0) for k in names}

    def hcount(rnd, st, j):
        return (rnd * NST + st) * JH + j + 1

    def fbcount(rnd, st, tti, j):
        return ((rnd * NST + st) * NTT + tti) * JH + j + 1

    def evacs_before(rnd, st):
        return (rnd * NST + st) * NTT

    def hbank(j):
        return arena[:, 3072 + (j % 2) * 512: 3072 + (j % 2) * 512 + STW]

    def gpsimd(g):
        if guard is not None:
            for (gs_, gv_) in guard:
                g.wait_ge(S[gs_], gv_)
        for rnd in range(NRND):
            e, jh = divmod(rnd, NJH)
            pairs = [(q, which) for q in range(NG) for which in (0, 1)]
            if rnd == 0:
                pairs = [(q, 0) for q in range(NG)] + [(q, 1) for q in range(NG)]
            for (q, which) in pairs:
                if True:
                    gidx = rnd * NG + q
                    slot = gidx % RING
                    relname = ("h" if which == 0 else "fb")
                    if gidx >= RING:
                        prnd, pq_ = divmod(gidx - RING, NG)
                        if which == 0:
                            g.wait_ge(S["h"], B["h"] + hcount(prnd, NST - 1, pq_ * GT + GT - 1))
                        else:
                            g.wait_ge(S["fb"], B["fb"] + fbcount(prnd, NST - 1, NTT - 1, pq_ * GT + GT - 1))
                    elif widx > 0 or B[relname] > 0:
                        g.wait_ge(S[relname], B[relname])
                    gg = (jh * JH + q * GT) // GT
                    ring = w1ring if which == 0 else w2ring
                    wsname = f"w{which + 1}_{slot}"
                    g.dma_start(out=ring[:, slot * GT * D:(slot + 1) * GT * D],
                                in_=(b["w1r"] if which == 0 else b["w2r"])[e, gg]
                                ).then_inc(S[wsname], 16)

    def scalar(s):
        s.wait_ge(S["z"], B["z"] + (tok // 128))
        s.activation(out=b["gsm"][:, 0:(tok // 128) * ne], in_=arena[:, 1024:1024 + (tok // 128) * ne],
                     func=AF.Exp).then_inc(S["g"], 1)
        for rnd in range(NRND):
            e, jh = divmod(rnd, NJH)
            for st in range(NST):
                for j in range(JH):
                    idx = hcount(rnd, st, j)
                    s.wait_ge(S["h"], B["h"] + idx)
                    if idx > 2:
                        s.wait_ge(S["cwm"], B["cwm"] + idx - 2)
                    elif B["cwm"] > 0:
                        s.wait_ge(S["cwm"], B["cwm"])
                    s.activation(out=Hgel[:, (j % 2) * STW:(j % 2) * STW + STW],
                                 in_=hbank(j), func=AF.Gelu,
                                 bias=b["b1s"][:, e * nj + jh * JH + j: e * nj + jh * JH + j + 1],
                                 ).then_inc(S["gel"], 1)

    def vector(v):
        NT = tok // 128
        v.wait_ge(S["g"], B["g"] + 1)
        for nt in range(NT):
            gt = b["gsm"][:, nt * ne:(nt + 1) * ne]
            ctx.vstep(v, v.reduce_sum(b["r1"][:, :], gt, axis=AX.X))
            ctx.vstep(v, v.reciprocal(b["r2"][:, :], b["r1"][:, :]))
            ctx.vstep(v, v.tensor_scalar(b["gs1"][:, 0:ne], gt, b["r2"][:, 0:1], None, ALU.mult))
            ctx.vstep(v, v.reduce_max(b["r1"][:, :], b["gs1"][:, 0:ne], axis=AX.X))
            ctx.vstep(v, v.tensor_scalar(b["gs2"][:, 0:ne], b["gs1"][:, 0:ne], b["r1"][:, 0:1], None, ALU.is_ge))
            ctx.vstep(v, v.scalar_tensor_tensor(b["gs3"][:, 0:ne], b["gs2"][:, 0:ne], -1e30,
                                                b["gs1"][:, 0:ne], ALU.mult, ALU.add))
            ctx.vstep(v, v.reduce_max(b["r1"][:, :], b["gs3"][:, 0:ne], axis=AX.X))
            ctx.vstep(v, v.tensor_scalar(b["gs2"][:, 0:ne], b["gs1"][:, 0:ne], b["r1"][:, 0:1], None, ALU.is_ge))
            ctx.vstep(v, v.scalar_tensor_tensor(b["gs3"][:, 0:ne], b["gs2"][:, 0:ne], 0.5,
                                                b["gs1"][:, 0:ne], ALU.mult, ALU.mult))
            v.tensor_copy(b["cwb_all"][:, nt * ne:(nt + 1) * ne], b["gs3"][:, 0:ne]).then_inc(S["cw"], 1)
            v.wait_ge(S["cw"], B["cw"] + nt + 1)
        v.wait_ge(S["cwt"], B["cwt"] + NT)
        v.tensor_copy(b["cwTs"][0:ne, 0:tok],
                      arena[:, 1536:1536 + tok // 2].bitcast(BF16)[0:ne, :]).then_inc(S["cwts"], 1)
        for rnd in range(NRND):
            for st in range(NST):
                prevs = rnd * NST + st - 1
                v.wait_ge(S["selm"], B["selm"] + rnd * NST + st + 1)
                v.tensor_copy(cwbs[:, (st % 2) * STW:(st % 2) * STW + STW],
                              arena[:, 2560:2560 + STW]).then_inc(S["cwbc"], 1)
                v.wait_ge(S["cwbc"], B["cwbc"] + rnd * NST + st + 1)
                for j in range(JH):
                    idx = hcount(rnd, st, j)
                    v.wait_ge(S["gel"], B["gel"] + idx)
                    if prevs >= 0:
                        v.wait_ge(S["fb"], B["fb"] + fbcount(rnd, st - 1, NTT - 1, j) if st > 0
                                  else B["fb"] + fbcount(rnd - 1, NST - 1, NTT - 1, j))
                    elif B["fb"] > 0:
                        v.wait_ge(S["fb"], B["fb"])
                    v.tensor_tensor(Hss[:, j * STW:(j + 1) * STW],
                                    Hgel[:, (j % 2) * STW:(j % 2) * STW + STW],
                                    cwbs[:, (st % 2) * STW:(st % 2) * STW + STW],
                                    ALU.mult).then_inc(S["cwm"], 1)
                for tti in range(NTT):
                    tt = NTT - 1 - tti
                    tk = st * NTT + tt
                    v.wait_ge(S["fb"], B["fb"] + fbcount(rnd, st, tti, JH - 1))
                    v.tensor_tensor(b["Xs"][:, tk * D:(tk + 1) * D],
                                    arena[:, tt * 1024:(tt + 1) * 1024],
                                    b["Xs"][:, tk * D:(tk + 1) * D],
                                    ALU.add).then_inc(S["ye"], 1)

    def tensor(t):
        NT = tok // 128
        for nt in range(NT):
            t.wait_ge(S["ln_xlnT"], C["ln_xlnT"] - NT + nt + 1)
            if nt == 0 and B["g"] > 0:
                t.wait_ge(S["g"], B["g"])
            for kc in range(KC):
                ins = t.matmul(arena[:, 1024 + nt * ne:1024 + (nt + 1) * ne],
                               b["XlnT"][:, kc * tok + nt * 128: kc * tok + (nt + 1) * 128],
                               b["wgs"][:, kc * ne:(kc + 1) * ne],
                               start=(kc == 0), stop=(kc == KC - 1))
            ins.then_inc(S["z"], 1)
        cwtb = arena[:, 1536:1536 + tok // 2].bitcast(BF16)

        def aphase(rnd, st, jr=None):
            gidx_base = rnd * NG
            prevs = rnd * NST + st - 1
            for j in (range(JH) if jr is None else jr):
                q = j // GT
                gidx = gidx_base + q
                t.wait_ge(S[f"w1_{gidx % RING}"], B[f"w1_{gidx % RING}"] + 16 * (gidx // RING + 1))
                idx = hcount(rnd, st, j)
                if j < 2 and prevs >= 0:
                    t.wait_ge(S["ye"], B["ye"] + prevs * NTT + 1)
                if idx > 2:
                    t.wait_ge(S["gel"], B["gel"] + idx - 2)
                elif B["gel"] > 0 and idx == 1:
                    t.wait_ge(S["gel"], B["gel"])
                slot = gidx % RING
                w1t = w1ring[:, slot * GT * D + (j % GT) * D: slot * GT * D + (j % GT + 1) * D]
                for kc in range(KC):
                    ins = t.matmul(hbank(j), w1t[:, kc * 128:(kc + 1) * 128],
                                   b["XlnT"][:, kc * tok + st * STW: kc * tok + (st + 1) * STW],
                                   start=(kc == 0), stop=(kc == KC - 1))
                ins.then_inc(S["h"], 1)

        PRE = min(JH, 4)
        aphase(0, 0, range(PRE))
        for nt in range(NT):
            t.wait_ge(S["cw"], B["cw"] + nt + 1)
            if nt == 0 and B["cwts"] > 0:
                t.wait_ge(S["cwts"], B["cwts"])
            ins = t.transpose(cwtb[0:ne, nt * 128:(nt + 1) * 128],
                              b["cwb_all"][:, nt * ne:(nt + 1) * ne], b["ident"][:, :])
            ins.then_inc(S["cwt"], 1)
        t.wait_ge(S["g"], B["g"] + 1)
        t.wait_ge(S["cwts"], B["cwts"] + 1)
        for rnd in range(NRND):
            e, jh = divmod(rnd, NJH)
            for st in range(NST):
                gidx_base = rnd * NG
                prevs = rnd * NST + st - 1
                if prevs >= 0:
                    t.wait_ge(S["ye"], B["ye"] + prevs * NTT + 2)
                t.wait_ge(S["cwbc"], B["cwbc"] + rnd * NST + st)
                ins = t.matmul(arena[:, 2560:2560 + STW], b["sel"][0:ne, e * 128:(e + 1) * 128],
                               b["cwTs"][0:ne, st * STW:(st + 1) * STW], start=True, stop=True)
                ins.then_inc(S["selm"], 1)
                if (rnd, st) != (0, 0):
                    aphase(rnd, st)
                else:
                    aphase(0, 0, range(PRE, JH))
                for tti in range(NTT):
                    tt = NTT - 1 - tti
                    for j in range(JH):
                        q = j // GT
                        gidx = gidx_base + q
                        t.wait_ge(S[f"w2_{gidx % RING}"], B[f"w2_{gidx % RING}"] + 16 * (gidx // RING + 1))
                        t.wait_ge(S["cwm"], B["cwm"] + hcount(rnd, st, j))
                        if j == 0:
                            if tt == NTT - 1:
                                t.wait_ge(S["gel"], B["gel"] + hcount(rnd, st, JH - 1))
                            if prevs >= 0:
                                t.wait_ge(S["ye"], B["ye"] + prevs * NTT + tti + 1)
                            if tt == 2 or NTT == 1:
                                t.wait_ge(S["cwbc"], B["cwbc"] + rnd * NST + st + 1)
                        slot = gidx % RING
                        w2t = w2ring[:, slot * GT * D + (j % GT) * D: slot * GT * D + (j % GT + 1) * D]
                        first = (j == 0)
                        last = (j == JH - 1)
                        if first and rnd == 0:
                            t.matmul(arena[:, tt * 1024: tt * 1024 + 512],
                                     b["cwTs"][0:ne, (st * NTT + tt) * 128:(st * NTT + tt + 1) * 128],
                                     b["b2s"][0:ne, 0:512], start=True, stop=False)
                            t.matmul(arena[:, tt * 1024 + 512:(tt + 1) * 1024],
                                     b["cwTs"][0:ne, (st * NTT + tt) * 128:(st * NTT + tt + 1) * 128],
                                     b["b2s"][0:ne, 512:1024], start=True, stop=False)
                        hst = Hss[:, j * STW + tt * 128: j * STW + (tt + 1) * 128]
                        st0 = first and rnd > 0
                        t.matmul(arena[:, tt * 1024: tt * 1024 + 512], hst,
                                 w2t[:, 0:512], start=st0, stop=last)
                        ins = t.matmul(arena[:, tt * 1024 + 512:(tt + 1) * 1024], hst,
                                       w2t[:, 512:1024], start=st0, stop=last)
                        ins.then_inc(S["fb"], 1)

    tot = dict(z=tok // 128, g=1, cw=tok // 128, cwt=tok // 128, cwts=1,
               h=NRND * NST * JH, gel=NRND * NST * JH, cwm=NRND * NST * JH,
               selm=NRND * NST, cwbc=NRND * NST, fb=NRND * NST * NTT * JH,
               ye=NRND * NST * NTT)
    for i in range(RING):
        tot[f"w1_{i}"] = 16 * (NRND * NG // RING)
        tot[f"w2_{i}"] = 16 * (NRND * NG // RING)
    for k, dv in tot.items():
        C[k] = B[k] + dv
    return {"gpsimd": gpsimd, "scalar": scalar, "vector": vector, "tensor": tensor}


def emit_ffn1(ctx, guard=None):
    b = ctx.buf
    S = ctx.sems
    C = ctx.cnt
    tok = ctx.tok
    ne, hh = ctx.ne1, 128
    NT = tok // 128
    NST = max(tok // 512, 1)
    STW = tok // NST
    NTT = STW // 128
    sc = b["scratch"]
    keys = sc[:, 0:16384]
    vals = sc[:, 16384:32768]
    Ssall = sc[:, 32768:40960]
    Sg = sc[:, 40960:41984]
    selbs = sc[:, 41984:43008]
    arena = ctx.arena
    names = ("z", "g", "cw", "cwt", "cwts", "h", "gel", "cwm", "selm", "cwbc",
             "fb", "ye", "w1_0", "w1_1", "w1_2")
    B = {k: C.get(k, 0) for k in names}

    def secount(st, e):
        return st * ne + e + 1

    def gpsimd(g):
        if guard is not None:
            for (gs_, gv_) in guard:
                g.wait_ge(S[gs_], gv_)
        g.dma_start(out=keys[:, 0:ne * KC * hh].rearrange("p (e m) -> p e m", m=KC * hh),
                    in_=b["keyr"].rearrange("e p m -> p e m")).then_inc(S["w1_0"], 16)
        g.dma_start(out=vals[:, 0:ne * D].rearrange("p (e d) -> p e d", d=D),
                    in_=b["valr"].rearrange("e p d -> p e d")).then_inc(S["w1_1"], 16)

    def scalar(s):
        s.wait_ge(S["z"], B["z"] + NT)
        s.activation(out=b["gsm"][:, 0:NT * ne], in_=arena[:, 1024:1024 + NT * ne],
                     func=AF.Sigmoid).then_inc(S["g"], 1)
        for st in range(NST):
            for e in range(ne):
                idx = secount(st, e)
                s.wait_ge(S["h"], B["h"] + idx)
                if idx > 2:
                    s.wait_ge(S["cwm"], B["cwm"] + idx - 2)
                elif B["cwm"] > 0:
                    s.wait_ge(S["cwm"], B["cwm"])
                sbk = arena[:, 3072 + (e % 2) * 512: 3072 + (e % 2) * 512 + STW]
                s.activation(out=Sg[:, (e % 2) * STW:(e % 2) * STW + STW],
                             in_=sbk, func=AF.Relu).then_inc(S["gel"], 1)

    def vector(v):
        v.wait_ge(S["g"], B["g"] + 1)
        ctx.vstep(v, v.tensor_scalar_mul(b["cwb_all"][:, 0:NT * ne], b["gsm"][:, 0:NT * ne], 0.5))
        ctx.ffn1_sgb_mark = ctx.cnt["vch"]
        v.wait_ge(S["cwt"], B["cwt"] + NT)
        v.tensor_copy(b["cwTs"][0:ne, 0:tok],
                      arena[:, 1536:1536 + tok // 2].bitcast(BF16)[0:ne, :]).then_inc(S["cwts"], 1)
        for st in range(NST):
            for e in range(ne):
                idx = secount(st, e)
                v.wait_ge(S["selm"], B["selm"] + idx)
                v.tensor_copy(selbs[:, (e % 2) * STW:(e % 2) * STW + STW],
                              arena[:, 1024:1024 + STW]).then_inc(S["cwbc"], 1)
                v.wait_ge(S["cwbc"], B["cwbc"] + idx)
                v.wait_ge(S["gel"], B["gel"] + idx)
                if st > 0:
                    v.wait_ge(S["fb"], B["fb"] + st)
                elif B["fb"] > 0:
                    v.wait_ge(S["fb"], B["fb"])
                v.tensor_tensor(Ssall[:, e * STW:(e + 1) * STW],
                                Sg[:, (e % 2) * STW:(e % 2) * STW + STW],
                                selbs[:, (e % 2) * STW:(e % 2) * STW + STW],
                                ALU.mult).then_inc(S["cwm"], 1)
            v.wait_ge(S["fb"], B["fb"] + st + 1)
            v.wait_ge(S["ye"], B["ye"] + st * NTT)
            for t in range(NTT):
                tk = st * NTT + t
                v.tensor_tensor(b["Xs"][:, tk * D:(tk + 1) * D],
                                arena[:, t * 1024:(t + 1) * 1024],
                                b["Xs"][:, tk * D:(tk + 1) * D],
                                ALU.add).then_inc(S["ye"], 1)

    def tensor(t):
        t.wait_ge(S["w1_0"], B["w1_0"] + 16)
        t.wait_ge(S["w1_1"], B["w1_1"] + 16)
        for nt in range(NT):
            t.wait_ge(S["ln_xlnT"], C["ln_xlnT"] - NT + nt + 1)
            if nt == 0 and B["g"] > 0:
                t.wait_ge(S["g"], B["g"])
            for kc in range(KC):
                ins = t.matmul(arena[:, 1024 + nt * ne:1024 + (nt + 1) * ne],
                               b["XlnT"][:, kc * tok + nt * 128: kc * tok + (nt + 1) * 128],
                               b["esels"][:, kc * ne:(kc + 1) * ne],
                               start=(kc == 0), stop=(kc == KC - 1))
            ins.then_inc(S["z"], 1)
        seltb = arena[:, 1536:1536 + tok // 2].bitcast(BF16)
        t.wait_ge(S["vch"], ctx.ffn1_sgb_mark)
        for nt in range(NT):
            if nt == 0 and B["cwts"] > 0:
                t.wait_ge(S["cwts"], B["cwts"])
            ins = t.transpose(seltb[0:ne, nt * 128:(nt + 1) * 128],
                              b["cwb_all"][:, nt * ne:(nt + 1) * ne], b["ident"][:, :])
            ins.then_inc(S["cwt"], 1)
        t.wait_ge(S["cwts"], B["cwts"] + 1)
        for st in range(NST):
            if st > 0:
                t.wait_ge(S["ye"], B["ye"] + st * NTT)
            elif B["ye"] > 0:
                t.wait_ge(S["ye"], B["ye"])
            for e in range(ne):
                idx = secount(st, e)
                if idx > 1:
                    t.wait_ge(S["cwbc"], B["cwbc"] + idx - 1)
                ins = t.matmul(arena[:, 1024:1024 + STW], b["sel16"][0:ne, e * 128:(e + 1) * 128],
                               b["cwTs"][0:ne, st * STW:(st + 1) * STW], start=True, stop=True)
                ins.then_inc(S["selm"], 1)
                if idx > 2:
                    t.wait_ge(S["gel"], B["gel"] + idx - 2)
                elif B["gel"] > 0 and idx == 1:
                    t.wait_ge(S["gel"], B["gel"])
                sbk = arena[:, 3072 + (e % 2) * 512: 3072 + (e % 2) * 512 + STW]
                for kc in range(KC):
                    ins = t.matmul(sbk, keys[:, e * KC * hh + kc * hh: e * KC * hh + (kc + 1) * hh],
                                   b["XlnT"][:, kc * tok + st * STW: kc * tok + (st + 1) * STW],
                                   start=(kc == 0), stop=(kc == KC - 1))
                ins.then_inc(S["h"], 1)
            t.wait_ge(S["cwbc"], B["cwbc"] + (st + 1) * ne)
            for e in range(ne):
                t.wait_ge(S["cwm"], B["cwm"] + secount(st, e))
                if e == 0:
                    t.wait_ge(S["gel"], B["gel"] + secount(st, ne - 1))
                first = (e == 0)
                last = (e == ne - 1)
                for tt in range(NTT):
                    hst = Ssall[:, e * STW + tt * 128: e * STW + (tt + 1) * 128]
                    t.matmul(arena[:, tt * 1024: tt * 1024 + 512], hst,
                             vals[:, e * D: e * D + 512], start=first, stop=last)
                    ins = t.matmul(arena[:, tt * 1024 + 512:(tt + 1) * 1024], hst,
                                   vals[:, e * D + 512:(e + 1) * D], start=first, stop=last)
            ins.then_inc(S["fb"], 1)

    tot = dict(z=NT, g=1, cwt=NT, cwts=1, h=NST * ne, gel=NST * ne, cwm=NST * ne,
               selm=NST * ne, cwbc=NST * ne, fb=NST, ye=NST * NTT,
               w1_0=16, w1_1=16)
    for k, dv in tot.items():
        C[k] = B[k] + dv
    return {"gpsimd": gpsimd, "scalar": scalar, "vector": vector, "tensor": tensor}


def emit_attn(ctx, layer, groups, guard):
    b = ctx.buf
    S = ctx.sems
    C = ctx.cnt
    l1 = (layer == 1)
    T = 2 * ctx.tok
    NTB = T // 128
    NTG = NTB // 4
    NQT = T // 512
    NP = 4
    sc = b["scratch"]
    wqbuf = sc[:, 0:6144]
    xring = sc[:, 6144:10240]
    QT = sc[:, 10240:14336]
    KT = sc[:, 14336:18432]
    Vaug = sc[:, 18432:22592]
    qrot = sc[:, 22592:24640]
    Pp = sc[:, 24896:26944]
    otmp = sc[:, 26944:27968]
    rb = sc[:, 27968:29024]
    zbA = sc[:, 26944:27968].bitcast(F32)[64:65, 0:512]
    zbB = sc[:, 27968:29024].bitcast(F32)[64:65, 0:512]
    wos = sc[:, 29024:33120]
    ybuf = sc[:, 33120:37216]
    tsc = [sc[:, 37216 + i * 256: 37216 + (i + 1) * 256] for i in range(4)]
    Ob = b["XlnT"]
    arena = ctx.arena
    SCC = 0.125

    names = ("ago_0", "ago_1", "cc", "wq_0", "wq_1", "wq_w", "xr_0", "xr_1", "qk", "rot", "qt", "qts",
             "S", "exp", "msk", "pv", "nrm", "bc", "on", "rbc", "ore_0", "ore_1",
             "kd", "pj", "yp", "yb", "yr_0", "yr_1", "ya", "yv", "ln_xlnT", "gch")
    B = {k: C.get(k, 0) for k in names}

    def kvw(p):
        return (128, 128) if not l1 else ((64, 64) if p == 0 else (0, 0))

    def pw(p):
        return 384 if not l1 else (256 if p == 0 else 128)

    nrot, nqt, nqts = [], [], []
    for p in range(NP):
        kw, vw = kvw(p)
        for tg in range(NTG):
            nrot.append(1 + (1 if kw else 0) + (1 if vw else 0))
            nqt.append(1 + (1 if kw else 0))
            nqts.append(1 + (1 if kw else 0))
    def pref(a, i):
        return sum(a[:i + 1])
    ROT_T, QT_T, QTS_T = sum(nrot), sum(nqt), sum(nqts)

    pq_list = [(p, qt) for p in range(NP) for qt in range(NQT)]

    def pqi(p, qt):
        return p * NQT + qt

    assert NQT % 2 == 0
    SCHED = []
    ug_ = 0
    dcum_ = 0
    for p in range(NP):
        for qtp in range(NQT // 2):
            qa, qb = 2 * qtp, 2 * qtp + 1
            na, nb = 4 * qa + 4, 4 * qb + 4
            order = []
            for k in range(nb):
                if k < na:
                    order.append((qa, k))
                order.append((qb, k))
            ents = []
            lastpar = {0: 0, 1: 0}
            for idx, (qt_, kb_) in enumerate(order):
                diag = kb_ >= 4 * qt_
                f2par = (1 if ug_ % 2 == 0 else 2) if idx < 2 else 0
                ents.append(("u", p, qt_, kb_, ug_, dcum_, f2par, diag))
                lastpar[ug_ % 2] = ug_
                ug_ += 1
                if diag:
                    dcum_ += 1
            lastA = max(i for i, e in enumerate(ents) if e[2] == qa)
            ents.insert(lastA + 1, ("nv", p, qa))
            ents.append(("nv", p, qb))
            ents.append(("nf", p, qa))
            ents.append(("nf", p, qb))
            ents.append(("nt", p, qa, lastpar[0] + 1))
            ents.append(("nt", p, qb, lastpar[1] + 1))
            SCHED.extend(ents)
    SCHED2 = []
    pend_nt = []
    ucount_since = 0
    for e in SCHED:
        if e[0] == "nt":
            pend_nt.append(e)
            continue
        if e[0] == "u" and pend_nt and e[1] == pend_nt[0][1]:
            SCHED2.append(e)
            ucount_since += 1
            if ucount_since >= 1:
                SCHED2.extend(pend_nt)
                pend_nt = []
                ucount_since = 0
            continue
        if pend_nt and (e[0] != "u"):
            SCHED2.append(e)
            continue
        if pend_nt:
            SCHED2.extend(pend_nt)
            pend_nt = []
            ucount_since = 0
        SCHED2.append(e)
    SCHED2.extend(pend_nt)
    SCHED = SCHED2
    NU = ug_
    ND = dcum_

    def uend(p):
        c = 0
        for e in SCHED:
            if e[0] == "u" and e[1] <= p:
                c += 1
        return c

    pv_at_norm = {}
    _c = 0
    for e in SCHED:
        if e[0] == "u":
            _c += 1
            pv_at_norm[(e[1], e[2])] = _c

    NTBH = ctx.tok // 256
    def sync(sy):
        for tb_ in range(NTBH):
            sy.wait_ge(S["ln_xlnT"], C["ln_xlnT"] - (ctx.tok // 128) + 2 * tb_ + 2)
            agd = b["ag_in_a"] if tb_ < NTBH // 2 else b["ag_in_b"]
            tbl = tb_ if tb_ < NTBH // 2 else tb_ - NTBH // 2
            sy.dma_start(
                out=agd[:, tbl * 2048:(tbl + 1) * 2048],
                in_=b["XlnT"][:, :].rearrange("p (kc tb dt) -> p tb kc dt",
                                              tb=NTBH, dt=256)[:, tb_]
            ).then_inc(S[f"ago_{0 if tb_ < NTBH // 2 else 1}"], 16)
        NG2 = NTB // 2
        for p in range(NP):
            for g2l in range(NG2):
                g2 = p * NG2 + g2l
                nt0 = g2l * 2
                half = (nt0 * 128) // ctx.tok
                tc = (nt0 * 128) % ctx.tok
                ccw = 1 if (tc // 256) < NTBH // 2 else 2
                sy.wait_ge(S["cc"], B["cc"] + ccw)
                if g2 > 1:
                    sy.wait_ge(S["qk"], B["qk"] + 2 * (g2 - 1))
                slot = g2 % 2
                tb = tc // 256
                ago_src = b["ag_out_a"] if tb < NTBH // 2 else b["ag_out_b"]
                tbl2 = tb if tb < NTBH // 2 else tb - NTBH // 2
                sy.dma_start(
                    out=xring[:, slot * 2048:(slot + 1) * 2048],
                    in_=ago_src[half * 128:(half + 1) * 128,
                                tbl2 * 2048:(tbl2 + 1) * 2048]).then_inc(S[f"xr_{slot}"], 16)
            if l1 and p == 0:
                sy.wait_ge(S["qts"], B["qts"] + pref(nqts, NTG - 1))
                sy.dma_start(out=KT[64:128, 0:T], in_=KT[0:64, 0:T]).then_inc(S["kd"], 16)
        NTS = T // 512
        for dch in range(KC):
            sy.wait_ge(S["yp"], B["yp"] + NTS * (dch + 1))
            rst = b["rs_in_a"] if dch < KC // 2 else b["rs_in_b"]
            dl = dch % (KC // 2)
            for r in range(2):
                sy.dma_start(
                    out=rst[r * (D // 2) + dl * 128: r * (D // 2) + (dl + 1) * 128, :],
                    in_=ybuf[:, r * ctx.tok:(r + 1) * ctx.tok]).then_inc(S["yb"], 16)
        qtk = ctx.tok // 2
        for hs in range(2):
            sy.wait_ge(S["cc"], B["cc"] + 3 + hs)
            rso = b["rs_out_a"] if hs == 0 else b["rs_out_b"]
            for q_ in range(2):
                sy.dma_start(
                    out=sc[:, hs * (KC // 2) * ctx.tok:(hs + 1) * (KC // 2) * ctx.tok]
                        .rearrange("p (kc t) -> p kc t", t=ctx.tok)[:, :, q_ * qtk:(q_ + 1) * qtk],
                    in_=rso.rearrange("(kc p) t -> p kc t", p=128)[:, :, q_ * qtk:(q_ + 1) * qtk]
                ).then_inc(S[f"yr_{q_}"], 16)

    def gpsimd(g):
        for (gs, gv) in guard:
            g.wait_ge(S[gs], gv)
        g.wait_ge(S["ln_xlnT"], C["ln_xlnT"])
        ctx.gstep(g, g.memset(
            Vaug.rearrange("p (bl h c2) -> p bl h c2", h=2, c2=65)[:, :, :, 64:65], 1.0))
        for bb in range(4):
            ctx.gstep(g, g.memset(b["masks"][:, bb * 512:(bb + 1) * 512], 0.0))
            ctx.gstep(g, g.affine_select(
                out=b["masks"][:, bb * 512:(bb + 1) * 512],
                in_=b["masks"][:, bb * 512:(bb + 1) * 512],
                compare_op=ALU.is_ge, fill=-1.0e9, base=-128 * bb,
                pattern=[[1, 512]], channel_multiplier=-1))
        ctx.attn_gch_mark = ctx.cnt["gch"]
        g.dma_start(out=wos, in_=b["wor0" if not l1 else "wor1"]).then_inc(S["wq_w"], 16)
        if l1:
            g.dma_start(out=b["bqs"][0:1, 0:1024], in_=b["bqr"]).then_inc(S["wq_w"], 16)
        g.wait_ge(S["ago_0"], B["ago_0"] + 16 * (NTBH // 2))
        g.collective_compute("AllGather", ALU.bypass, replica_groups=groups,
                             ins=[b["ag_in_ta"].ap().opt()],
                             outs=[b["ag_out_ta"].ap().opt()]).then_inc(S["cc"], 1)
        g.wait_ge(S["ago_1"], B["ago_1"] + 16 * (NTBH - NTBH // 2))
        g.collective_compute("AllGather", ALU.bypass, replica_groups=groups,
                             ins=[b["ag_in_tb"].ap().opt()],
                             outs=[b["ag_out_tb"].ap().opt()]).then_inc(S["cc"], 1)
        def reloc(p_):
            for qt_ in range(NQT):
                u = pqi(p_, qt_)
                g.wait_ge(S["on"], B["on"] + 2 * u + 2)
                g.dma_start(out=Ob[64:128, p_ * T + qt_ * 512: p_ * T + (qt_ + 1) * 512],
                            in_=otmp[0:64, (u % 2) * 512:(u % 2) * 512 + 512]).then_inc(S[f"ore_{u % 2}"], 16)
        for p in range(NP):
            slot = p % 2
            if p >= 2:
                g.wait_ge(S["qk"], B["qk"] + 4 * NTG * (p - 1))
            g.dma_start(out=wqbuf[:, slot * 3072: slot * 3072 + KC * pw(p)],
                        in_=b[f"wqp{layer}_{p}"]).then_inc(S[f"wq_{slot}"], 16)
            if p >= 1:
                reloc(p - 1)
        reloc(NP - 1)
        g.wait_ge(S["yb"], B["yb"] + 16 * (KC // 2) * 2)
        g.collective_compute("ReduceScatter", ALU.add, replica_groups=groups,
                             ins=[b["rs_in_ta"].ap().opt()],
                             outs=[b["rs_out_ta"].ap().opt()]).then_inc(S["cc"], 1)
        g.wait_ge(S["yb"], B["yb"] + 16 * KC * 2)
        g.collective_compute("ReduceScatter", ALU.add, replica_groups=groups,
                             ins=[b["rs_in_tb"].ap().opt()],
                             outs=[b["rs_out_tb"].ap().opt()]).then_inc(S["cc"], 1)

    def scalar(s):
        for e in SCHED:
            if e[0] != "u":
                continue
            (_, p, qt, kb, ug, dcum, f2, diag) = e
            u = ug + 1
            s.wait_ge(S["S"], B["S"] + u)
            if u > 2:
                s.wait_ge(S["pv"], B["pv"] + u - 2)
            elif B["pv"] > 0:
                s.wait_ge(S["pv"], B["pv"])
            ub = ug % 2
            s.activation(out=Pp[:, ub * 1024:(ub + 1) * 1024],
                         in_=arena[:, ub * 1024:(ub + 1) * 1024],
                         func=AF.Exp, scale=SCC).then_inc(S["exp"], 1)

    def vector(v):
        for p in range(NP):
            kw, vw = kvw(p)
            for tg in range(NTG):
                gi = p * NTG + tg
                v.wait_ge(S["qk"], B["qk"] + 4 * (gi + 1))
                nr = 0
                for which in range(2):
                    w64 = 128 if which == 0 else kw
                    if w64 == 0:
                        continue
                    if gi >= 2:
                        v.wait_ge(S["qt"], B["qt"] + pref(nqt, gi - 3) + which + 1)
                    nh2 = w64 // 64
                    base_c = 0 if which == 0 else 128
                    pr = arena[:, 0:2048].rearrange("p (tl c) -> p tl c", c=512)[:, :, base_c:base_c + w64] \
                        .rearrange("p tl (h c) -> p tl h c", c=64)
                    ev = pr[:, :, :, 0:32]
                    od = pr[:, :, :, 32:64]
                    ctb = b["crep"][:, tg * 256:(tg + 1) * 256].rearrange("p (tl r c) -> p tl r c", r=2, c=32)[:, :, 0:nh2, :]
                    stb = b["srep"][:, tg * 256:(tg + 1) * 256].rearrange("p (tl r c) -> p tl r c", r=2, c=32)[:, :, 0:nh2, :]
                    t1 = tsc[0].rearrange("p (tl h c) -> p tl h c", h=2, c=32)[:, :, 0:nh2, :]
                    t2 = tsc[1].rearrange("p (tl h c) -> p tl h c", h=2, c=32)[:, :, 0:nh2, :]
                    t3 = tsc[2].rearrange("p (tl h c) -> p tl h c", h=2, c=32)[:, :, 0:nh2, :]
                    t4 = tsc[3].rearrange("p (tl h c) -> p tl h c", h=2, c=32)[:, :, 0:nh2, :]
                    qr = qrot[:, (gi % 2) * 1024 + which * 512: (gi % 2) * 1024 + which * 512 + 512] \
                        .rearrange("p (tl h c) -> p tl h c", h=2, c=64)[:, :, 0:nh2, :]
                    ctx.vstep(v, v.tensor_tensor(t1, ev, ctb, ALU.mult))
                    ctx.vstep(v, v.tensor_tensor(t2, od, stb, ALU.mult))
                    ctx.vstep(v, v.tensor_tensor(t3, od, ctb, ALU.mult))
                    ctx.vstep(v, v.tensor_tensor(t4, ev, stb, ALU.mult))
                    if not l1:
                        ctx.vstep(v, v.tensor_sub(qr[:, :, :, 0:32], t1, t2))
                        ins = v.tensor_add(qr[:, :, :, 32:64], t3, t4)
                    else:
                        ctx.vstep(v, v.tensor_add(qr[:, :, :, 0:32], t1, t2))
                        ins = v.tensor_sub(qr[:, :, :, 32:64], t4, t3)
                    ins.then_inc(S["rot"], 1)
                    nr += 1
                if vw > 0:
                    if p > 0 or B["pv"] > 0:
                        v.wait_ge(S["pv"], B["pv"] + (uend(p - 1) if p > 0 else 0))
                    nhv = 1 if l1 else 2
                    vbase = 128 + kw
                    vsrc = arena[:, 0:2048].rearrange("p (tl c) -> p tl c", c=512)[:, :, vbase:vbase + vw] \
                        .rearrange("p tl (h c) -> p tl h c", c=64)
                    vdst = Vaug[:, tg * 4 * 130:(tg + 1) * 4 * 130].rearrange("p (tl x) -> p tl x", x=130) \
                        .rearrange("p tl (h c) -> p tl h c", c=65)[:, :, 0:nhv, 0:64]
                    v.tensor_copy(vdst, vsrc).then_inc(S["rot"], 1)
                for which in range(2):
                    w64 = 128 if which == 0 else kw
                    if w64 == 0:
                        continue
                    v.wait_ge(S["qt"], B["qt"] + pref(nqt, gi - 1) + which + 1)
                    if p > 0 and tg == 0:
                        v.wait_ge(S["S"], B["S"] + uend(p - 1))
                    bkq = arena[:, (4 + which) * 512:(4 + which) * 512 + 512].bitcast(BF16)
                    dst = (QT if which == 0 else KT)
                    nrow = 128 if (which == 0 or not l1) else 64
                    v.tensor_copy(
                        dst[0:nrow, tg * 512:(tg + 1) * 512].rearrange("p (tl c) -> p tl c", c=128),
                        bkq[0:nrow, 0:512].rearrange("p (tl c) -> p tl c", c=128)).then_inc(S["qts"], 1)
            for e in SCHED:
                if e[1] != p:
                    continue
                if e[0] == "u":
                    continue
                elif e[0] == "nv":
                    qt = e[2]
                    pq = pqi(p, qt)
                    ob_ = 2048 + (qt % 2) * 1024
                    v.wait_ge(S["pv"], B["pv"] + pv_at_norm[(p, qt)])
                    if pq == 0:
                        v.wait_ge(S["ago_0"], B["ago_0"] + 16 * (NTBH // 2))
                        v.wait_ge(S["ago_1"], B["ago_1"] + 16 * (NTBH - NTBH // 2))
                    if pq > 0:
                        v.wait_ge(S["bc"], B["bc"] + pq)
                    with ctx.nc.allow_low_precision(reason="recip f32->bf16 store, same as f32 recip then cast"):
                        v.reciprocal(b["rzb"][0:1, 0:1024],
                                     arena[64:65, ob_:ob_ + 1024]).then_inc(S["nrm"], 1)
                elif e[0] == "nf":
                    qt = e[2]
                    pq = pqi(p, qt)
                    strm = qt % 2
                    ob_ = 2048 + strm * 1024
                    rbrow = strm * 64
                    v.wait_ge(S["bc"], B["bc"] + pq + 1)
                    v.tensor_copy(rb[rbrow:rbrow + 64, 0:1024],
                                  arena[64:128, ob_:ob_ + 1024]).then_inc(S["rbc"], 1)
                    v.wait_ge(S["rbc"], B["rbc"] + pq + 1)
                    v.tensor_tensor(Ob[0:64, p * T + qt * 512: p * T + (qt + 1) * 512],
                                    arena[0:64, ob_:ob_ + 512], rb[rbrow:rbrow + 64, 0:512],
                                    ALU.mult).then_inc(S["on"], 1)
                    if pq >= 2:
                        v.wait_ge(S[f"ore_{pq % 2}"], B[f"ore_{pq % 2}"] + 16 * (pq // 2))
                    v.tensor_tensor(otmp[0:64, (pq % 2) * 512:(pq % 2) * 512 + 512],
                                    arena[0:64, ob_ + 512:ob_ + 1024], rb[rbrow:rbrow + 64, 512:1024],
                                    ALU.mult).then_inc(S["on"], 1)
        NTS = T // 512
        for dch in range(KC):
            for tst in range(NTS):
                idx = dch * NTS + tst + 1
                v.wait_ge(S["pj"], B["pj"] + idx)
                if dch >= 1 and tst == 0:
                    v.wait_ge(S["yb"], B["yb"] + 32 * dch)
                pb = arena[:, ((idx - 1) % 4) * 512:((idx - 1) % 4) * 512 + 512]
                if l1:
                    ins = v.tensor_scalar(ybuf[:, tst * 512:(tst + 1) * 512], pb,
                                          b["bos2"][:, dch:dch + 1], None, ALU.add)
                else:
                    ins = v.tensor_copy(ybuf[:, tst * 512:(tst + 1) * 512], pb)
                ins.then_inc(S["yp"], 1)
        for nt in range(ctx.tok // 128):
            v.wait_ge(S["ya"], B["ya"] + nt + 1)
            pbk = arena[:, (nt % 2) * 512:(nt % 2) * 512 + 512].bitcast(BF16)
            v.tensor_tensor(b["Xs"][:, nt * D:(nt + 1) * D],
                            pbk[:, 0:1024],
                            b["Xs"][:, nt * D:(nt + 1) * D], ALU.add).then_inc(S["yv"], 1)

    def tensor(t):
        t.wait_ge(S["gch"], ctx.attn_gch_mark)
        t.wait_ge(S["wq_w"], B["wq_w"] + (32 if l1 else 16))
        for p in range(NP):
            kw, vw = kvw(p)
            slot = p % 2
            t.wait_ge(S[f"wq_{slot}"], B[f"wq_{slot}"] + 16 * (p // 2 + 1))
            for tg in range(NTG):
                gi = p * NTG + tg
                if gi > 0:
                    t.wait_ge(S["rot"], B["rot"] + pref(nrot, gi - 1))
                for sub in range(2):
                    g2 = gi * 2 + sub
                    xslot = g2 % 2
                    t.wait_ge(S[f"xr_{xslot}"], B[f"xr_{xslot}"] + 16 * (g2 // 2 + 1))
                    for tl2 in range(2):
                        tl = sub * 2 + tl2
                        for kc in range(KC):
                            ins = t.matmul(arena[:, tl * 512: tl * 512 + pw(p)],
                                           xring[:, xslot * 2048 + kc * 256 + tl2 * 128: xslot * 2048 + kc * 256 + (tl2 + 1) * 128],
                                           wqbuf[:, slot * 3072 + kc * pw(p): slot * 3072 + (kc + 1) * pw(p)],
                                           start=(kc == 0), stop=(kc == KC - 1 and not l1))
                        if l1:
                            ins = t.matmul(arena[:, tl * 512: tl * 512 + pw(p)],
                                           b["onesc"][0:1, 0:128],
                                           b["bqs"][0:1, p * 256: p * 256 + pw(p)],
                                           start=False, stop=True)
                        ctx.inc(ins, "qk")
                for which in range(2):
                    w64 = 128 if which == 0 else kw
                    if w64 == 0:
                        continue
                    t.wait_ge(S["rot"], B["rot"] + pref(nrot, gi - 1) + which + 1)
                    if gi >= 1:
                        t.wait_ge(S["qts"], B["qts"] + pref(nqts, gi - 1))
                    if p > 0 and tg == 0 and which == 0:
                        t.wait_ge(S["on"], B["on"] + 2 * p * NQT)
                    nblk = 4
                    bkq = arena[:, (4 + which) * 512:(4 + which) * 512 + 512].bitcast(BF16)
                    for i in range(nblk):
                        ins = t.transpose(bkq[:, i * 128:(i + 1) * 128],
                                          qrot[:, (gi % 2) * 1024 + which * 512 + i * 128:
                                               (gi % 2) * 1024 + which * 512 + (i + 1) * 128],
                                          b["ident"][:, :])
                    ins.then_inc(S["qt"], 1)
            t.wait_ge(S["qts"], B["qts"] + pref(nqts, p * NTG + NTG - 1))
            t.wait_ge(S["rot"], B["rot"] + pref(nrot, p * NTG + NTG - 1))
            if l1:
                t.wait_ge(S["kd"], B["kd"] + 16)
            def emit_S(e):
                (_, _, qt, kb, ug, dcum, f2par, diag) = e
                pq = pqi(p, qt)
                u = ug + 1
                ub = ug % 2
                if u > 2:
                    t.wait_ge(S["exp"], B["exp"] + u - 2)
                elif B["exp"] > 0:
                    t.wait_ge(S["exp"], B["exp"])
                for hh_ in range(2):
                    lo = hh_ * 64
                    ins = t.matmul(arena[:, ub * 1024 + hh_ * 512: ub * 1024 + (hh_ + 1) * 512],
                                   KT[lo:lo + 64, kb * 128:(kb + 1) * 128],
                                   QT[lo:lo + 64, qt * 512:(qt + 1) * 512],
                                   start=True, stop=not diag)
                if diag:
                    bb = kb - 4 * qt
                    for hh_ in range(2):
                        ins = t.matmul(arena[:, ub * 1024 + hh_ * 512: ub * 1024 + (hh_ + 1) * 512],
                                       b["ident"][:, :], b["masks"][:, bb * 512:(bb + 1) * 512],
                                       start=False, stop=True)
                ins.then_inc(S["S"], 1)

            def emit_PV(e):
                (_, _, qt, kb, ug, dcum, f2par, diag) = e
                pq = pqi(p, qt)
                u = ug + 1
                ub = ug % 2
                nkb = 4 * qt + 4
                if kb == 0:
                    if pq >= 2:
                        t.wait_ge(S["on"], B["on"] + 2 * pq - 2)
                        t.wait_ge(S["rbc"], B["rbc"] + pq - 1)
                    else:
                        if B["on"] > 0:
                            t.wait_ge(S["on"], B["on"])
                        if B["rbc"] > 0:
                            t.wait_ge(S["rbc"], B["rbc"])
                t.wait_ge(S["exp"], B["exp"] + u)
                ob_ = 2048 + (qt % 2) * 1024
                for hh_ in range(2):
                    hv = 0 if l1 else hh_
                    ins = t.matmul(arena[:, ob_ + hh_ * 512: ob_ + (hh_ + 1) * 512][0:65, :],
                                   Vaug[:, kb * 130 + hv * 65: kb * 130 + (hv + 1) * 65],
                                   Pp[:, ub * 1024 + hh_ * 512: ub * 1024 + (hh_ + 1) * 512],
                                   start=(kb == 0), stop=(kb == nkb - 1))
                ins.then_inc(S["pv"], 1)

            pend = None
            for e in SCHED:
                if e[1] != p:
                    continue
                if e[0] == "u":
                    emit_S(e)
                    if pend is not None:
                        emit_PV(pend)
                    pend = e
                elif e[0] == "nt":
                    if pend is not None and pqi(p, pend[2]) // 2 == pqi(p, e[2]) // 2:
                        emit_PV(pend)
                        pend = None
                    qt = e[2]
                    pq = pqi(p, qt)
                    bco = 2048 + (qt % 2) * 1024
                    t.wait_ge(S["nrm"], B["nrm"] + pq + 1)
                    t.matmul(arena[64:128, bco:bco + 512], b["onesc"][0:1, 0:64], b["rzb"][0:1, 0:512],
                             start=True, stop=True)
                    ins = t.matmul(arena[64:128, bco + 512:bco + 1024], b["onesc"][0:1, 0:64],
                                   b["rzb"][0:1, 512:1024], start=True, stop=True)
                    ins.then_inc(S["bc"], 1)
            if pend is not None:
                emit_PV(pend)
        npq_ = NP * NQT
        t.wait_ge(S["on"], B["on"] + 2 * npq_)
        t.wait_ge(S["rbc"], B["rbc"] + npq_)
        t.wait_ge(S["ore_0"], B["ore_0"] + 16 * ((npq_ + 1) // 2))
        t.wait_ge(S["ore_1"], B["ore_1"] + 16 * (npq_ // 2))
        NTS = T // 512
        for dch in range(KC):
            for tst in range(NTS):
                idx = dch * NTS + tst + 1
                if idx > 4:
                    t.wait_ge(S["yp"], B["yp"] + idx - 4)
                pb = arena[:, ((idx - 1) % 4) * 512:((idx - 1) % 4) * 512 + 512]
                for cc in range(4):
                    ins = t.matmul(pb,
                                   wos[:, cc * 1024 + dch * 128:cc * 1024 + (dch + 1) * 128],
                                   Ob[:, cc * T + tst * 512: cc * T + (tst + 1) * 512],
                                   start=(cc == 0), stop=(cc == 3))
                ctx.inc(ins, "pj")
        for nt in range(ctx.tok // 128):
            t.wait_ge(S[f"yr_{0 if nt < ctx.tok // 256 else 1}"],
                      B["yr_0" if nt < ctx.tok // 256 else "yr_1"] + 32)
            if nt >= 2:
                t.wait_ge(S["yv"], B["yv"] + nt - 1)
            pbk = arena[:, (nt % 2) * 512:(nt % 2) * 512 + 512].bitcast(BF16)
            for kc in range(KC):
                ins = t.transpose(pbk[:, kc * 128:(kc + 1) * 128],
                                  sc[:, kc * ctx.tok + nt * 128: kc * ctx.tok + (nt + 1) * 128],
                                  b["ident"][:, :])
            ins.then_inc(S["ya"], 1)

    npq = NP * NQT
    NTS = T // 512
    tot = dict(ago_0=16 * (ctx.tok // 512), ago_1=16 * (ctx.tok // 256 - ctx.tok // 512), cc=4, xr_0=16 * NP * NTG, xr_1=16 * NP * NTG,
               qk=4 * NP * NTG, rot=ROT_T, qt=QT_T, qts=QTS_T,
               S=NU, exp=NU, pv=NU, nrm=npq, bc=npq, on=2 * npq,
               rbc=npq, ore_0=16 * ((npq + 1) // 2), ore_1=16 * (npq // 2),
               pj=KC * NTS, yp=KC * NTS, yb=16 * 16, yr_0=32, yr_1=32, ya=ctx.tok // 128, yv=ctx.tok // 128,
               wq_0=16 * 2, wq_1=16 * 2, wq_w=16 * (2 if l1 else 1))
    if l1:
        tot["kd"] = 16
    for k, dv in tot.items():
        C[k] = B[k] + dv
    return {"sync": sync, "gpsimd": gpsimd, "scalar": scalar, "vector": vector, "tensor": tensor}


def build_fused(groups=None, n_cores=8, tok=TOK, ne0=NE, nj0=NJ, ne1=16):
    if groups is None:
        groups = [[2 * i, 2 * i + 1] for i in range(n_cores // 2)]
    T2 = 2 * tok
    gt0 = min(4, nj0 // 2)
    nc = bass.Bass("TRN2", target_bir_lowering=False, debug=False)
    dram = lambda n, s, dt, k: nc.dram_tensor(n, s, dt, kind=k)
    x_in = dram("x_in", [tok, D], F32, "ExternalInput").ap()
    w1r = dram("w1r", [ne0, nj0 // gt0, 128, gt0 * D], F32, "ExternalInput").ap()
    w2r = dram("w2r", [ne0, nj0 // gt0, 128, gt0 * D], F32, "ExternalInput").ap()
    b1r = dram("b1r", [128, ne0 * nj0], F32, "ExternalInput").ap()
    b2r = dram("b2r", [ne0, D], F32, "ExternalInput").ap()
    wgr = dram("wgr", [128, KC * ne0], F32, "ExternalInput").ap()
    keyr = dram("keyr", [ne1, 128, KC * 128], F32, "ExternalInput").ap()
    valr = dram("valr", [ne1, 128, D], F32, "ExternalInput").ap()
    eselr = dram("eselr", [128, KC * ne1], F32, "ExternalInput").ap()
    ctabr = dram("ctabr", [128, T2 // 2], F32, "ExternalInput").ap()
    stabr = dram("stabr", [128, T2 // 2], F32, "ExternalInput").ap()
    wqp = {}
    for p in range(4):
        wqp[(0, p)] = dram(f"wqp0_{p}", [128, KC * 384], F32, "ExternalInput").ap()
        pwl1 = 256 if p == 0 else 128
        wqp[(1, p)] = dram(f"wqp1_{p}", [128, KC * pwl1], F32, "ExternalInput").ap()
    wor0 = dram("wor0", [128, 4 * D], F32, "ExternalInput").ap()
    wor1 = dram("wor1", [128, 4 * D], F32, "ExternalInput").ap()
    bqr = dram("bqr", [1, 1024], F32, "ExternalInput").ap()
    bor = dram("bor", [128, KC], F32, "ExternalInput").ap()
    x_out = dram("x_out", [tok, D], F32, "ExternalOutput").ap()
    ag_in_a = nc.dram_tensor("ag_in_a", [128, KC * tok // 2], BF16)
    ag_in_b = nc.dram_tensor("ag_in_b", [128, KC * tok // 2], BF16)
    ag_out_a = nc.dram_tensor("ag_out_a", [256, KC * tok // 2], BF16)
    ag_out_b = nc.dram_tensor("ag_out_b", [256, KC * tok // 2], BF16)
    rs_in_ta = nc.dram_tensor("rs_in_a", [D, tok], BF16)
    rs_in_tb = nc.dram_tensor("rs_in_b", [D, tok], BF16)
    rs_out_ta = nc.dram_tensor("rs_out_a", [D // 2, tok], BF16)
    rs_out_tb = nc.dram_tensor("rs_out_b", [D // 2, tok], BF16)

    ctx = FCtx(nc)
    ctx.tok, ctx.ne0, ctx.nj0, ctx.ne1 = tok, ne0, nj0, ne1
    stack = ExitStack()
    sb = lambda n, s, dt: stack.enter_context(nc.sbuf_tensor(n, s, dt))
    b = ctx.buf
    b["Xs"] = sb("Xs", [128, (tok // 128) * D], F32)
    b["XlnT"] = sb("XlnT", [128, KC * tok], BF16)
    b["scratch"] = sb("scratch", [128, 43008], BF16)
    b["junk"] = sb("junk", [128, D], BF16)
    b["xlnbf"] = sb("xlnbf", [128, D], BF16)
    for n in ("sums", "sqs", "means", "m2s", "vars_", "stds", "rss", "negmrs"):
        b[n] = sb(n, [128, 16], F32)
    b["gsm"] = sb("gsm", [128, 256], F32)
    b["cwb_all"] = sb("cwb_all", [128, 256], BF16)
    b["gs1"] = sb("gs1", [128, NE], F32)
    b["gs2"] = sb("gs2", [128, NE], F32)
    b["gs3"] = sb("gs3", [128, NE], F32)
    b["r1"] = sb("r1", [128, 1], F32)
    b["r2"] = sb("r2", [128, 1], F32)
    b["epst"] = sb("epst", [128, 1], F32)
    b["cwTs"] = sb("cwTs", [16, tok], BF16)
    b["sel"] = sb("sel", [16, 16 * 128], BF16)
    b["sel16"] = b["sel"]
    b["ident"] = sb("ident", [128, 128], BF16)
    b["onesc"] = sb("onesc", [1, 128], BF16)
    b["masks"] = b["scratch"][:, 38240:40288]
    b["crep"] = sb("crep", [128, T2 // 2], BF16)
    b["srep"] = sb("srep", [128, T2 // 2], BF16)
    b["b1s"] = sb("b1s", [128, ne0 * nj0], F32)
    b["b2s"] = sb("b2s", [ne0, D], BF16)
    b["wgs"] = sb("wgs", [128, KC * ne0], BF16)
    b["esels"] = sb("esels", [128, KC * ne1], BF16)

    b["rzb"] = b["scratch"][:, 40288:41312]
    b["bqs"] = b["scratch"][:, 41312:42336]
    b["bors"] = sb("bors", [128, KC], F32)
    b["bos2"] = sb("bos2", [128, KC], F32)
    b["w1r"], b["w2r"] = w1r, w2r
    b["keyr"], b["valr"] = keyr, valr
    b["ag_in_a"], b["ag_out_a"] = ag_in_a.ap(), ag_out_a.ap()
    b["ag_in_b"], b["ag_out_b"] = ag_in_b.ap(), ag_out_b.ap()
    b["rs_in_a"], b["rs_out_a"] = rs_in_ta.ap(), rs_out_ta.ap()
    b["rs_in_b"], b["rs_out_b"] = rs_in_tb.ap(), rs_out_tb.ap()
    b["ag_in_ta"], b["ag_out_ta"] = ag_in_a, ag_out_a
    b["ag_in_tb"], b["ag_out_tb"] = ag_in_b, ag_out_b
    b["rs_in_ta"], b["rs_out_ta"] = rs_in_ta, rs_out_ta
    b["rs_in_tb"], b["rs_out_tb"] = rs_in_tb, rs_out_tb
    b["wor0"], b["wor1"], b["bqr"] = wor0, wor1, bqr
    for p in range(4):
        b[f"wqp0_{p}"] = wqp[(0, p)]
        b[f"wqp1_{p}"] = wqp[(1, p)]
    ctx.arena = stack.enter_context(nc.psum_tensor("arena", [128, 4096], F32))

    semnames = (["x", "pre", "preh", "gch", "vch",
                 "ln_sq", "ln_sum", "ln_var", "ln_std", "ln_rs", "ln_xln", "ln_tp", "ln_xlnT",
                 "z", "g", "cw", "cwt", "cwts", "h", "gel", "cwm", "selm", "cwbc", "fb", "ye",
                 "ago_0", "ago_1", "cc", "wq_0", "wq_1", "xr_0", "qk", "rot", "qt", "qts",
                 "S", "exp", "msk", "pv", "nrm", "bc", "on", "ore_0", "ore_1",
                 "kd", "pj", "yp", "yb", "yr_0", "yr_1", "ya", "yv", "wq_w", "rbc", "xr_1", "out"]
                + [f"w1_{i}" for i in range(4)] + [f"w2_{i}" for i in range(4)])
    for n in semnames:
        ctx.mksem(n, stack)
    S = ctx.sems
    C = ctx.cnt
    for n in semnames:
        C[n] = 0

    block = stack.enter_context(nc.Block())

    @block.sync
    def _(sy):
        sy.dma_start(out=b["Xs"][:, :].rearrange("p (nt d) -> p nt d", d=D),
                     in_=x_in.rearrange("(nt p) d -> p nt d", p=128)).then_inc(S["x"], 16)
        sy.dma_start(out=b["b1s"][:, :], in_=b1r).then_inc(S["preh"], 16)
        sy.dma_start(out=b["bors"][:, :], in_=bor).then_inc(S["preh"], 16)

    @block.gpsimd
    def _(g):
        g.dma_start(out=b["b2s"][:, :], in_=b2r).then_inc(S["pre"], 16)
        g.dma_start(out=b["wgs"][:, :], in_=wgr).then_inc(S["pre"], 16)
        g.dma_start(out=b["esels"][:, :], in_=eselr).then_inc(S["pre"], 16)
        g.dma_start(out=b["crep"][:, :], in_=ctabr).then_inc(S["pre"], 16)
        g.dma_start(out=b["srep"][:, :], in_=stabr).then_inc(S["pre"], 16)
        ctx.gstep(g, g.memset(b["epst"][:, :], EPS))
        ctx.gstep(g, g.memset(b["ident"][:, :], 0.0))
        ctx.gstep(g, g.affine_select(out=b["ident"][:, :], in_=b["ident"][:, :],
                                     compare_op=ALU.not_equal, fill=1.0, base=0,
                                     pattern=[[-1, 128]], channel_multiplier=1))
        ctx.gstep(g, g.memset(b["onesc"][:, :], 1.0))
        ctx.gstep(g, g.memset(b["sel"][:, :], 0.0))
        ctx.gstep(g, g.affine_select(out=b["sel"][:, :], in_=b["sel"][:, :],
                                     compare_op=ALU.not_equal, fill=1.0, base=0,
                                     pattern=[[-1, 16], [0, 128]], channel_multiplier=1))
    NCONST = ctx.cnt["gch"]

    @block.vector
    def _(v):
        v.wait_ge(S["preh"], 32)
        ctx.vstep(v, v.tensor_scalar_mul(b["bos2"][:, :], b["bors"][:, :], 0.5))

    def xs_ready_initial(eng):
        eng.wait_ge(S["x"], 16)
        eng.wait_ge(S["pre"], 80)
        eng.wait_ge(S["preh"], 32)
        eng.wait_ge(S["gch"], NCONST)

    def mk_xs_ready(semname):
        val = C[semname]
        def f(eng):
            eng.wait_ge(S[semname], val)
        return f

    def mk_noop():
        def f(eng):
            pass
        return f

    ntt_ = tok // 128

    def mk_tile_ffn():
        base = C["ye"]
        def f(eng, nt):
            eng.wait_ge(S["ye"], base - ((ntt_ // 4 - 1 - nt // 4) * 4 + nt % 4))
        return f

    def mk_tile_attn():
        base = C["yv"]
        def f(eng, nt):
            eng.wait_ge(S["yv"], base - (ntt_ - nt - 1))
        return f

    def run(ph):
        for en in ("sync", "gpsimd", "scalar", "vector", "tensor"):
            if en in ph:
                getattr(block, en)(ph[en])

    run(emit_ln_phase(ctx, xs_ready_initial))
    run(emit_ffn0(ctx, 0))
    run(emit_ln_phase(ctx, mk_noop(), xs_tile=mk_tile_ffn()))
    run(emit_attn(ctx, 0, groups, guard=[("ye", C["ye"])]))
    run(emit_ln_phase(ctx, mk_noop(), xs_tile=mk_tile_attn()))
    run(emit_ffn0(ctx, 1, guard=[("ya", C["ya"]), ("yb", C["yb"])]))
    run(emit_ln_phase(ctx, mk_noop(), xs_tile=mk_tile_ffn()))
    run(emit_ffn1(ctx, guard=[("ye", C["ye"])]))
    run(emit_ln_phase(ctx, mk_noop(), xs_tile=mk_tile_ffn()))
    run(emit_attn(ctx, 1, groups, guard=[("ye", C["ye"])]))
    run(emit_ln_phase(ctx, mk_noop(), xs_tile=mk_tile_attn()))
    run(emit_ffn1(ctx, guard=[("ya", C["ya"]), ("yb", C["yb"])]))

    final_ye = C["ye"]

    @block.sync
    def _(sy):
        sy.wait_ge(S["ye"], final_ye)
        sy.dma_start(out=x_out.rearrange("(nt p) d -> p nt d", p=128),
                     in_=b["Xs"][:, :].rearrange("p (nt d) -> p nt d", d=D)).then_inc(S["out"], 16)
        sy.wait_ge(S["out"], 16)

    stack.close()
    return nc


def prep_fused_core(inputs, c, tok=TOK):
    hbase = 8 * (c % 2)
    heads = list(range(hbase, hbase + 8))
    m = {}
    m.update(prep_ffn0(inputs["L0_ffn_W1"], inputs["L0_ffn_W2"], inputs["L0_ffn_b1"],
                       inputs["L0_ffn_b2"], inputs["L0_ffn_w_gate"]))
    f1 = prep_ffn1(inputs["L1_ffn_expert_sel"], inputs["L1_ffn_keys"], inputs["L1_ffn_values"])
    m["keyr"], m["valr"], m["eselr"] = f1["keyr"], f1["valr"], f1["eselr"]
    crep, srep = _rope_tabs(2 * tok, 2)
    m["ctabr"], m["stabr"] = crep, srep
    w_qkv = inputs["L0_attn_w_qkv"]
    perm = _eo_perm()
    for p in range(4):
        h0, h1 = heads[2 * p], heads[2 * p + 1]
        cols = []
        for part in (0, 1):
            for h in (h0, h1):
                cols.append(w_qkv[:, part * D + h * 64: part * D + (h + 1) * 64][:, perm])
        for h in (h0, h1):
            cols.append(w_qkv[:, 2 * D + h * 64: 2 * D + (h + 1) * 64])
        wsl = np.concatenate(cols, axis=1)
        m[f"wqp0_{p}"] = np.ascontiguousarray(
            wsl.reshape(KC, 128, 384).transpose(1, 0, 2).reshape(128, KC * 384))
    Wq, bq = inputs["L1_attn_Wq"], inputs["L1_attn_bq"]
    Wk, bk = inputs["L1_attn_Wk"], inputs["L1_attn_bk"]
    Wv, bv = inputs["L1_attn_Wv"], inputs["L1_attn_bv"]
    bqarr = np.zeros((1, 1024), np.float32)
    for p in range(4):
        h0, h1 = heads[2 * p], heads[2 * p + 1]
        cols = [Wq[:, h0 * 64:(h0 + 1) * 64], Wq[:, h1 * 64:(h1 + 1) * 64]]
        bias = [bq[h0 * 64:(h0 + 1) * 64], bq[h1 * 64:(h1 + 1) * 64]]
        if p == 0:
            cols += [Wk, Wv]
            bias += [bk, bv]
        wsl = np.concatenate(cols, axis=1)
        pwl = wsl.shape[1]
        m[f"wqp1_{p}"] = np.ascontiguousarray(
            wsl.reshape(KC, 128, pwl).transpose(1, 0, 2).reshape(128, KC * pwl))
        bqarr[0, p * 256: p * 256 + pwl] = np.concatenate(bias)
    m["bqr"] = bqarr
    for li, wo in ((0, inputs["L0_attn_w_out"]), (1, inputs["L1_attn_Wo"])):
        sl = wo[hbase * 64:(hbase + 8) * 64]
        m[f"wor{li}"] = np.ascontiguousarray(
            sl.reshape(4, 128, D).transpose(1, 0, 2).reshape(128, 4 * D))
    m["bor"] = np.ascontiguousarray(
        inputs["L1_attn_bo"].reshape(KC, 128).T.astype(np.float32))
    return m


def kernel_fused(**inputs):
    inputs = {k: np.asarray(v) for k, v in inputs.items()}
    x = np.ascontiguousarray(inputs["L0_attn_x"].astype(np.float32).reshape(16384, D))
    nc = build_fused()
    in_maps = []
    for c in range(8):
        m = prep_fused_core(inputs, c)
        m["x_in"] = np.ascontiguousarray(x[c * TOK:(c + 1) * TOK])
        in_maps.append(m)
    res = run_bass_kernel_spmd(nc, in_maps, core_ids=list(range(8)))
    out = np.concatenate([res.results[c]["x_out"] for c in range(8)], axis=0)
    return np.ascontiguousarray(out.reshape(4, 4096, D))
